# version 13
# baseline (speedup 1.0000x reference)
import math
import numpy as np
import concourse.bass as bass
import concourse.mybir as mybir
from concourse.bass_utils import run_bass_kernel_spmd

F32 = mybir.dt.float32
BF16 = mybir.dt.bfloat16
OP = mybir.AluOpType
ACT = mybir.ActivationFunctionType
AX = mybir.AxisListType

T = 2048
D = 1024
NT = T // 128
NP = D // 128
H = 16
HD = 64
FF = 4096
NF = FF // 128
NEG = -1.0e30
BIGM = 80.0
TINY = 1e-38
EPS = 1e-5
INVSQ = 1.0 / math.sqrt(HD)


def _extract_rounds(qt):
    mu = 0.032 * 128.0 * (qt + 1)
    ub = mu + 6.0 * math.sqrt(mu)
    return max(0, int(math.ceil((ub - 64.0) / 8.0)))


EXTR = [0] + [_extract_rounds(qt) for qt in range(1, NT)]
XCAP = 64


class Stream:
    def __init__(self, name, auto_serial):
        self.name = name
        self.auto = auto_serial
        self.n = 0
        self.ops = []
        self.sem = None

    def op(self, fn, waits=(), inc=True):
        prev = self.n if self.auto else 0
        if inc:
            self.n += 1
        self.ops.append((fn, tuple(waits), inc, prev))
        return self.n

    def wait(self, waits):
        self.ops.append((None, tuple(waits), False, 0))

    def realize(self, eng):
        for fn, waits, inc, serial in self.ops:
            if serial:
                eng.wait_ge(self.sem, serial)
            for s, val in waits:
                if val > 0:
                    eng.wait_ge(getattr(s, "sem", s), val)
            if fn is None:
                continue
            inst = fn(eng)
            if inc:
                inst.then_inc(self.sem, 1)


class DmaQueue:

    def __init__(self, stream, ring=12):
        self.stream = stream
        self.ring = ring
        self.sems = []
        self.uses = [0] * ring
        self.count = 0

    def dma(self, fn, waits=()):
        slot = self.count % self.ring
        self.count += 1
        prior = self.uses[slot]
        self.uses[slot] += 1
        sem = None

        q = self

        def emit(eng, fn=fn, slot=slot, prior=prior):
            if prior:
                eng.wait_ge(q.sems[slot], 16 * prior)
            inst = fn(eng)
            inst.then_inc(q.sems[slot], 16)
            return inst

        self.stream.op(emit, waits=waits, inc=False)
        return (_QSem(self, slot), 16 * (prior + 1))

    def all_tokens(self):
        return [(_QSem(self, i), 16 * u) for i, u in enumerate(self.uses) if u > 0]


class _QSem:

    def __init__(self, q, slot):
        self.q = q
        self.slot = slot

    @property
    def sem(self):
        return self.q.sems[self.slot]


def build_program(debug=()):
    nc = bass.Bass("TRN2", target_bir_lowering=False, debug=False)

    def din(name, shape):
        return nc.dram_tensor(name, shape, F32, kind="ExternalInput").ap()

    x0 = din("x0", [T, D])
    wqT = din("wqT", [D, D]); wkT = din("wkT", [D, D])
    wvT = din("wvT", [D, D]); woT = din("woT", [D, D])
    W1 = din("W1", [D, FF]); B1 = din("B1", [FF])
    W2 = din("W2", [FF, D]); B2 = din("B2", [D])
    LNG = din("LNG", [D]); LNB = din("LNB", [D])
    WR = din("WR", [D, 8]); BR = din("BR", [8])
    qkvT = din("qkvT", [D, 3 * D])
    projT = din("projT", [D, D]); projB = din("projB", [D])
    Lw1 = din("Lw1", [D, FF]); Lb1 = din("Lb1", [FF])
    Lw2 = din("Lw2", [FF, D]); Lb2 = din("Lb2", [D])
    out = nc.dram_tensor("out", [T, D], F32, kind="ExternalOutput").ap()

    s_w1b = nc.dram_tensor("s_w1b", [D, FF], BF16).ap()
    s_pw1b = nc.dram_tensor("s_pw1b", [D, FF], BF16).ap()
    s_w2b = nc.dram_tensor("s_w2b", [FF, D], BF16).ap()
    s_pw2b = nc.dram_tensor("s_pw2b", [FF, D], BF16).ap()
    s_lw1b = nc.dram_tensor("s_lw1b", [D, FF], BF16).ap()
    s_lw2b = nc.dram_tensor("s_lw2b", [FF, D], BF16).ap()
    s_act = nc.dram_tensor("s_act", [2, NF, NT, 128, 128], BF16).ap()
    s_act1 = nc.dram_tensor("s_act1", [1, NF, NT, 128, 128], BF16).ap()
    s_attnfm = nc.dram_tensor("s_attnfm", [NP, 128, T], BF16).ap()
    s_attnfm1 = nc.dram_tensor("s_attnfm1", [D, T], BF16).ap()
    s_x1 = nc.dram_tensor("s_x1", [T, D], F32).ap()
    s_x2 = nc.dram_tensor("s_x2", [T, D], F32).ap()

    dbg = {}

    def dbg_out(name, shape, dtype=F32):
        if name in debug:
            dbg[name] = nc.dram_tensor("dbg_" + name, shape, dtype,
                                       kind="ExternalOutput").ap()
        return dbg.get(name)

    TS = Stream("tensor", auto_serial=False)
    VS = Stream("vector", auto_serial=True)
    SS = Stream("scalar", auto_serial=True)
    GS = Stream("gpsimd", auto_serial=True)
    SY = Stream("sync", auto_serial=False)
    LD = DmaQueue(SY)
    STQ = DmaQueue(SS)
    LDG = DmaQueue(GS)
    streams = [TS, VS, SS, GS, SY]

    from contextlib import ExitStack
    ctx = ExitStack()
    E = ctx.enter_context
    for s, nm in [(TS, "semt"), (VS, "semv"), (SS, "sems"), (GS, "semg"), (SY, "semy")]:
        s.sem = E(nc.semaphore(nm))
    LD.sems = [E(nc.semaphore(f"semld{i}")) for i in range(LD.ring)]
    STQ.sems = [E(nc.semaphore(f"semst{i}")) for i in range(STQ.ring)]
    LDG.sems = [E(nc.semaphore(f"semldg{i}")) for i in range(LDG.ring)]

    def barrier():
        snap = [(s, s.n) for s in streams]
        qtoks = LD.all_tokens() + STQ.all_tokens() + LDG.all_tokens()
        for s in streams:
            w = [(o, v) for o, v in snap if o is not s and v > 0] + qtoks
            s.wait(w)

    eye_bf = E(nc.sbuf_tensor("eye_bf", [128, 128], BF16))
    eye_f = E(nc.sbuf_tensor("eye_f", [128, 128], F32))
    causal01 = E(nc.sbuf_tensor("causal01", [128, 128], F32))
    iota64 = E(nc.sbuf_tensor("iota64", [128, XCAP], F32))
    ones_r = E(nc.sbuf_tensor("ones_r", [1, 128], F32))
    ones_c = E(nc.sbuf_tensor("ones_c", [128, 1], F32))
    thrsb = E(nc.sbuf_tensor("thrsb", [128, 8], F32))
    negbig = E(nc.sbuf_tensor("negbig", [128, 1], F32))
    negbigm = E(nc.sbuf_tensor("negbigm", [128, 1], F32))
    negbig_b = E(nc.sbuf_tensor("negbig_b", [128, 1], BF16))
    neg80c = E(nc.sbuf_tensor("neg80c", [128, 1], F32))
    p0b = E(nc.sbuf_tensor("p0b", [128, NT], F32))
    b1fm = E(nc.sbuf_tensor("b1fm", [128, NF], F32))
    pb1fm = E(nc.sbuf_tensor("pb1fm", [128, NF], F32))
    lb1fm = E(nc.sbuf_tensor("lb1fm", [128, NF], F32))
    cvec_bc = E(nc.sbuf_tensor("cvec_bc", [128, D], F32))
    dvec_bc = E(nc.sbuf_tensor("dvec_bc", [128, D], F32))
    lbias_bc = E(nc.sbuf_tensor("lbias_bc", [128, D], F32))

    psum = [E(nc.psum_tensor(f"ps{i}", [128, 512], F32)) for i in range(6)]
    psumb = [E(nc.psum_tensor(f"psb{i}", [128, 1024], BF16)) for i in range(2)]
    psb_rd = {i: (VS, 0) for i in range(2)}

    def psb_produce(i, fn, waits=(), inc=True):
        s_, tok = psb_rd[i]
        w = list(waits)
        if tok > 0:
            w.append((s_, tok))
        return TS.op(fn, waits=w, inc=inc)

    def psb_consume(i, stream, fn, waits=()):
        tok = stream.op(fn, waits=waits)
        psb_rd[i] = (stream, tok)
        return tok
    ps_rd = {i: (VS, 0) for i in range(8)}

    def ps_produce(i, fn, waits=(), inc=True):
        s, tok = ps_rd[i]
        w = list(waits)
        if tok > 0:
            w.append((s, tok))
        return TS.op(fn, waits=w, inc=inc)

    def ps_consume(i, stream, fn, waits=()):
        tok = stream.op(fn, waits=waits)
        ps_rd[i] = (stream, tok)
        return tok

    def pbcast(dst_col, src11, bank, waits=()):
        w = src11.shape[-1]
        t = ps_produce(bank, lambda e: e.matmul(psum[bank][:, 0:w], ones_r[:],
                                                src11, start=True, stop=True),
                       waits=waits)
        return ps_consume(bank, VS, lambda e: e.tensor_copy(dst_col, psum[bank][:, 0:w]),
                          waits=[(TS, t)])

    def xpsum(dst11, src_col, bank, waits=()):
        t = ps_produce(bank, lambda e: e.matmul(psum[bank][0:1, 0:1], src_col,
                                                ones_c[:], start=True, stop=True),
                       waits=waits)
        return ps_consume(bank, VS, lambda e: e.tensor_copy(dst11, psum[bank][0:1, 0:1]),
                          waits=[(TS, t)])

    def emit_consts():
        g1 = GS.op(lambda e: e.iota(causal01[:], [[1, 128]], base=0,
                                    channel_multiplier=-1,
                                    allow_small_or_imprecise_dtypes=True))
        VS.op(lambda e: e.tensor_scalar(causal01[:], causal01[:], 0.0, None,
                                        op0=OP.is_gt), waits=[(GS, g1)])
        g2 = GS.op(lambda e: e.iota(iota64[:], [[1, XCAP]], base=0,
                                    channel_multiplier=0,
                                    allow_small_or_imprecise_dtypes=True))
        g3 = GS.op(lambda e: e.iota(eye_f[:], [[1, 128]], base=0,
                                    channel_multiplier=-1,
                                    allow_small_or_imprecise_dtypes=True))
        VS.op(lambda e: e.tensor_scalar(eye_f[:], eye_f[:], 0.0, None,
                                        op0=OP.is_equal), waits=[(GS, g3)])
        VS.op(lambda e: e.tensor_copy(eye_bf[:], eye_f[:]))
        VS.op(lambda e: e.memset(ones_r[:], 1.0))
        VS.op(lambda e: e.memset(ones_c[:], 1.0))
        VS.op(lambda e: e.memset(thrsb[:], 0.0))
        VS.op(lambda e: e.memset(negbig[:], NEG))
        VS.op(lambda e: e.memset(negbigm[:], -BIGM))
        VS.op(lambda e: e.memset(negbig_b[:], NEG))
        VS.op(lambda e: e.memset(neg80c[:], -80.0))
        _ = g2

    def emit_wprep(w_hbm, wb_hbm, pwb_hbm, n, thr_idx, sub_iters=24, full_iters=26, tag=""):
        width = n // 128
        k = max(1, int(n * 0.05))
        nchunks = max(1, width // 4096)
        cw = width // nchunks
        wv = w_hbm.rearrange("(p a) c -> p (a c)", p=128) if w_hbm.ndim == 2 else w_hbm
        wbv = wb_hbm.rearrange("(p a) c -> p (a c)", p=128) if wb_hbm.ndim == 2 else wb_hbm
        pwbv = pwb_hbm.rearrange("(p a) c -> p (a c)", p=128) if pwb_hbm.ndim == 2 else pwb_hbm

        with nc.sbuf_tensor("aw" + tag, [128, width], F32) as aw, \
             nc.sbuf_tensor("wchunk" + tag, [128, cw], F32) as wchunk, \
             nc.sbuf_tensor("bchunk" + tag, [128, cw], BF16) as bchunk, \
             nc.sbuf_tensor("mscr" + tag, [128, cw], BF16) as mscr, \
             nc.sbuf_tensor("cnt8" + tag, [128, nchunks], F32) as cnt8, \
             nc.sbuf_tensor("tneg" + tag, [128, 1], F32) as tneg, \
             nc.sbuf_tensor("taub" + tag, [128, 1], F32) as taub, \
             nc.sbuf_tensor("mscrb" + tag, [128, 4096], BF16) as mscrb, \
             nc.sbuf_tensor("cnt8b" + tag, [128, 8], F32) as cnt8b, \
             nc.sbuf_tensor("cnt1" + tag, [128, 1], F32) as cnt1, \
             nc.sbuf_tensor("st" + tag, [1, 16], F32) as st, \
             nc.sbuf_tensor("tau" + tag, [128, 1], F32) as tau:

            prev_st = None
            st_toks = []
            sa = 0
            for c in range(nchunks):
                lt = LD.dma(lambda e, c=c: e.dma_start(out=wchunk[:], in_=wv[:, c * cw:(c + 1) * cw]),
                            waits=[(VS, VS.n)] + ([(SS, sa)] if sa else []))
                sa = SS.op(lambda e, c=c: e.activation(aw[:, c * cw:(c + 1) * cw], wchunk[:],
                                                       ACT.Abs), waits=[lt])
                vb = VS.op(lambda e: e.tensor_copy(bchunk[:], wchunk[:]),
                           waits=[lt] + ([prev_st] if prev_st is not None else []))
                prev_st = STQ.dma(lambda e, c=c: e.dma_start(out=wbv[:, c * cw:(c + 1) * cw], in_=bchunk[:]),
                                  waits=[(VS, vb)])
                st_toks.append(prev_st)
            VS.wait([(SS, sa)])

            VS.op(lambda e: e.memset(st[:], 0.0))
            VS.op(lambda e: e.memset(st[0:1, 1:2], 64.0))
            VS.op(lambda e: e.memset(st[0:1, 3:4], 64.0))
            ntotf = float(n)

            def mid_of(lo_i, hi_i, mid_i):
                VS.op(lambda e: e.tensor_tensor(st[0:1, mid_i:mid_i + 1],
                                                st[0:1, lo_i:lo_i + 1],
                                                st[0:1, hi_i:hi_i + 1], op=OP.add))
                VS.op(lambda e: e.tensor_scalar(st[0:1, mid_i:mid_i + 1],
                                                st[0:1, mid_i:mid_i + 1], 0.5,
                                                None, op0=OP.mult))

            def update(lo_i, hi_i, mid_i, cnt_i, s0, s1, t_lo, t_hi):
                VS.op(lambda e, t_lo=t_lo: e.tensor_scalar(
                    st[0:1, s0:s0 + 1], st[0:1, cnt_i:cnt_i + 1], t_lo, None, op0=OP.is_ge))
                VS.op(lambda e: e.tensor_tensor(st[0:1, s1:s1 + 1], st[0:1, mid_i:mid_i + 1],
                                                st[0:1, lo_i:lo_i + 1], op=OP.subtract))
                VS.op(lambda e: e.scalar_tensor_tensor(
                    st[0:1, lo_i:lo_i + 1], st[0:1, s1:s1 + 1], st[0:1, s0:s0 + 1],
                    st[0:1, lo_i:lo_i + 1], op0=OP.mult, op1=OP.add))
                VS.op(lambda e, t_hi=t_hi: e.tensor_scalar(
                    st[0:1, s0:s0 + 1], st[0:1, cnt_i:cnt_i + 1], t_hi, None, op0=OP.is_lt))
                VS.op(lambda e: e.tensor_tensor(st[0:1, s1:s1 + 1], st[0:1, mid_i:mid_i + 1],
                                                st[0:1, hi_i:hi_i + 1], op=OP.subtract))
                VS.op(lambda e: e.scalar_tensor_tensor(
                    st[0:1, hi_i:hi_i + 1], st[0:1, s1:s1 + 1], st[0:1, s0:s0 + 1],
                    st[0:1, hi_i:hi_i + 1], op0=OP.mult, op1=OP.add))

            def sub_bisect(lo_i, hi_i, target, iters):
                step = 64
                aw_sub = aw.rearrange("p (a s) -> p a s", s=step)[:, :, 0]
                for _ in range(iters):
                    mid_of(lo_i, hi_i, 4)
                    pbcast(tau[:], st[0:1, 4:5], 5, waits=[(VS, VS.n)])
                    vtok = VS.op(lambda e, a=aw_sub: e.tensor_scalar(
                        mscr[:, 0:width // step], a, tau[:], 0.0,
                        op0=OP.is_gt, op1=OP.add, accum_out=cnt1[:]))
                    xpsum(st[0:1, 5:6], cnt1[:], 5, waits=[(VS, vtok)])
                    update(lo_i, hi_i, 4, 5, 6, 7, float(target), float(target))

            def joint_full(k_t, k_b, iters):
                for it_ in range(iters):
                    exact_t = it_ >= iters - 5
                    mid_of(0, 1, 4)
                    pbcast(tau[:], st[0:1, 4:5], 5, waits=[(VS, VS.n)])
                    VS.op(lambda e: e.tensor_scalar(tneg[:], tau[:], -1.0, None, op0=OP.mult))
                    mid_of(2, 3, 8)
                    pbcast(taub[:], st[0:1, 8:9], 5, waits=[(VS, VS.n)])
                    if not exact_t:
                        stok = 0
                        for c in range(nchunks):
                            stok = SS.op(lambda e, c=c: e.activation(
                                mscr[:], aw[:, c * cw:(c + 1) * cw], ACT.Sign, bias=tneg[:],
                                accum_out=cnt8[:, c:c + 1]), waits=[(VS, VS.n)])
                        for c in range(nchunks):
                            VS.op(lambda e, c=c: e.tensor_scalar(
                                mscrb[:], aw[:, c * cw:(c + 1) * cw], taub[:], 0.0,
                                op0=OP.is_gt, op1=OP.add, accum_out=cnt8b[:, c:c + 1]))
                        vtop = VS.op(lambda e: e.reduce_sum(cnt1[:], cnt8[:], axis=AX.X),
                                     waits=[(SS, stok)])
                        xpsum(st[0:1, 5:6], cnt1[:], 5, waits=[(VS, vtop)])
                        VS.op(lambda e: e.tensor_scalar(st[0:1, 5:6], st[0:1, 5:6], 0.5,
                                                        0.5 * ntotf, op0=OP.mult, op1=OP.add))
                    else:
                        for c in range(nchunks):
                            VS.op(lambda e, c=c: e.tensor_scalar(
                                mscr[:], aw[:, c * cw:(c + 1) * cw], tau[:], 0.0,
                                op0=OP.is_gt, op1=OP.add, accum_out=cnt8[:, c:c + 1]))
                        vtop = VS.op(lambda e: e.reduce_sum(cnt1[:], cnt8[:], axis=AX.X))
                        xpsum(st[0:1, 5:6], cnt1[:], 5, waits=[(VS, vtop)])
                        for c in range(nchunks):
                            VS.op(lambda e, c=c: e.tensor_scalar(
                                mscrb[:], aw[:, c * cw:(c + 1) * cw], taub[:], 0.0,
                                op0=OP.is_gt, op1=OP.add, accum_out=cnt8b[:, c:c + 1]))
                    vbot = VS.op(lambda e: e.reduce_sum(cnt1[:], cnt8b[:], axis=AX.X))
                    xpsum(st[0:1, 9:10], cnt1[:], 5, waits=[(VS, vbot)])
                    dz = 0.0 if exact_t else 0.6
                    update(0, 1, 4, 5, 6, 7, k_t + dz, k_t - dz)
                    update(2, 3, 8, 9, 10, 11, float(k_b), float(k_b))

            ssub = 64 if width >= 4096 else 1
            kb = n - k + 1
            if ssub > 1:
                sub_m = 10.0 * math.sqrt(max(k / ssub, 1.0))
                sub_bisect(0, 1, k / ssub + sub_m, sub_iters)
                sub_bisect(2, 3, max(k / ssub - sub_m, 0.5), sub_iters)
                VS.op(lambda e: e.tensor_copy(st[0:1, 1:2], st[0:1, 3:4]))
                VS.op(lambda e: e.memset(st[0:1, 2:3], 0.0))
                VS.op(lambda e: e.memset(st[0:1, 3:4], 64.0))
                sub_bisect(2, 3, kb / ssub + sub_m, sub_iters)
                VS.op(lambda e: e.memset(st[0:1, 8:9], 0.0))
                VS.op(lambda e: e.memset(st[0:1, 9:10], 64.0))
                sub_bisect(8, 9, max(kb / ssub - sub_m, 0.5), sub_iters)
                VS.op(lambda e: e.tensor_copy(st[0:1, 3:4], st[0:1, 9:10]))
                joint_full(float(k), float(kb), full_iters)
            else:
                joint_full(float(k), float(kb), 40)
            VS.op(lambda e, ti=thr_idx: e.tensor_copy(thrsb[0:1, 2 * ti:2 * ti + 1],
                                                      st[0:1, 0:1]))
            VS.op(lambda e, ti=thr_idx: e.tensor_copy(thrsb[0:1, 2 * ti + 1:2 * ti + 2],
                                                      st[0:1, 3:4]))
            pbcast(thrsb[:, 2 * thr_idx:2 * thr_idx + 2],
                   thrsb[0:1, 2 * thr_idx:2 * thr_idx + 2], 5, waits=[(VS, VS.n)])
            prev_st = None
            for c in range(nchunks):
                lt = LD.dma(lambda e, c=c: e.dma_start(out=bchunk[:], in_=wbv[:, c * cw:(c + 1) * cw]),
                            waits=[(VS, VS.n), st_toks[c], st_toks[-1]]
                                  + ([prev_st] if prev_st is not None else []))
                VS.op(lambda e, c=c: e.tensor_scalar(wchunk[:], aw[:, c * cw:(c + 1) * cw],
                                                     thrsb[:, 2 * thr_idx:2 * thr_idx + 1],
                                                     None, op0=OP.is_gt),
                      waits=[lt])
                VS.op(lambda e, c=c: e.scalar_tensor_tensor(
                    wchunk[:], aw[:, c * cw:(c + 1) * cw],
                    thrsb[:, 2 * thr_idx + 1:2 * thr_idx + 2], wchunk[:],
                    op0=OP.is_le, op1=OP.subtract))
                VS.op(lambda e: e.tensor_scalar(wchunk[:], wchunk[:], 0.8, None, op0=OP.mult))
                vp = VS.op(lambda e: e.scalar_tensor_tensor(mscr[:], bchunk[:], 1.0,
                                                            wchunk[:], op0=OP.mult,
                                                            op1=OP.mult))
                vp = VS.op(lambda e: e.tensor_tensor(mscr[:], mscr[:], bchunk[:], op=OP.add))
                prev_st = STQ.dma(lambda e, c=c: e.dma_start(out=pwbv[:, c * cw:(c + 1) * cw], in_=mscr[:]),
                                  waits=[(VS, vp)])
        barrier()

    def emit_bias_prep():
        with nc.sbuf_tensor("b1a", [128, 128], F32) as b1a, \
             nc.sbuf_tensor("ms", [128, NF], F32) as ms, \
             nc.sbuf_tensor("cnt1", [128, 1], F32) as cnt1, \
             nc.sbuf_tensor("st", [1, 8], F32) as st, \
             nc.sbuf_tensor("tau", [128, 1], F32) as tau, \
             nc.sbuf_tensor("b2r", [1, D], F32) as b2r, \
             nc.sbuf_tensor("pb2r", [1, D], F32) as pb2r, \
             nc.sbuf_tensor("lbr", [1, D], F32) as lbr:

            lt = LD.dma(lambda e: e.dma_start(
                out=b1a[0:NF, 0:128].rearrange("a b -> a b"),
                in_=B1.rearrange("(f p) -> f p", p=128)))
            tb1 = ps_produce(4, lambda e: e.transpose(
                psum[4][:, 0:NF], b1a[0:NF, 0:128], eye_f[0:NF, 0:NF]),
                waits=[lt])
            ps_consume(4, VS, lambda e: e.tensor_copy(b1fm[:], psum[4][:, 0:NF]),
                       waits=[(TS, tb1)])
            lt2 = LD.dma(lambda e: e.dma_start(
                out=b1a[0:NF, 0:128].rearrange("a b -> a b"),
                in_=Lb1.rearrange("(f p) -> f p", p=128)),
                waits=[(VS, VS.n)])
            tb2 = ps_produce(4, lambda e: e.transpose(
                psum[4][:, 0:NF], b1a[0:NF, 0:128], eye_f[0:NF, 0:NF]),
                waits=[lt2])
            ps_consume(4, VS, lambda e: e.tensor_copy(lb1fm[:], psum[4][:, 0:NF]),
                       waits=[(TS, tb2)])

            def small_select(src, n, iters=40):
                pass

            k1 = max(1, int(FF * 0.05))
            va = SS.op(lambda e: e.activation(b1a[:, 0:NF], b1fm[:], ACT.Abs),
                       waits=[(VS, VS.n)])
            VS.wait([(SS, va)])
            VS.op(lambda e: e.memset(st[:], 0.0))
            VS.op(lambda e: e.memset(st[0:1, 1:2], 64.0))
            VS.op(lambda e: e.memset(st[0:1, 3:4], 64.0))

            def bis(lo_i, hi_i, target, iters, src, w):
                for _ in range(iters):
                    VS.op(lambda e: e.tensor_tensor(st[0:1, 4:5], st[0:1, lo_i:lo_i + 1],
                                                    st[0:1, hi_i:hi_i + 1], op=OP.add))
                    VS.op(lambda e: e.tensor_scalar(st[0:1, 4:5], st[0:1, 4:5], 0.5,
                                                    None, op0=OP.mult))
                    pbcast(tau[:], st[0:1, 4:5], 5, waits=[(VS, VS.n)])
                    vtok = VS.op(lambda e: e.tensor_scalar(ms[:, 0:w], src[:, 0:w], tau[:],
                                                           0.0, op0=OP.is_gt, op1=OP.add,
                                                           accum_out=cnt1[:]))
                    xpsum(st[0:1, 5:6], cnt1[:], 5, waits=[(VS, vtok)])
                    VS.op(lambda e: e.tensor_scalar(st[0:1, 6:7], st[0:1, 5:6],
                                                    float(target), None, op0=OP.is_ge))
                    VS.op(lambda e: e.tensor_tensor(st[0:1, 7:8], st[0:1, 4:5],
                                                    st[0:1, lo_i:lo_i + 1], op=OP.subtract))
                    VS.op(lambda e: e.scalar_tensor_tensor(
                        st[0:1, lo_i:lo_i + 1], st[0:1, 7:8], st[0:1, 6:7],
                        st[0:1, lo_i:lo_i + 1], op0=OP.mult, op1=OP.add))
                    VS.op(lambda e: e.tensor_tensor(st[0:1, 7:8], st[0:1, 4:5],
                                                    st[0:1, hi_i:hi_i + 1], op=OP.subtract))
                    VS.op(lambda e: e.tensor_scalar(st[0:1, 6:7], st[0:1, 6:7], -1.0, 1.0,
                                                    op0=OP.mult, op1=OP.add))
                    VS.op(lambda e: e.scalar_tensor_tensor(
                        st[0:1, hi_i:hi_i + 1], st[0:1, 7:8], st[0:1, 6:7],
                        st[0:1, hi_i:hi_i + 1], op0=OP.mult, op1=OP.add))

            bis(0, 1, k1, 40, b1a, NF)
            bis(2, 3, FF - k1 + 1, 40, b1a, NF)
            b1a_n = b1a[:, 0:NF]
            pbcast(tau[:], st[0:1, 0:1], 5, waits=[(VS, VS.n)])
            VS.op(lambda e: e.tensor_scalar(ms[:], b1a[:, 0:NF], tau[:], None, op0=OP.is_gt))
            pbcast(tau[:], st[0:1, 3:4], 5, waits=[(VS, VS.n)])
            VS.op(lambda e: e.scalar_tensor_tensor(ms[:], b1a[:, 0:NF], tau[:], ms[:],
                                                   op0=OP.is_le, op1=OP.subtract))
            VS.op(lambda e: e.tensor_scalar(ms[:], ms[:], 0.8, None, op0=OP.mult))
            VS.op(lambda e: e.tensor_mul(ms[:], ms[:], b1fm[:]))
            VS.op(lambda e: e.tensor_add(pb1fm[:], ms[:], b1fm[:]))

            lt3 = LD.dma(lambda e: e.dma_start(out=b2r[:], in_=B2.rearrange("(o d) -> o d", o=1)))
            k2 = max(1, int(D * 0.05))
            va2 = SS.op(lambda e: e.activation(pb2r[:], b2r[:], ACT.Abs),
                        waits=[lt3, (VS, VS.n)])
            VS.wait([(SS, va2)])
            _ = (va, va2)
            VS.op(lambda e: e.memset(st[:], 0.0))
            VS.op(lambda e: e.memset(st[0:1, 1:2], 64.0))
            VS.op(lambda e: e.memset(st[0:1, 3:4], 64.0))

            def bis1(lo_i, hi_i, target, iters):
                for _ in range(iters):
                    VS.op(lambda e: e.tensor_tensor(st[0:1, 4:5], st[0:1, lo_i:lo_i + 1],
                                                    st[0:1, hi_i:hi_i + 1], op=OP.add))
                    VS.op(lambda e: e.tensor_scalar(st[0:1, 4:5], st[0:1, 4:5], 0.5,
                                                    None, op0=OP.mult))
                    VS.op(lambda e: e.tensor_scalar(lbr[:], pb2r[:], st[0:1, 4:5], 0.0,
                                                    op0=OP.is_gt, op1=OP.add,
                                                    accum_out=st[0:1, 5:6]))
                    VS.op(lambda e: e.tensor_scalar(st[0:1, 6:7], st[0:1, 5:6],
                                                    float(target), None, op0=OP.is_ge))
                    VS.op(lambda e: e.tensor_tensor(st[0:1, 7:8], st[0:1, 4:5],
                                                    st[0:1, lo_i:lo_i + 1], op=OP.subtract))
                    VS.op(lambda e: e.scalar_tensor_tensor(
                        st[0:1, lo_i:lo_i + 1], st[0:1, 7:8], st[0:1, 6:7],
                        st[0:1, lo_i:lo_i + 1], op0=OP.mult, op1=OP.add))
                    VS.op(lambda e: e.tensor_tensor(st[0:1, 7:8], st[0:1, 4:5],
                                                    st[0:1, hi_i:hi_i + 1], op=OP.subtract))
                    VS.op(lambda e: e.tensor_scalar(st[0:1, 6:7], st[0:1, 6:7], -1.0, 1.0,
                                                    op0=OP.mult, op1=OP.add))
                    VS.op(lambda e: e.scalar_tensor_tensor(
                        st[0:1, hi_i:hi_i + 1], st[0:1, 7:8], st[0:1, 6:7],
                        st[0:1, hi_i:hi_i + 1], op0=OP.mult, op1=OP.add))

            bis1(0, 1, k2, 40)
            bis1(2, 3, D - k2 + 1, 40)
            VS.op(lambda e: e.tensor_scalar(lbr[:], pb2r[:], st[0:1, 0:1], None, op0=OP.is_gt))
            VS.op(lambda e: e.scalar_tensor_tensor(lbr[:], pb2r[:], st[0:1, 3:4], lbr[:],
                                                   op0=OP.is_le, op1=OP.subtract))
            VS.op(lambda e: e.tensor_scalar(lbr[:], lbr[:], 0.8, None, op0=OP.mult))
            VS.op(lambda e: e.tensor_mul(lbr[:], lbr[:], b2r[:]))
            VS.op(lambda e: e.tensor_add(pb2r[:], lbr[:], b2r[:]))
            pbcast(cvec_bc[:, 0:512], b2r[0:1, 0:512], 5, waits=[(VS, VS.n)])
            pbcast(cvec_bc[:, 512:1024], b2r[0:1, 512:1024], 5)
            VS.op(lambda e: e.tensor_sub(pb2r[:], pb2r[:], b2r[:]))
            pbcast(dvec_bc[:, 0:512], pb2r[0:1, 0:512], 5, waits=[(VS, VS.n)])
            pbcast(dvec_bc[:, 512:1024], pb2r[0:1, 512:1024], 5)
            lt4 = LD.dma(lambda e: e.dma_start(out=b2r[:], in_=projB.rearrange("(o d) -> o d", o=1)),
                         waits=[(VS, VS.n), (TS, TS.n)])
            lt5 = LD.dma(lambda e: e.dma_start(out=pb2r[:], in_=Lb2.rearrange("(o d) -> o d", o=1)),
                         waits=[(VS, VS.n), (TS, TS.n)])
            vv = VS.op(lambda e: e.tensor_add(lbr[:], b2r[:], pb2r[:]),
                       waits=[lt4, lt5])
            pbcast(lbias_bc[:, 0:512], lbr[0:1, 0:512], 5, waits=[(VS, vv)])
            pbcast(lbias_bc[:, 512:1024], lbr[0:1, 512:1024], 5)
            _ = lt4
            if "thr" in debug:
                d = dbg_out("thr", [128, 8])
                STQ.dma(lambda e, d=d: e.dma_start(out=d, in_=thrsb[:]), waits=[(VS, VS.n)])
            if "biases" in debug:
                d = dbg_out("biases", [128, 2 * NF + 2 * D])
                vv2 = VS.op(lambda e: e.engine_nop(), inc=True)
                STQ.dma(lambda e, d=d: e.dma_start(out=d[:, 0:NF], in_=b1fm[:]), waits=[(VS, vv2)])
                STQ.dma(lambda e, d=d: e.dma_start(out=d[:, NF:2 * NF], in_=pb1fm[:]))
                STQ.dma(lambda e, d=d: e.dma_start(out=d[:, 2 * NF:2 * NF + D], in_=cvec_bc[:]))
                STQ.dma(lambda e, d=d: e.dma_start(out=d[:, 2 * NF + D:2 * NF + 2 * D], in_=dvec_bc[:]))
        barrier()

    def emit_pre(xnfm):
        with nc.sbuf_tensor("xt", [128, D], F32) as xt, \
             nc.sbuf_tensor("gam_bc", [128, D], F32) as gam_bc, \
             nc.sbuf_tensor("bet_bc", [128, D], F32) as bet_bc, \
             nc.sbuf_tensor("xc", [128, D], F32) as xc, \
             nc.sbuf_tensor("sqs", [128, D], F32) as sqs, \
             nc.sbuf_tensor("hfm", [128, D], BF16) as hfm, \
             nc.sbuf_tensor("mu", [128, 1], F32) as mu, \
             nc.sbuf_tensor("musq", [128, 1], F32) as musq, \
             nc.sbuf_tensor("ssq", [128, 1], F32) as ssq, \
             nc.sbuf_tensor("veps", [128, 1], F32) as veps, \
             nc.sbuf_tensor("rstd", [128, 1], F32) as rstd, \
             nc.sbuf_tensor("f2", [128, 1], F32) as f2, \
             nc.sbuf_tensor("sc8", [128, 8], F32) as sc8, \
             nc.sbuf_tensor("wrsb", [128, NP, 8], BF16) as wrsb, \
             nc.sbuf_tensor("grow", [1, D], F32) as grow, \
             nc.sbuf_tensor("brow", [1, D], F32) as brow, \
             nc.sbuf_tensor("brr", [1, 8], F32) as brr, \
             nc.sbuf_tensor("brbc", [128, 8], F32) as brbc:

            tg = LD.dma(lambda e: e.dma_start(out=grow[:], in_=LNG.rearrange("(o d) -> o d", o=1)))
            tb = LD.dma(lambda e: e.dma_start(out=brow[:], in_=LNB.rearrange("(o d) -> o d", o=1)))
            tr = LD.dma(lambda e: e.dma_start(out=brr[:], in_=BR.rearrange("(o d) -> o d", o=1)))
            tw = LDG.dma(lambda e: e.dma_start(out=wrsb[:], in_=WR.rearrange("(j p) e -> p j e", p=128)))
            pbcast(gam_bc[:, 0:512], grow[0:1, 0:512], 5, waits=[tg])
            pbcast(gam_bc[:, 512:1024], grow[0:1, 512:1024], 5)
            pbcast(bet_bc[:, 0:512], brow[0:1, 0:512], 5, waits=[tb])
            pbcast(bet_bc[:, 512:1024], brow[0:1, 512:1024], 5)
            pbcast(brbc[:], brr[0:1, :], 5, waits=[tr])

            sq_prev = 0
            tposx_prev = 0
            tposh_prev = 0
            for i in range(NT):
                lx = LD.dma(lambda e, i=i: e.dma_start(out=xt[:], in_=x0[i * 128:(i + 1) * 128, :]),
                            waits=[(VS, VS.n)] + ([(SS, sq_prev)] if sq_prev else []))
                VS.op(lambda e: e.reduce_sum(mu[:], xt[:], axis=AX.X), waits=[lx])
                VS.op(lambda e: e.tensor_scalar(mu[:], mu[:], 1.0 / D, None, op0=OP.mult))
                sq = SS.op(lambda e: e.activation(sqs[:], xt[:], ACT.Square, accum_out=ssq[:]),
                           waits=[lx, (VS, VS.n)])
                sq_prev = sq
                VS.op(lambda e: e.tensor_mul(musq[:], mu[:], mu[:]), waits=[(SS, sq)])
                VS.op(lambda e: e.scalar_tensor_tensor(veps[:], ssq[:], 1.0 / D, musq[:],
                                                       op0=OP.mult, op1=OP.subtract))
                VS.op(lambda e: e.tensor_scalar(veps[:], veps[:], EPS, None, op0=OP.add))
                s2 = SS.op(lambda e: e.activation(rstd[:], veps[:], ACT.Sqrt),
                           waits=[(VS, VS.n)])
                VS.op(lambda e: e.reciprocal(rstd[:], rstd[:]), waits=[(SS, s2)])
                vb = VS.op(lambda e: e.tensor_scalar(xc[:], xt[:], mu[:], rstd[:],
                                                     op0=OP.subtract, op1=OP.mult),
                           waits=([(TS, tposh_prev)] if tposh_prev else []))
                for j in range(NP):
                    bk = j % 2
                    t = ps_produce(bk, lambda e, j=j, bk=bk: e.transpose(
                        psum[bk][:, 0:128], xc[:, j * 128:(j + 1) * 128], eye_f[:]),
                        waits=[(VS, vb)])
                    ps_consume(bk, VS, lambda e, j=j, i=i, bk=bk: e.tensor_copy(
                        xnfm[:, j, i * 128:(i + 1) * 128], psum[bk][:, 0:128]),
                        waits=[(TS, t)])
                    tposx_prev = t
                VS.op(lambda e: e.reciprocal(f2[:], veps[:]))
                VS.op(lambda e: e.tensor_scalar(f2[:], f2[:], -EPS, 1.0,
                                                op0=OP.mult, op1=OP.add))
                VS.op(lambda e: e.tensor_scalar(f2[:], f2[:], EPS, None, op0=OP.add))
                s3 = SS.op(lambda e: e.activation(f2[:], f2[:], ACT.Sqrt), waits=[(VS, VS.n)])
                VS.op(lambda e: e.reciprocal(f2[:], f2[:]), waits=[(SS, s3)])
                VS.op(lambda e: e.scalar_tensor_tensor(sqs[:], xc[:], f2[:], gam_bc[:],
                                                       op0=OP.mult, op1=OP.mult))
                vh = VS.op(lambda e: e.tensor_tensor(xc[:], sqs[:], bet_bc[:], op=OP.add),
                           waits=[(TS, tposx_prev)])
                for j in range(NP):
                    bk = 2 + (j % 2)
                    t = ps_produce(bk, lambda e, j=j, bk=bk: e.transpose(
                        psum[bk][:, 0:128], xc[:, j * 128:(j + 1) * 128], eye_f[:]),
                        waits=[(VS, vh)])
                    ps_consume(bk, VS, lambda e, j=j, bk=bk: e.tensor_copy(
                        hfm[:, j * 128:(j + 1) * 128], psum[bk][:, 0:128]),
                        waits=[(TS, t)])
                    tposh_prev = t
                vhf = VS.n
                s_rd = ps_rd[4]
                for j in range(NP):
                    w = ([(VS, vhf), tw] if j == 0 else [])
                    if j == 0 and s_rd[1] > 0:
                        w.append(s_rd)
                    TS.op(lambda e, j=j: e.matmul(
                        psum[4][:, 0:8], hfm[:, j * 128:(j + 1) * 128], wrsb[:, j, :],
                        start=(j == 0), stop=(j == NP - 1)),
                        waits=w, inc=(j == NP - 1))
                tt = TS.n
                VS.op(lambda e: e.tensor_add(sc8[:], psum[4][:, 0:8], brbc[:]),
                      waits=[(TS, tt)])
                ps_rd[4] = (VS, VS.n)
                VS.op(lambda e: e.tensor_scalar(sc8[:], sc8[:], sc8[:, 0:1], None,
                                                op0=OP.subtract))
                s4 = SS.op(lambda e: e.activation(sc8[:], sc8[:], ACT.Exp, accum_out=f2[:]),
                           waits=[(VS, VS.n)])
                VS.op(lambda e, i=i: e.reciprocal(p0b[:, i:i + 1], f2[:]), waits=[(SS, s4)])

            if "xnfm" in debug:
                d = dbg_out("xnfm", [128, NP * T], BF16)
                STQ.dma(lambda e, d=d: e.dma_start(
                    out=d.rearrange("p (j t) -> p j t", j=NP), in_=xnfm[:]),
                    waits=[(VS, VS.n)])
            if "p0" in debug:
                d = dbg_out("p0", [128, NT])
                STQ.dma(lambda e, d=d: e.dma_start(out=d, in_=p0b[:]), waits=[(VS, VS.n)])
        barrier()


    def emit_attn(ffnA_spec=None):
        from contextlib import ExitStack
        actx = ExitStack()
        A = actx.enter_context
        qfm = A(nc.sbuf_tensor("qfm", [128, NP, T], BF16))
        kfm = A(nc.sbuf_tensor("kfm", [128, NP, T], BF16))
        vtm = A(nc.sbuf_tensor("vtm", [128, NT, D], BF16))
        if True:

            with nc.sbuf_tensor("wprj", [128, NP, D], BF16) as wprj, \
                 nc.sbuf_tensor("ptile", [128, 512], F32) as ptile, \
                 nc.sbuf_tensor("pabs", [128, 512], F32) as pabs, \
                 nc.sbuf_tensor("m8", [128, 8, 8], F32) as m8:
                for (wT_src, dstfm) in ((wqT, qfm), (wkT, kfm), (wvT, None)):
                    lw = LDG.dma(lambda e, wT_src=wT_src: e.dma_start(
                        out=wprj[:], in_=wT_src.rearrange("(j p) d -> p j d", p=128)),
                        waits=[(TS, TS.n)])
                    wsb = wprj
                    for i in range(NT):
                        for n in range(2):
                            bk = n % 2
                            for kk in range(NP):
                                w = [lw] if kk == 0 else []
                                if kk == 0:
                                    sprev, ptok = ps_rd[bk]
                                    if ptok > 0:
                                        w.append((sprev, ptok))
                                TS.op(lambda e, i=i, n=n, kk=kk, bk=bk, wsb=wsb: e.matmul(
                                    psum[bk][:], xnfm[:, kk, i * 128:(i + 1) * 128],
                                    wsb[:, kk, n * 512:(n + 1) * 512],
                                    start=(kk == 0), stop=(kk == NP - 1)),
                                    waits=w, inc=(kk == NP - 1))
                            tmm = TS.n
                            evp = ps_consume(bk, VS, lambda e, bk=bk: e.tensor_copy(ptile[:], psum[bk][:]),
                                             waits=[(TS, tmm)])
                            sab = SS.op(lambda e: e.activation(pabs[:], ptile[:], ACT.Abs),
                                        waits=[(VS, evp)])
                            VS.wait([(SS, sab)])
                            for hh in range(8):
                                VS.op(lambda e, hh=hh: e.max(out=m8[:, hh, :], in_=pabs[:, hh * 64:(hh + 1) * 64]))
                            p3 = pabs.rearrange("p (h d) -> p h d", h=8)
                            VS.op(lambda e, p3=p3: e.tensor_tensor(
                                p3, p3, m8[:, :, 1:2].to_broadcast([128, 8, 64]), op=OP.is_ge))
                            if dstfm is None:
                                VS.op(lambda e, i=i, n=n: e.tensor_tensor(
                                    vtm[:, i, n * 512:(n + 1) * 512], pabs[:], ptile[:], op=OP.mult))
                            else:
                                vsp = VS.op(lambda e: e.tensor_mul(ptile[:], pabs[:], ptile[:]))
                                for jj in range(4):
                                    tb = ps_produce(2 + jj % 2, lambda e, jj=jj: e.transpose(
                                        psum[2 + jj % 2][:, 0:128],
                                        ptile[:, jj * 128:(jj + 1) * 128], eye_f[:]),
                                        waits=[(VS, vsp)])
                                    ps_consume(2 + jj % 2, VS,
                                               lambda e, i=i, n=n, jj=jj, dstfm=dstfm: e.tensor_copy(
                                                   dstfm[:, n * 4 + jj, i * 128:(i + 1) * 128],
                                                   psum[2 + jj % 2][:, 0:128]),
                                               waits=[(TS, tb)])
            barrier()

            scP = [A(nc.sbuf_tensor(f"scp{p}", [128, T], BF16)) for p in range(2)]
            zmP = [A(nc.sbuf_tensor(f"zmp{p}", [128, T], BF16)) for p in range(2)]
            pmP = [A(nc.sbuf_tensor(f"pmp{p}", [128, T], mybir.dt.uint8)) for p in range(2)]
            czP = [A(nc.sbuf_tensor(f"czp{p}", [128, T], F32)) for p in range(2)]
            wbpP = [A(nc.sbuf_tensor(f"wbpp{p}", [128, T], BF16)) for p in range(2)]
            wtbP = [A(nc.sbuf_tensor(f"wtbp{p}", [128, T], BF16)) for p in range(2)]
            x8P = [A(nc.sbuf_tensor(f"x8p{p}", [128, XCAP], F32)) for p in range(2)]
            colP = [A(nc.sbuf_tensor(f"colp{p}", [128, 8], F32)) for p in range(2)]
            otileP = [A(nc.sbuf_tensor(f"otilep{p}", [128, 64], BF16)) for p in range(2)]
            ofmP = [A(nc.sbuf_tensor(f"ofmp{p}", [64, 128], BF16)) for p in range(2)]
            funit, n_units = (None, 0)
            if ffnA_spec is not None:
                funit, n_units = make_ffnA(A, *ffnA_spec, banks=(2, 3))
            bands = [(h, qt) for h in range(H) for qt in range(NT)]
            fin = {0: None, 1: None}
            state = {}

            def front(i):
                h, qt = bands[i]
                par = i % 2
                sc = scP[par]
                pt, po = h // 2, (h % 2) * 64
                Wk = (qt + 1) * 128
                nch = (Wk + 511) // 512
                qsl = qfm[po:po + 64, pt, qt * 128:(qt + 1) * 128]
                f = fin[par]
                evtoks = []
                for cch in range(nch):
                    cw_ = min(512, Wk - cch * 512)
                    bk = par
                    sprev, ptok = ps_rd[bk]
                    w = [(sprev, ptok)] if ptok > 0 else []
                    TS.op(lambda e, cch=cch, cw_=cw_, bk=bk, qsl=qsl, pt=pt, po=po: e.matmul(
                        psum[bk][:, 0:cw_], qsl,
                        kfm[po:po + 64, pt, cch * 512:cch * 512 + cw_],
                        start=True, stop=True), waits=w, inc=True)
                    tmm = TS.n
                    ww = [(TS, tmm)]
                    if f is not None and cch == 0:
                        ww.append((VS, f["vs_sc"]))
                    ev = SS.op(lambda e, cch=cch, cw_=cw_, bk=bk, sc=sc: e.activation(
                        sc[:, cch * 512:cch * 512 + cw_], psum[bk][:, 0:cw_],
                        ACT.Identity, scale=INVSQ), waits=ww)
                    ps_rd[bk] = (SS, ev)
                    evtoks.append(ev)
                ctok = VS.op(lambda e, qt=qt, sc=sc: e.scalar_tensor_tensor(
                    sc[:, qt * 128:qt * 128 + 128], causal01[:], NEG,
                    sc[:, qt * 128:qt * 128 + 128], op0=OP.mult, op1=OP.add),
                    waits=[(SS, evtoks[-1])])
                sgn = None
                if qt > 0:
                    scol = colP[par][:, 7:8]
                    jk = wtbP[par]
                    sw = [(VS, ctok)]
                    if f is not None:
                        sw.append((TS, f["tav"]))
                    sgn = SS.op(lambda e, Wk=Wk, sc=sc, scol=scol, jk=jk: e.activation(
                        jk[:, 0:Wk], sc[:, 0:Wk], ACT.Sign, accum_out=scol),
                        waits=sw)
                return {"evlast": evtoks[-1], "ctok": ctok, "sgn": sgn}

            last_spill = {0: None, 1: None}

            def epilogue(f2):
                par, h, qt = f2["par"], f2["h"], f2["qt"]
                pt, po = h // 2, (h % 2) * 64
                Dcol = colP[par][:, 4:5]
                rDcol = colP[par][:, 5:6]
                otile, ofm = otileP[par], ofmP[par]
                avbank = 4 + par
                VS.op(lambda e, Dcol=Dcol, rDcol=rDcol: e.tensor_scalar_max(
                    rDcol, Dcol, 1e-30), waits=[(SS, f2["sexp"])])
                VS.op(lambda e, rDcol=rDcol: e.reciprocal(rDcol, rDcol))
                ov = ps_consume(avbank, VS, lambda e, otile=otile, rDcol=rDcol, avbank=avbank: e.tensor_scalar(
                    otile[:], psum[avbank][:, 0:64], rDcol, None, op0=OP.mult),
                    waits=[(TS, f2["tav"])])
                obb = f2["obb"]
                tb = psb_produce(obb, lambda e, otile=otile, obb=obb: e.transpose(
                    psumb[obb][0:64, 0:128], otile[:, 0:64], eye_bf[:]),
                    waits=[(VS, ov)])
                ow = [(TS, tb)]
                if last_spill[par] is not None:
                    ow.append(last_spill[par])
                oev = psb_consume(obb, SS, lambda e, ofm=ofm, obb=obb: e.activation(
                    ofm[:], psumb[obb][0:64, 0:128], ACT.Identity), waits=ow)
                last_spill[par] = STQ.dma(lambda e, pt=pt, po=po, qt=qt, ofm=ofm: e.dma_start(
                    out=s_attnfm[pt, po:po + 64, qt * 128:(qt + 1) * 128],
                    in_=ofm[:]), waits=[(SS, oev)])

            def back(i, st_):
                h, qt = bands[i]
                par = i % 2
                f = fin[par]
                if f is not None:
                    epilogue(f)
                sc, zm, pm, cz = scP[par], zmP[par], pmP[par], czP[par]
                wbp, wtb, x8 = wbpP[par], wtbP[par], x8P[par]
                pcol = colP[par][:, 0:1]
                mcol = colP[par][:, 1:2]
                dcol = colP[par][:, 2:3]
                taucol = colP[par][:, 3:4]
                Dcol = colP[par][:, 4:5]
                pt, po = h // 2, (h % 2) * 64
                Wk = (qt + 1) * 128
                R = EXTR[qt]
                ctok = st_["ctok"]
                sgn = st_["sgn"]
                wT = [(TS, f["ts_wT"])] if f is not None else []
                if qt == 0:
                    VS.op(lambda e, sc=sc, wbp=wbp: e.tensor_copy(wbp[:, 0:128], sc[:, 0:128]),
                          waits=wT)
                    for r in range(8):
                        VS.op(lambda e, r=r, wbp=wbp, x8=x8: e.max(
                            out=x8[:, r * 8:r * 8 + 8], in_=wbp[:, 0:128]))
                        VS.op(lambda e, r=r, wbp=wbp, x8=x8: e.match_replace(
                            out=wbp[:, 0:128], in_to_replace=x8[:, r * 8:r * 8 + 8],
                            in_values=wbp[:, 0:128], imm_value=NEG))
                    VS.op(lambda e, sc=sc, wbp=wbp, zm=zm: e.tensor_tensor(
                        zm[:, 0:128], sc[:, 0:128], wbp[:, 0:128], op=OP.not_equal))
                else:
                    zcol = colP[par][:, 6:7]
                    scol = colP[par][:, 7:8]
                    VS.op(lambda e, Wk=Wk, sc=sc, zm=zm, zcol=zcol: e.tensor_scalar(
                        zm[:, 0:Wk], sc[:, 0:Wk], 0.0, 0.0, op0=OP.is_equal,
                        op1=OP.add, accum_out=zcol))
                    VS.op(lambda e, Wk=Wk, zm=zm, cz=cz: e.tensor_tensor_scan(
                        cz[:, 0:Wk], zm[:, 0:Wk], zm[:, 0:Wk], 0.0,
                        op0=OP.add, op1=OP.bypass),
                        waits=[(SS, f["sexp"])] if f is not None else [])
                    VS.op(lambda e, pcol=pcol, scol=scol, zcol=zcol: e.tensor_sub(
                        pcol, scol, zcol), waits=[(SS, sgn)])
                    VS.op(lambda e, Wk=Wk, pcol=pcol: e.tensor_scalar(
                        pcol, pcol, 0.5, 0.5 * Wk, op0=OP.mult, op1=OP.add))
                    VS.op(lambda e, pcol=pcol, mcol=mcol: e.tensor_scalar(
                        mcol, pcol, -1.0, 64.0, op0=OP.mult, op1=OP.add))
                    VS.op(lambda e, Wk=Wk, zm=zm, cz=cz, mcol=mcol: e.scalar_tensor_tensor(
                        zm[:, 0:Wk], cz[:, 0:Wk], mcol, zm[:, 0:Wk],
                        op0=OP.is_le, op1=OP.mult))
                    if R > 0:
                        VS.op(lambda e, Wk=Wk, sc=sc, pm=pm: e.tensor_scalar(
                            pm[:, 0:Wk], sc[:, 0:Wk], 0.0, None, op0=OP.is_le))
                        sneg = SS.op(lambda e, Wk=Wk, sc=sc, wbp=wbp: e.activation(
                            wbp[:, 0:Wk], sc[:, 0:Wk], ACT.Identity, scale=-1.0),
                            waits=[(VS, VS.n)] + wT)
                        VS.op(lambda e, Wk=Wk, wbp=wbp, pm=pm: e.copy_predicated(
                            wbp[:, 0:Wk], pm[:, 0:Wk], negbig_b[:].to_broadcast([128, Wk])),
                            waits=[(SS, sneg)])
                        for r in range(R):
                            VS.op(lambda e, r=r, Wk=Wk, wbp=wbp, x8=x8: e.max(
                                out=x8[:, r * 8:r * 8 + 8], in_=wbp[:, 0:Wk]))
                            VS.op(lambda e, r=r, Wk=Wk, wbp=wbp, x8=x8: e.match_replace(
                                out=wbp[:, 0:Wk], in_to_replace=x8[:, r * 8:r * 8 + 8],
                                in_values=wbp[:, 0:Wk], imm_value=NEG))
                        VS.op(lambda e, pcol=pcol, dcol=dcol: e.tensor_scalar(
                            dcol, pcol, -64.0, None, op0=OP.add))
                        VS.op(lambda e, dcol=dcol: e.tensor_scalar_max(dcol, dcol, 0.0))
                        VS.op(lambda e, R=R, dcol=dcol: e.tensor_scalar_min(
                            dcol, dcol, float(8 * R - 1)))
                        VS.op(lambda e, R=R, x8=x8, dcol=dcol, taucol=taucol: e.scalar_tensor_tensor(
                            x8[:, 0:8 * R], iota64[:, 0:8 * R], dcol, x8[:, 0:8 * R],
                            op0=OP.is_equal, op1=OP.mult, accum_out=taucol))
                        VS.op(lambda e, taucol=taucol: e.tensor_scalar(
                            taucol, taucol, -1.0, None, op0=OP.mult))
                        VS.op(lambda e, taucol=taucol: e.tensor_scalar_max(taucol, taucol, TINY))
                    else:
                        VS.op(lambda e, taucol=taucol: e.memset(taucol, TINY))
                    VS.op(lambda e, Wk=Wk, sc=sc, zm=zm, taucol=taucol: e.scalar_tensor_tensor(
                        zm[:, 0:Wk], sc[:, 0:Wk], taucol, zm[:, 0:Wk],
                        op0=OP.is_ge, op1=OP.add))
                vw = []
                if qt == 0 and f is not None:
                    vw.append((SS, f["sexp"]))
                vm = VS.op(lambda e, Wk=Wk, sc=sc, zm=zm, cz=cz: e.scalar_tensor_tensor(
                    cz[:, 0:Wk], zm[:, 0:Wk], 80.0, sc[:, 0:Wk],
                    op0=OP.mult, op1=OP.add), waits=vw)
                sexp = SS.op(lambda e, Wk=Wk, cz=cz, wbp=wbp, Dcol=Dcol: e.activation(
                    wbp[:, 0:Wk], cz[:, 0:Wk], ACT.Exp, bias=neg80c[:],
                    accum_out=Dcol), waits=[(VS, vm)] + wT)
                wtev = []
                nkb = qt + 1
                ngrp = (nkb + 7) // 8
                wT_last = 0
                for g in range(ngrp):
                    bb = g % 2
                    blocks = list(range(g * 8, min((g + 1) * 8, nkb)))
                    tb = 0
                    for ii, kb in enumerate(blocks):
                        if ii == 0:
                            tb = psb_produce(bb, lambda e, kb=kb, bb=bb, wbp=wbp: e.transpose(
                                psumb[bb][:, (kb % 8) * 128:(kb % 8 + 1) * 128],
                                wbp[:, kb * 128:(kb + 1) * 128], eye_bf[:]),
                                waits=[(SS, sexp)])
                        else:
                            tb = TS.op(lambda e, kb=kb, bb=bb, wbp=wbp: e.transpose(
                                psumb[bb][:, (kb % 8) * 128:(kb % 8 + 1) * 128],
                                wbp[:, kb * 128:(kb + 1) * 128], eye_bf[:]))
                    wT_last = tb
                    wtev.append(psb_consume(bb, SS, lambda e, g=g, bb=bb, nb=len(blocks), wtb=wtb: e.activation(
                        wtb[:, g * 1024:g * 1024 + nb * 128], psumb[bb][:, 0:nb * 128],
                        ACT.Identity), waits=[(TS, tb)]))
                avbank = 4 + par
                sprev, ptok = ps_rd[avbank]
                for kb in range(nkb):
                    w = []
                    if kb == 0 and ptok > 0:
                        w.append((sprev, ptok))
                    if kb % 8 == 0:
                        w.append((SS, wtev[kb // 8]))
                    TS.op(lambda e, kb=kb, h=h, qt=qt, wtb=wtb, avbank=avbank: e.matmul(
                        psum[avbank][:, 0:64], wtb[:, kb * 128:(kb + 1) * 128],
                        vtm[:, kb, h * 64:(h + 1) * 64],
                        start=(kb == 0), stop=(kb == qt)),
                        waits=w, inc=(kb == qt))
                tav = TS.n
                if "band" in debug and i == 1:
                    dwb = dbg_out("band_w", [128, 256], BF16)
                    dzm = dbg_out("band_g", [128, 256], BF16)
                    dcl = dbg_out("band_col", [128, 8])
                    t1 = STQ.dma(lambda e, wbp=wbp: e.dma_start(out=dwb, in_=wbp[:, 0:256]),
                                 waits=[(SS, sexp)])
                    t2 = STQ.dma(lambda e, zm=zm: e.dma_start(out=dzm, in_=zm[:, 0:256]),
                                 waits=[(SS, sexp)])
                    t3 = STQ.dma(lambda e, par=par: e.dma_start(
                        out=dcl[:, 0:2], in_=colP[par][:, 0:2]), waits=[(SS, sexp)])
                    t4 = STQ.dma(lambda e, par=par: e.dma_start(
                        out=dcl[:, 3:5], in_=colP[par][:, 3:5]), waits=[(SS, sexp)])
                    t5 = STQ.dma(lambda e, par=par: e.dma_start(
                        out=dcl[:, 6:8], in_=colP[par][:, 6:8]), waits=[(SS, sexp)])
                    VS.wait([t1, t2, t3, t4, t5])
                    SS.wait([t1, t2, t3, t4, t5])
                return {"vs_sc": vm, "ts_wT": wT_last, "sexp": sexp, "tav": tav,
                        "obb": ngrp % 2, "par": par, "h": h, "qt": qt}

            for i in range(len(bands) + 1):
                if i < len(bands):
                    state[i] = front(i)
                if i >= 1:
                    fin[(i - 1) % 2] = back(i - 1, state.pop(i - 1))
                    if funit is not None and i - 1 < n_units:
                        funit(i - 1)
            epilogue(fin[len(bands) % 2])
            epilogue(fin[(len(bands) + 1) % 2])
            if funit is not None:
                for u in range(len(bands), n_units):
                    funit(u)
        actx.close()
        barrier()


    def emit_wconv(w_hbm, wb_hbm, n, tag):
        width = n // 128
        nchunks = max(1, width // 4096)
        cw = width // nchunks
        wv = w_hbm.rearrange("(p a) c -> p (a c)", p=128)
        wbv = wb_hbm.rearrange("(p a) c -> p (a c)", p=128)
        with nc.sbuf_tensor("wcv" + tag, [128, cw], F32) as wchunk, \
             nc.sbuf_tensor("bcv" + tag, [128, cw], BF16) as bchunk:
            prev_st = None
            for c in range(nchunks):
                lt = LD.dma(lambda e, c=c: e.dma_start(out=wchunk[:], in_=wv[:, c * cw:(c + 1) * cw]),
                            waits=[(VS, VS.n)])
                vb = VS.op(lambda e: e.tensor_copy(bchunk[:], wchunk[:]),
                           waits=[lt] + ([prev_st] if prev_st is not None else []))
                prev_st = STQ.dma(lambda e, c=c: e.dma_start(out=wbv[:, c * cw:(c + 1) * cw], in_=bchunk[:]),
                                  waits=[(VS, vb)])
        barrier()

    def emit_wo_x1():
        with nc.sbuf_tensor("wob", [128, NP, D], BF16) as wob, \
             nc.sbuf_tensor("afm0", [128, NP, 128], BF16) as afm0, \
             nc.sbuf_tensor("afm1", [128, NP, 128], BF16) as afm1, \
             nc.sbuf_tensor("xt1a", [128, D], F32) as xt1a, \
             nc.sbuf_tensor("xt1b", [128, D], F32) as xt1b, \
             nc.sbuf_tensor("x1ta", [128, D], F32) as x1ta, \
             nc.sbuf_tensor("x1tb", [128, D], F32) as x1tb:
            afmP_, xtP_, x1tP_ = [afm0, afm1], [xt1a, xt1b], [x1ta, x1tb]
            lwo = LDG.dma(lambda e: e.dma_start(out=wob[:], in_=woT.rearrange("(j p) d -> p j d", p=128)))
            stP = [None, None]
            mmP = [0, 0]
            vvP = [0, 0]
            for i in range(NT):
                par = i % 2
                afm, xt1, x1t = afmP_[par], xtP_[par], x1tP_[par]
                la = LD.dma(lambda e, i=i, afm=afm: e.dma_start(
                    out=afm[:], in_=s_attnfm[:, :, i * 128:(i + 1) * 128].rearrange("j p t -> p j t")),
                    waits=[(TS, mmP[par])])
                lx = LD.dma(lambda e, i=i, xt1=xt1: e.dma_start(out=xt1[:], in_=x0[i * 128:(i + 1) * 128, :]),
                            waits=[(VS, vvP[par])] + ([stP[par]] if stP[par] is not None else []))
                for n in range(2):
                    bk = 2 * par + n % 2
                    sprev, ptok = ps_rd[bk]
                    for kk in range(NP):
                        w = []
                        if kk == 0:
                            w = [la, lwo]
                            if ptok > 0:
                                w.append((sprev, ptok))
                        TS.op(lambda e, n=n, kk=kk, bk=bk, afm=afm: e.matmul(
                            psum[bk][:], afm[:, kk, :], wob[:, kk, n * 512:(n + 1) * 512],
                            start=(kk == 0), stop=(kk == NP - 1)),
                            waits=w, inc=(kk == NP - 1))
                    tmm = TS.n
                    ps_consume(bk, VS, lambda e, n=n, bk=bk, x1t=x1t, xt1=xt1: e.tensor_tensor(
                        x1t[:, n * 512:(n + 1) * 512], psum[bk][:],
                        xt1[:, n * 512:(n + 1) * 512], op=OP.add),
                        waits=[(TS, tmm), lx] + ([stP[par]] if stP[par] is not None else []))
                mmP[par] = TS.n
                vvP[par] = VS.n
                stP[par] = STQ.dma(lambda e, i=i, x1t=x1t: e.dma_start(
                    out=s_x1[i * 128:(i + 1) * 128, :], in_=x1t[:]), waits=[(VS, vvP[par])])
        barrier()

    def make_ffnA(A, wb_list, bfm_list, act_hbm, gelu_fn, xsrc, tag, banks=(2, 3)):
        nexp = len(wb_list)
        w1c2 = A(nc.sbuf_tensor("w1c" + tag, [128, 2, NP, 128], BF16))
        pw1c2 = A(nc.sbuf_tensor("pw1c" + tag, [128, 2, NP, 128], BF16))
        actt2 = A(nc.sbuf_tensor("actt" + tag, [128, 2, 512], BF16))
        prev_stores = {0: [], 1: []}
        lastmm = {0: 0, 1: 0}
        ltoks_cur = {}
        upf = 4 * nexp

        def unit(u):
            ft, r = divmod(u, upf)
            nch, ei = divmod(r, nexp)
            fp = ft % 2
            wcs = [w1c2[:, fp], pw1c2[:, fp]][:nexp]
            if r == 0:
                ltoks_cur.clear()
                for e2 in range(nexp):
                    ltoks_cur[e2] = LDG.dma(lambda e, ft=ft, e2=e2, wcs=wcs: e.dma_start(
                        out=wcs[e2],
                        in_=wb_list[e2][:, ft * 128:(ft + 1) * 128].rearrange(
                            "(j p) f -> p j f", p=128)),
                        waits=[(TS, lastmm[fp])])
            bk = banks[u % len(banks)]
            ap = u % 2
            sprev, ptok = ps_rd[bk]
            for kk in range(NP):
                w = []
                if kk == 0:
                    w = [ltoks_cur[ei]]
                    if ptok > 0:
                        w.append((sprev, ptok))
                TS.op(lambda e, nch=nch, ei=ei, kk=kk, bk=bk, wcs=wcs: e.matmul(
                    psum[bk][:], wcs[ei][:, kk, :],
                    xsrc[:, kk, nch * 512:(nch + 1) * 512],
                    start=(kk == 0), stop=(kk == NP - 1)),
                    waits=w, inc=(kk == NP - 1))
            tmm = TS.n
            sg = SS.op(lambda e, ft=ft, ei=ei, bk=bk, ap=ap: e.activation(
                actt2[:, ap], psum[bk][:], gelu_fn, bias=bfm_list[ei][:, ft:ft + 1]),
                waits=[(TS, tmm)] + prev_stores[ap])
            ps_rd[bk] = (SS, sg)
            prev_stores[ap] = []
            for tt4 in range(4):
                prev_stores[ap].append(STQ.dma(
                    lambda e, ft=ft, ei=ei, nch=nch, tt4=tt4, ap=ap: e.dma_start(
                        out=act_hbm[ei, ft, nch * 4 + tt4, :, :],
                        in_=actt2[:, ap, tt4 * 128:(tt4 + 1) * 128]),
                    waits=[(SS, sg)]))
            if r == upf - 1:
                lastmm[fp] = TS.n
        return unit, NF * upf

    def emit_ffnA(wb_list, bfm_list, act_hbm, gelu_fn, xsrc, tag):
        from contextlib import ExitStack
        fctx = ExitStack()
        unit, n_units = make_ffnA(fctx.enter_context, wb_list, bfm_list, act_hbm,
                                  gelu_fn, xsrc, tag, banks=(0, 1, 2, 3))
        for u in range(n_units):
            unit(u)
        fctx.close()
        barrier()

    def emit_ffnB0():
        with nc.sbuf_tensor("w2r", [128, NF, D], BF16) as w2r, \
             nc.sbuf_tensor("pw2r", [128, NF, D], BF16) as pw2r, \
             nc.sbuf_tensor("acts", [128, 2, NF, 128], BF16) as acts, \
             nc.sbuf_tensor("y0t", [128, 512], F32) as y0t, \
             nc.sbuf_tensor("x1t2", [128, 2, D], F32) as x1t2, \
             nc.sbuf_tensor("p0c", [128, 1], F32) as p0c, \
             nc.sbuf_tensor("q0c", [128, 1], F32) as q0c, \
             nc.sbuf_tensor("cvt", [128, D], F32) as cvt:
            lw2 = LDG.dma(lambda e: e.dma_start(
                out=w2r[:], in_=s_w2b.rearrange("(j p) d -> p j d", p=128)))
            lpw2 = LDG.dma(lambda e: e.dma_start(
                out=pw2r[:], in_=s_pw2b.rearrange("(j p) d -> p j d", p=128)))
            stP = [None, None]
            mmP = [0, 0]
            vvP = [0, 0]
            for i in range(NT):
                par = i % 2
                acts_, x1t_ = acts[:], x1t2[:, par]
                la = LD.dma(lambda e, i=i, acts_=acts_: e.dma_start(
                    out=acts_, in_=s_act[:, :, i, :, :].rearrange("e f p t -> p e f t")),
                    waits=[(TS, max(mmP))])
                lx = LD.dma(lambda e, i=i, x1t_=x1t_: e.dma_start(
                    out=x1t_, in_=s_x1[i * 128:(i + 1) * 128, :]),
                    waits=[(VS, vvP[par])] + ([stP[par]] if stP[par] is not None else []))
                VS.op(lambda e, i=i: e.tensor_copy(p0c[:], p0b[:, i:i + 1]))
                VS.op(lambda e: e.tensor_scalar(q0c[:], p0c[:], -1.0, 1.0, op0=OP.mult, op1=OP.add))
                VS.op(lambda e: e.scalar_tensor_tensor(cvt[:], dvec_bc[:], q0c[:], cvec_bc[:],
                                                       op0=OP.mult, op1=OP.add))
                for n in range(2):
                    for (ei, wsb) in ((0, w2r), (1, pw2r)):
                        bk = 2 * par + ei
                        sprev, ptok = ps_rd[bk]
                        for kk in range(NF):
                            w = []
                            if kk == 0:
                                w = [la, lw2 if ei == 0 else lpw2]
                                if ptok > 0:
                                    w.append((sprev, ptok))
                            TS.op(lambda e, n=n, ei=ei, kk=kk, bk=bk, wsb=wsb, acts_=acts_: e.matmul(
                                psum[bk][:], acts_[:, ei, kk, :], wsb[:, kk, n * 512:(n + 1) * 512],
                                start=(kk == 0), stop=(kk == NF - 1)),
                                waits=w, inc=(kk == NF - 1))
                        if ei == 0:
                            t0 = TS.n
                        else:
                            t1 = TS.n
                    b0, b1 = 2 * par, 2 * par + 1
                    ev0 = ps_consume(b0, VS, lambda e, b0=b0: e.tensor_copy(y0t[:], psum[b0][:]),
                                     waits=[(TS, t0)])
                    vv = ps_consume(b1, VS, lambda e, n=n, b1=b1: e.tensor_sub(y0t[:], y0t[:], psum[b1][:]),
                                    waits=[(TS, t1)])
                    VS.op(lambda e, n=n, b1=b1: e.scalar_tensor_tensor(
                        y0t[:], y0t[:], p0c[:], psum[b1][:], op0=OP.mult, op1=OP.add))
                    ps_rd[b1] = (VS, VS.n)
                    VS.op(lambda e, n=n: e.tensor_tensor(
                        y0t[:], y0t[:], cvt[:, n * 512:(n + 1) * 512], op=OP.add))
                    VS.op(lambda e, n=n, x1t_=x1t_: e.tensor_tensor(
                        x1t_[:, n * 512:(n + 1) * 512], y0t[:],
                        x1t_[:, n * 512:(n + 1) * 512], op=OP.add), waits=[lx])
                mmP[par] = TS.n
                vvP[par] = VS.n
                stP[par] = STQ.dma(lambda e, i=i, x1t_=x1t_: e.dma_start(
                    out=s_x1[i * 128:(i + 1) * 128, :], in_=x1t_), waits=[(VS, vvP[par])])
        barrier()

    def emit_l1_ln(xnfm):
        with nc.sbuf_tensor("xtl", [128, D], F32) as xt, \
             nc.sbuf_tensor("xcl", [128, D], F32) as xc, \
             nc.sbuf_tensor("sql", [128, D], F32) as sqs, \
             nc.sbuf_tensor("mul_", [128, 1], F32) as mu, \
             nc.sbuf_tensor("musql", [128, 1], F32) as musq, \
             nc.sbuf_tensor("ssql", [128, 1], F32) as ssq, \
             nc.sbuf_tensor("vepsl", [128, 1], F32) as veps, \
             nc.sbuf_tensor("rstdl", [128, 1], F32) as rstd:
            sq_prev = 0
            tposx_prev = 0
            for i in range(NT):
                lx = LD.dma(lambda e, i=i: e.dma_start(out=xt[:], in_=s_x1[i * 128:(i + 1) * 128, :]),
                            waits=[(VS, VS.n)] + ([(SS, sq_prev)] if sq_prev else []))
                VS.op(lambda e: e.reduce_sum(mu[:], xt[:], axis=AX.X), waits=[lx])
                VS.op(lambda e: e.tensor_scalar(mu[:], mu[:], 1.0 / D, None, op0=OP.mult))
                sq = SS.op(lambda e: e.activation(sqs[:], xt[:], ACT.Square, accum_out=ssq[:]),
                           waits=[lx, (VS, VS.n)])
                sq_prev = sq
                VS.op(lambda e: e.tensor_mul(musq[:], mu[:], mu[:]), waits=[(SS, sq)])
                VS.op(lambda e: e.scalar_tensor_tensor(veps[:], ssq[:], 1.0 / D, musq[:],
                                                       op0=OP.mult, op1=OP.subtract))
                VS.op(lambda e: e.tensor_scalar(veps[:], veps[:], EPS, None, op0=OP.add))
                s2 = SS.op(lambda e: e.activation(rstd[:], veps[:], ACT.Sqrt),
                           waits=[(VS, VS.n)])
                VS.op(lambda e: e.reciprocal(rstd[:], rstd[:]), waits=[(SS, s2)])
                vb = VS.op(lambda e: e.tensor_scalar(xc[:], xt[:], mu[:], rstd[:],
                                                     op0=OP.subtract, op1=OP.mult),
                           waits=([(TS, tposx_prev)] if tposx_prev else []))
                for j in range(NP):
                    bk = j % 2
                    t = ps_produce(bk, lambda e, j=j, bk=bk: e.transpose(
                        psum[bk][:, 0:128], xc[:, j * 128:(j + 1) * 128], eye_f[:]),
                        waits=[(VS, vb)])
                    ps_consume(bk, VS, lambda e, j=j, i=i, bk=bk: e.tensor_copy(
                        xnfm[:, j, i * 128:(i + 1) * 128], psum[bk][:, 0:128]),
                        waits=[(TS, t)])
                    tposx_prev = t
        barrier()

    BIG1 = 400.0

    def emit_l1_attn(xnfm):
        from contextlib import ExitStack
        actx = ExitStack()
        A = actx.enter_context
        qfm = A(nc.sbuf_tensor("qfm1", [128, NP, T], BF16))
        kfm = A(nc.sbuf_tensor("kfm1", [128, NP, T], BF16))
        vtm = A(nc.sbuf_tensor("vtm1", [128, NT, D], BF16))
        qb = A(nc.sbuf_tensor("qb1", [64, H, 32], F32))
        kb = A(nc.sbuf_tensor("kb1", [64, H, 32], F32))
        bmask = A(nc.sbuf_tensor("bmask1", [32, H, 32], BF16))
        cmat = A(nc.sbuf_tensor("cmat1", [32, 32], F32))
        m8c = A(nc.sbuf_tensor("m8c1", [32, 8], F32))
        em = A(nc.sbuf_tensor("em1", [32, T], BF16))
        emv = A(nc.sbuf_tensor("emv1", [32, T], F32))
        em2 = A(nc.sbuf_tensor("em21", [32, T], BF16))
        eyem = A(nc.sbuf_tensor("eyem1", [32, 32], F32))

        g1 = GS.op(lambda e: e.iota(emv[:], [[1, T]], base=0, channel_multiplier=-64,
                                    allow_small_or_imprecise_dtypes=True))
        VS.op(lambda e: e.tensor_scalar(em2[:], emv[:], 0.0, None, op0=OP.is_ge),
              waits=[(GS, g1)])
        VS.op(lambda e: e.tensor_scalar(em[:], emv[:], 63.0, None, op0=OP.is_le))
        VS.op(lambda e: e.tensor_mul(em[:], em[:], em2[:]))
        g2 = GS.op(lambda e: e.iota(eyem[:], [[1, 32]], base=0, channel_multiplier=-1,
                                    allow_small_or_imprecise_dtypes=True))
        VS.op(lambda e: e.tensor_scalar(eyem[:], eyem[:], 0.0, None, op0=OP.is_equal),
              waits=[(GS, g2)])

        with nc.sbuf_tensor("wprj1", [128, NP, D], BF16) as wprj:
            for (off, dstfm) in ((0, qfm), (D, kfm), (2 * D, None)):
                lw = LDG.dma(lambda e, off=off: e.dma_start(
                    out=wprj[:], in_=qkvT[:, off:off + D].rearrange("(j p) d -> p j d", p=128)),
                    waits=[(TS, TS.n)])
                if dstfm is None:
                    for i in range(NT):
                        for n in range(2):
                            bk = n % 2
                            sprev, ptok = ps_rd[bk]
                            for kk in range(NP):
                                w = []
                                if kk == 0:
                                    w = [lw]
                                    if ptok > 0:
                                        w.append((sprev, ptok))
                                TS.op(lambda e, i=i, n=n, kk=kk, bk=bk: e.matmul(
                                    psum[bk][:], xnfm[:, kk, i * 128:(i + 1) * 128],
                                    wprj[:, kk, n * 512:(n + 1) * 512],
                                    start=(kk == 0), stop=(kk == NP - 1)),
                                    waits=w, inc=(kk == NP - 1))
                            tmm = TS.n
                            ps_consume(bk, VS, lambda e, i=i, n=n, bk=bk: e.tensor_copy(
                                vtm[:, i, n * 512:(n + 1) * 512], psum[bk][:]),
                                waits=[(TS, tmm)])
                else:
                    for j in range(NP):
                        for ch in range(4):
                            bk = ch % 2
                            sprev, ptok = ps_rd[bk]
                            for kk in range(NP):
                                w = []
                                if kk == 0:
                                    w = [lw]
                                    if ptok > 0:
                                        w.append((sprev, ptok))
                                TS.op(lambda e, j=j, ch=ch, kk=kk, bk=bk: e.matmul(
                                    psum[bk][:], wprj[:, kk, j * 128:(j + 1) * 128],
                                    xnfm[:, kk, ch * 512:(ch + 1) * 512],
                                    start=(kk == 0), stop=(kk == NP - 1)),
                                    waits=w, inc=(kk == NP - 1))
                            tmm = TS.n
                            ps_consume(bk, VS, lambda e, j=j, ch=ch, bk=bk, dstfm=dstfm: e.tensor_copy(
                                dstfm[:, j, ch * 512:(ch + 1) * 512], psum[bk][:]),
                                waits=[(TS, tmm)])
        barrier()

        for h in range(H):
            pt, po = h // 2, (h % 2) * 64
            v1 = VS.op(lambda e, h=h, pt=pt, po=po: e.reduce_sum(
                qb[:, h, :], qfm[po:po + 64, pt, :].rearrange("p (m w) -> p m w", w=64),
                axis=AX.X))
            v2 = VS.op(lambda e, h=h, pt=pt, po=po: e.reduce_sum(
                kb[:, h, :], kfm[po:po + 64, pt, :].rearrange("p (m w) -> p m w", w=64),
                axis=AX.X))
            sprev, ptok = ps_rd[5]
            w = [(VS, v2)]
            if ptok > 0:
                w.append((sprev, ptok))
            TS.op(lambda e, h=h: e.matmul(psum[5][0:32, 0:32], qb[:, h, :], kb[:, h, :],
                                          start=True, stop=True), waits=w, inc=True)
            tc = TS.n
            ev = ps_consume(5, SS, lambda e: e.activation(cmat[:], psum[5][0:32, 0:32],
                                                          ACT.Identity, scale=0.125 / 4096.0),
                            waits=[(TS, tc), (VS, VS.n)])
            VS.op(lambda e: e.max(out=m8c[:], in_=cmat[:]), waits=[(SS, ev)])
            VS.op(lambda e, h=h: e.scalar_tensor_tensor(
                bmask[:, h, :], cmat[:], m8c[:, 1:2], eyem[:],
                op0=OP.is_ge, op1=OP.max))
        barrier()

        sc1P = [A(nc.sbuf_tensor(f"sc1P{p}", [128, T], BF16)) for p in range(2)]
        vmP = [A(nc.sbuf_tensor(f"vmP{p}", [128, T], F32)) for p in range(2)]
        wbp1P = [A(nc.sbuf_tensor(f"wbp1P{p}", [128, T], BF16)) for p in range(2)]
        wtb1P = [A(nc.sbuf_tensor(f"wtb1P{p}", [128, T], BF16)) for p in range(2)]
        M2P = [A(nc.sbuf_tensor(f"M2P{p}", [128, 32], F32)) for p in range(2)]
        col1P = [A(nc.sbuf_tensor(f"col1P{p}", [128, 4], F32)) for p in range(2)]
        otile1P = [A(nc.sbuf_tensor(f"otile1P{p}", [128, 64], BF16)) for p in range(2)]
        ofm1P = [A(nc.sbuf_tensor(f"ofm1P{p}", [64, 128], BF16)) for p in range(2)]
        bands1 = [(h, qt) for h in range(H) for qt in range(NT)]
        fin1 = {0: None, 1: None}
        st1 = {}

        def front1(i):
            h, qt = bands1[i]
            par = i % 2
            sc, M2 = sc1P[par], M2P[par]
            pt, po = h // 2, (h % 2) * 64
            qsl = qfm[po:po + 64, pt, qt * 128:(qt + 1) * 128]
            f = fin1[par]
            mbank = 4 + par
            sprev, ptok = ps_rd[mbank]
            w = [(VS, VS.n)]
            if ptok > 0:
                w.append((sprev, ptok))
            TS.op(lambda e, qt=qt, h=h, mbank=mbank: e.matmul(
                psum[mbank][:, 448:480], em[:, qt * 128:(qt + 1) * 128], bmask[:, h, :],
                start=True, stop=True), waits=w, inc=True)
            tm2 = TS.n
            evlast = 0
            vtoks = {}
            for cch in range(4):
                bk = 2 * par + cch % 2
                sprev, ptok = ps_rd[bk]
                w = [(sprev, ptok)] if ptok > 0 else []
                TS.op(lambda e, cch=cch, bk=bk, qsl=qsl, pt=pt, po=po: e.matmul(
                    psum[bk][:], qsl, kfm[po:po + 64, pt, cch * 512:(cch + 1) * 512],
                    start=True, stop=True), waits=w, inc=True)
                tmm = TS.n
                if cch < 2:
                    ww = [(TS, tmm)]
                    if f is not None and cch == 0:
                        ww.append((VS, f["vs_sc"]))
                    evlast = SS.op(lambda e, cch=cch, bk=bk, sc=sc: e.activation(
                        sc[:, cch * 512:(cch + 1) * 512], psum[bk][:],
                        ACT.Identity, scale=0.125), waits=ww)
                    ps_rd[bk] = (SS, evlast)
                else:
                    vtoks[cch] = tmm
            return {"evlast": evlast, "tm2": tm2, "vtoks": vtoks}

        last_spill1 = {0: None, 1: None}

        def epilogue1(f2):
            par, h, qt = f2["par"], f2["h"], f2["qt"]
            pt, po = h // 2, (h % 2) * 64
            Dcol = col1P[par][:, 1:2]
            rDcol = col1P[par][:, 2:3]
            otile, ofm = otile1P[par], ofm1P[par]
            avbank = 4 + par
            VS.op(lambda e, Dcol=Dcol, rDcol=rDcol: e.tensor_scalar_max(
                rDcol, Dcol, 1e-30), waits=[(SS, f2["ss_exp"])])
            VS.op(lambda e, rDcol=rDcol: e.reciprocal(rDcol, rDcol))
            ov = ps_consume(avbank, VS, lambda e, otile=otile, rDcol=rDcol, avbank=avbank: e.tensor_scalar(
                otile[:], psum[avbank][:, 0:64], rDcol, None, op0=OP.mult),
                waits=[(TS, f2["tav"])])
            tb = psb_produce(0, lambda e, otile=otile: e.transpose(
                psumb[0][0:64, 0:128], otile[:, 0:64], eye_bf[:]),
                waits=[(VS, ov)])
            ow = [(TS, tb)]
            if last_spill1[par] is not None:
                ow.append(last_spill1[par])
            oev = psb_consume(0, SS, lambda e, ofm=ofm: e.activation(
                ofm[:], psumb[0][0:64, 0:128], ACT.Identity), waits=ow)
            last_spill1[par] = STQ.dma(lambda e, pt=pt, po=po, qt=qt, ofm=ofm: e.dma_start(
                out=s_attnfm1.rearrange("(j p) t -> j p t", p=128)[
                    pt, po:po + 64, qt * 128:(qt + 1) * 128],
                in_=ofm[:]), waits=[(SS, oev)])

        def back1(i, st_):
            h, qt = bands1[i]
            par = i % 2
            f = fin1[par]
            sc, M2, vm = sc1P[par], M2P[par], vmP[par]
            wbp, wtb = wbp1P[par], wtb1P[par]
            mxc = col1P[par][:, 0:1]
            Dcol = col1P[par][:, 1:2]
            pt, po = h // 2, (h % 2) * 64
            for cch, tmm in st_["vtoks"].items():
                bk = 2 * par + cch % 2
                vev = VS.op(lambda e, cch=cch, bk=bk, sc=sc: e.tensor_scalar(
                    sc[:, cch * 512:(cch + 1) * 512], psum[bk][:], 0.125,
                    None, op0=OP.mult), waits=[(TS, tmm)])
                ps_rd[bk] = (VS, vev)
            mbank = 4 + par
            VS.op(lambda e, M2=M2, mbank=mbank: e.tensor_copy(
                M2[:], psum[mbank][:, 448:480]), waits=[(TS, st_["tm2"])])
            ps_rd[mbank] = (VS, VS.op(lambda e, M2=M2: e.tensor_scalar(
                M2[:], M2[:], 1e30, -1e30, op0=OP.mult, op1=OP.add)))
            if f is not None:
                epilogue1(f)
            sc3 = sc.rearrange("p (m b) -> p m b", b=64)
            vm3 = vm.rearrange("p (m b) -> p m b", b=64)
            vw = [(SS, st_["evlast"])]
            if f is not None:
                vw.append((SS, f["ss_exp"]))
            vmt = VS.op(lambda e, sc3=sc3, vm3=vm3, M2=M2: e.scalar_tensor_tensor(
                vm3, M2[:].unsqueeze(2).to_broadcast([128, 32, 64]), 1.0, sc3,
                op0=OP.mult, op1=OP.add), waits=vw)
            VS.op(lambda e, vm=vm, mxc=mxc: e.reduce_max(mxc, vm[:], axis=AX.X))
            vneg = VS.op(lambda e, mxc=mxc: e.tensor_scalar(mxc, mxc, -1.0, None, op0=OP.mult))
            ew = [(VS, vneg)]
            if f is not None:
                ew.append(f["vs_wbp"])
            sexp = SS.op(lambda e, vm=vm, wbp=wbp, Dcol=Dcol, mxc=mxc: e.activation(
                wbp[:], vm[:], ACT.Exp, bias=mxc, accum_out=Dcol), waits=ew)
            wtev = []
            for g in range(2):
                bb = g % 2
                tb = 0
                for ii in range(8):
                    kbi = g * 8 + ii
                    if ii == 0:
                        tb = psb_produce(bb, lambda e, kbi=kbi, bb=bb, wbp=wbp: e.transpose(
                            psumb[bb][:, (kbi % 8) * 128:(kbi % 8 + 1) * 128],
                            wbp[:, kbi * 128:(kbi + 1) * 128], eye_bf[:]),
                            waits=[(SS, sexp)])
                    else:
                        tb = TS.op(lambda e, kbi=kbi, bb=bb, wbp=wbp: e.transpose(
                            psumb[bb][:, (kbi % 8) * 128:(kbi % 8 + 1) * 128],
                            wbp[:, kbi * 128:(kbi + 1) * 128], eye_bf[:]))
                wtev.append((SS, psb_consume(bb, SS, lambda e, g=g, bb=bb, wtb=wtb: e.activation(
                    wtb[:, g * 1024:(g + 1) * 1024], psumb[bb][:, 0:1024],
                    ACT.Identity), waits=[(TS, tb)])))
            avbank = 4 + par
            sprev, ptok = ps_rd[avbank]
            for kbi in range(NT):
                w = []
                if kbi == 0 and ptok > 0:
                    w.append((sprev, ptok))
                if kbi % 8 == 0:
                    w.append(wtev[kbi // 8])
                TS.op(lambda e, kbi=kbi, h=h, wtb=wtb, avbank=avbank: e.matmul(
                    psum[avbank][:, 0:64], wtb[:, kbi * 128:(kbi + 1) * 128],
                    vtm[:, kbi, h * 64:(h + 1) * 64],
                    start=(kbi == 0), stop=(kbi == NT - 1)),
                    waits=w, inc=(kbi == NT - 1))
            tav = TS.n
            return {"vs_sc": vmt, "vs_wbp": wtev[-1],
                    "ss_exp": sexp, "tav": tav, "par": par, "h": h, "qt": qt}

        for i in range(len(bands1) + 1):
            if i < len(bands1):
                st1[i] = front1(i)
            if i >= 1:
                fin1[(i - 1) % 2] = back1(i - 1, st1.pop(i - 1))
        epilogue1(fin1[len(bands1) % 2])
        epilogue1(fin1[(len(bands1) + 1) % 2])
        actx.close()
        barrier()

    def emit_l1_proj():
        with nc.sbuf_tensor("prjb", [128, NP, D], BF16) as prjb, \
             nc.sbuf_tensor("afml1", [128, 2, NP, 128], BF16) as afm, \
             nc.sbuf_tensor("xt2", [128, 2, D], F32) as xt2, \
             nc.sbuf_tensor("x2t", [128, 2, D], F32) as x2t:
            lwo = LDG.dma(lambda e: e.dma_start(out=prjb[:], in_=projT.rearrange("(j p) d -> p j d", p=128)))
            stP = [None, None]
            mmP = [0, 0]
            vvP = [0, 0]
            for i in range(NT):
                par = i % 2
                afm_, xt2_, x2t_ = afm[:, par], xt2[:, par], x2t[:, par]
                la = LD.dma(lambda e, i=i, afm_=afm_: e.dma_start(
                    out=afm_, in_=s_attnfm1.rearrange("(j p) t -> p j t", p=128)[:, :, i * 128:(i + 1) * 128]),
                    waits=[(TS, mmP[par])])
                lx = LD.dma(lambda e, i=i, xt2_=xt2_: e.dma_start(out=xt2_, in_=s_x1[i * 128:(i + 1) * 128, :]),
                            waits=[(VS, vvP[par])] + ([stP[par]] if stP[par] is not None else []))
                for n in range(2):
                    bk = 2 * par + n % 2
                    sprev, ptok = ps_rd[bk]
                    for kk in range(NP):
                        w = []
                        if kk == 0:
                            w = [la, lwo]
                            if ptok > 0:
                                w.append((sprev, ptok))
                        TS.op(lambda e, n=n, kk=kk, bk=bk, afm_=afm_: e.matmul(
                            psum[bk][:], afm_[:, kk, :], prjb[:, kk, n * 512:(n + 1) * 512],
                            start=(kk == 0), stop=(kk == NP - 1)),
                            waits=w, inc=(kk == NP - 1))
                    tmm = TS.n
                    ps_consume(bk, VS, lambda e, n=n, bk=bk, x2t_=x2t_, xt2_=xt2_: e.tensor_tensor(
                        x2t_[:, n * 512:(n + 1) * 512], psum[bk][:],
                        xt2_[:, n * 512:(n + 1) * 512], op=OP.add),
                        waits=[(TS, tmm), lx] + ([stP[par]] if stP[par] is not None else []))
                mmP[par] = TS.n
                vvP[par] = VS.n
                stP[par] = STQ.dma(lambda e, i=i, x2t_=x2t_: e.dma_start(
                    out=s_x2[i * 128:(i + 1) * 128, :], in_=x2t_), waits=[(VS, vvP[par])])
        barrier()

    def emit_ffnB1():
        with nc.sbuf_tensor("w2r1", [128, NF, D], BF16) as w2r, \
             nc.sbuf_tensor("acts1", [128, 2, NF, 128], BF16) as acts, \
             nc.sbuf_tensor("yt1", [128, 512], F32) as yt, \
             nc.sbuf_tensor("x2t2", [128, 2, D], F32) as x2t2:
            lw2 = LDG.dma(lambda e: e.dma_start(
                out=w2r[:], in_=s_lw2b.rearrange("(j p) d -> p j d", p=128)))
            stP = [None, None]
            mmP = [0, 0]
            vvP = [0, 0]
            for i in range(NT):
                par = i % 2
                acts_, x2t_ = acts[:, par], x2t2[:, par]
                la = LD.dma(lambda e, i=i, acts_=acts_: e.dma_start(
                    out=acts_, in_=s_act1[0, :, i, :, :].rearrange("f p t -> p f t")),
                    waits=[(TS, mmP[par])])
                lx = LD.dma(lambda e, i=i, x2t_=x2t_: e.dma_start(out=x2t_, in_=s_x2[i * 128:(i + 1) * 128, :]),
                            waits=[(VS, vvP[par])] + ([stP[par]] if stP[par] is not None else []))
                for n in range(2):
                    bk = 2 * par + n % 2
                    sprev, ptok = ps_rd[bk]
                    for kk in range(NF):
                        w = []
                        if kk == 0:
                            w = [la, lw2]
                            if ptok > 0:
                                w.append((sprev, ptok))
                        TS.op(lambda e, n=n, kk=kk, bk=bk, acts_=acts_: e.matmul(
                            psum[bk][:], acts_[:, kk, :], w2r[:, kk, n * 512:(n + 1) * 512],
                            start=(kk == 0), stop=(kk == NF - 1)),
                            waits=w, inc=(kk == NF - 1))
                    tmm = TS.n
                    ps_consume(bk, VS, lambda e, n=n, bk=bk: e.tensor_tensor(
                        yt[:], psum[bk][:], lbias_bc[:, n * 512:(n + 1) * 512], op=OP.add),
                        waits=[(TS, tmm), lx])
                    VS.op(lambda e, n=n, x2t_=x2t_: e.tensor_tensor(
                        x2t_[:, n * 512:(n + 1) * 512], yt[:],
                        x2t_[:, n * 512:(n + 1) * 512], op=OP.add))
                mmP[par] = TS.n
                vvP[par] = VS.n
                stP[par] = STQ.dma(lambda e, i=i, x2t_=x2t_: e.dma_start(
                    out=out[i * 128:(i + 1) * 128, :], in_=x2t_), waits=[(VS, vvP[par])])
        barrier()

    emit_consts()
    emit_wprep(W1, s_w1b, s_pw1b, D * FF, 0, tag="w1")
    emit_wprep(W2, s_w2b, s_pw2b, FF * D, 1, tag="w2")
    emit_bias_prep()
    xnfm = E(nc.sbuf_tensor("xnfm", [128, NP, T], BF16))
    emit_pre(xnfm)
    emit_attn(ffnA_spec=([s_w1b, s_pw1b], [b1fm, pb1fm], s_act,
                         ACT.Gelu_apprx_tanh, xnfm, "f0"))
    if "attnfm" in debug:
        d = dbg_out("attnfm", [NP, 128, T], BF16)
        STQ.dma(lambda e, d=d: e.dma_start(out=d, in_=s_attnfm))
        barrier()
    emit_wconv(Lw1, s_lw1b, D * FF, "l1")
    emit_wconv(Lw2, s_lw2b, FF * D, "l2")
    emit_wo_x1()
    emit_ffnB0()
    if "x1" in debug:
        d = dbg_out("x1", [T, D], F32)
        STQ.dma(lambda e, d=d: e.dma_start(out=d, in_=s_x1))
        barrier()
    emit_l1_ln(xnfm)
    emit_l1_attn(xnfm)
    emit_l1_proj()
    emit_ffnA([s_lw1b], [lb1fm], s_act1, ACT.Gelu, xnfm, "f1")
    emit_ffnB1()

    SY.wait(STQ.all_tokens())

    with nc.Block() as block:
        block.tensor(lambda e: TS.realize(e))
        block.vector(lambda e: VS.realize(e))
        block.scalar(lambda e: SS.realize(e))
        block.gpsimd(lambda e: GS.realize(e))
        block.sync(lambda e: SY.realize(e))

    ctx.close()
    return nc, dbg


def host_inputs(inputs, core):
    x = np.ascontiguousarray(inputs["L0_attn_x"][core])
    m = {
        "x0": x,
        "wqT": np.ascontiguousarray(inputs["L0_attn_wq"].T),
        "wkT": np.ascontiguousarray(inputs["L0_attn_wk"].T),
        "wvT": np.ascontiguousarray(inputs["L0_attn_wv"].T),
        "woT": np.ascontiguousarray(inputs["L0_attn_wo"].T),
        "W1": np.ascontiguousarray(inputs["L0_ffn_W1"]),
        "B1": np.ascontiguousarray(inputs["L0_ffn_b1"]),
        "W2": np.ascontiguousarray(inputs["L0_ffn_W2"]),
        "B2": np.ascontiguousarray(inputs["L0_ffn_b2"]),
        "LNG": np.ascontiguousarray(inputs["L0_ffn_ln_gamma"]),
        "LNB": np.ascontiguousarray(inputs["L0_ffn_ln_beta"]),
        "WR": np.ascontiguousarray(inputs["L0_ffn_Wr"]),
        "BR": np.ascontiguousarray(inputs["L0_ffn_br"]),
        "qkvT": np.ascontiguousarray(inputs["L1_attn_qkv_w"].T),
        "projT": np.ascontiguousarray(inputs["L1_attn_proj_w"].T),
        "projB": np.ascontiguousarray(inputs["L1_attn_proj_b"]),
        "Lw1": np.ascontiguousarray(inputs["L1_ffn_w1"]),
        "Lb1": np.ascontiguousarray(inputs["L1_ffn_b1"]),
        "Lw2": np.ascontiguousarray(inputs["L1_ffn_w2"]),
        "Lb2": np.ascontiguousarray(inputs["L1_ffn_b2"]),
    }
    return {k: np.asarray(v, dtype=np.float32) for k, v in m.items()}


def kernel(**inputs):
    nc, _ = build_program()
    in_maps = [host_inputs(inputs, c) for c in range(8)]
    res = run_bass_kernel_spmd(nc, in_maps, core_ids=list(range(8)))
    return np.stack([r["out"] for r in res.results], axis=0)


# version 14
# speedup vs baseline: 1.0424x; 1.0424x over previous
import math
import numpy as np
import concourse.bass as bass
import concourse.mybir as mybir
from concourse.bass_utils import run_bass_kernel_spmd

F32 = mybir.dt.float32
BF16 = mybir.dt.bfloat16
OP = mybir.AluOpType
ACT = mybir.ActivationFunctionType
AX = mybir.AxisListType

T = 2048
D = 1024
NT = T // 128
NP = D // 128
H = 16
HD = 64
FF = 4096
NF = FF // 128
NEG = -1.0e30
BIGM = 80.0
TINY = 1e-38
EPS = 1e-5
INVSQ = 1.0 / math.sqrt(HD)


def _extract_rounds(qt):
    mu = 0.032 * 128.0 * (qt + 1)
    ub = mu + 6.0 * math.sqrt(mu)
    return max(0, int(math.ceil((ub - 64.0) / 8.0)))


EXTR = [0] + [_extract_rounds(qt) for qt in range(1, NT)]
XCAP = 64


class Stream:
    def __init__(self, name, auto_serial):
        self.name = name
        self.auto = auto_serial
        self.n = 0
        self.ops = []
        self.sem = None

    def op(self, fn, waits=(), inc=True):
        prev = self.n if self.auto else 0
        if inc:
            self.n += 1
        self.ops.append((fn, tuple(waits), inc, prev))
        return self.n

    def wait(self, waits):
        self.ops.append((None, tuple(waits), False, 0))

    def realize(self, eng):
        for fn, waits, inc, serial in self.ops:
            if serial:
                eng.wait_ge(self.sem, serial)
            for s, val in waits:
                if val > 0:
                    eng.wait_ge(getattr(s, "sem", s), val)
            if fn is None:
                continue
            inst = fn(eng)
            if inc:
                inst.then_inc(self.sem, 1)


class DmaQueue:

    def __init__(self, stream, ring=12):
        self.stream = stream
        self.ring = ring
        self.sems = []
        self.uses = [0] * ring
        self.count = 0

    def dma(self, fn, waits=()):
        slot = self.count % self.ring
        self.count += 1
        prior = self.uses[slot]
        self.uses[slot] += 1
        sem = None

        q = self

        def emit(eng, fn=fn, slot=slot, prior=prior):
            if prior:
                eng.wait_ge(q.sems[slot], 16 * prior)
            inst = fn(eng)
            inst.then_inc(q.sems[slot], 16)
            return inst

        self.stream.op(emit, waits=waits, inc=False)
        return (_QSem(self, slot), 16 * (prior + 1))

    def all_tokens(self):
        return [(_QSem(self, i), 16 * u) for i, u in enumerate(self.uses) if u > 0]


class _QSem:

    def __init__(self, q, slot):
        self.q = q
        self.slot = slot

    @property
    def sem(self):
        return self.q.sems[self.slot]


def build_program(debug=()):
    nc = bass.Bass("TRN2", target_bir_lowering=False, debug=False)

    def din(name, shape):
        return nc.dram_tensor(name, shape, F32, kind="ExternalInput").ap()

    x0 = din("x0", [T, D])
    wqT = din("wqT", [D, D]); wkT = din("wkT", [D, D])
    wvT = din("wvT", [D, D]); woT = din("woT", [D, D])
    W1 = din("W1", [D, FF]); B1 = din("B1", [FF])
    W2 = din("W2", [FF, D]); B2 = din("B2", [D])
    LNG = din("LNG", [D]); LNB = din("LNB", [D])
    WR = din("WR", [D, 8]); BR = din("BR", [8])
    qkvT = din("qkvT", [D, 3 * D])
    projT = din("projT", [D, D]); projB = din("projB", [D])
    Lw1 = din("Lw1", [D, FF]); Lb1 = din("Lb1", [FF])
    Lw2 = din("Lw2", [FF, D]); Lb2 = din("Lb2", [D])
    out = nc.dram_tensor("out", [T, D], F32, kind="ExternalOutput").ap()

    s_w1b = nc.dram_tensor("s_w1b", [D, FF], BF16).ap()
    s_pw1b = nc.dram_tensor("s_pw1b", [D, FF], BF16).ap()
    s_w2b = nc.dram_tensor("s_w2b", [FF, D], BF16).ap()
    s_pw2b = nc.dram_tensor("s_pw2b", [FF, D], BF16).ap()
    s_lw1b = nc.dram_tensor("s_lw1b", [D, FF], BF16).ap()
    s_lw2b = nc.dram_tensor("s_lw2b", [FF, D], BF16).ap()
    s_act = nc.dram_tensor("s_act", [2, NF, NT, 128, 128], BF16).ap()
    s_act1 = nc.dram_tensor("s_act1", [1, NF, NT, 128, 128], BF16).ap()
    s_attnfm = nc.dram_tensor("s_attnfm", [NP, 128, T], BF16).ap()
    s_attnfm1 = nc.dram_tensor("s_attnfm1", [D, T], BF16).ap()
    s_x1 = nc.dram_tensor("s_x1", [T, D], F32).ap()
    s_x2 = nc.dram_tensor("s_x2", [T, D], F32).ap()

    dbg = {}

    def dbg_out(name, shape, dtype=F32):
        if name in debug:
            dbg[name] = nc.dram_tensor("dbg_" + name, shape, dtype,
                                       kind="ExternalOutput").ap()
        return dbg.get(name)

    TS = Stream("tensor", auto_serial=False)
    VS = Stream("vector", auto_serial=True)
    SS = Stream("scalar", auto_serial=True)
    GS = Stream("gpsimd", auto_serial=True)
    SY = Stream("sync", auto_serial=False)
    LD = DmaQueue(SY)
    STQ = DmaQueue(SS)
    LDG = DmaQueue(GS)
    streams = [TS, VS, SS, GS, SY]

    from contextlib import ExitStack
    ctx = ExitStack()
    E = ctx.enter_context
    for s, nm in [(TS, "semt"), (VS, "semv"), (SS, "sems"), (GS, "semg"), (SY, "semy")]:
        s.sem = E(nc.semaphore(nm))
    LD.sems = [E(nc.semaphore(f"semld{i}")) for i in range(LD.ring)]
    STQ.sems = [E(nc.semaphore(f"semst{i}")) for i in range(STQ.ring)]
    LDG.sems = [E(nc.semaphore(f"semldg{i}")) for i in range(LDG.ring)]

    def barrier():
        snap = [(s, s.n) for s in streams]
        qtoks = LD.all_tokens() + STQ.all_tokens() + LDG.all_tokens()
        for s in streams:
            w = [(o, v) for o, v in snap if o is not s and v > 0] + qtoks
            s.wait(w)

    eye_bf = E(nc.sbuf_tensor("eye_bf", [128, 128], BF16))
    eye_f = E(nc.sbuf_tensor("eye_f", [128, 128], F32))
    causal01 = E(nc.sbuf_tensor("causal01", [128, 128], F32))
    iota64 = E(nc.sbuf_tensor("iota64", [128, XCAP], F32))
    ones_r = E(nc.sbuf_tensor("ones_r", [1, 128], F32))
    ones_c = E(nc.sbuf_tensor("ones_c", [128, 1], F32))
    thrsb = E(nc.sbuf_tensor("thrsb", [128, 8], F32))
    negbig = E(nc.sbuf_tensor("negbig", [128, 1], F32))
    negbigm = E(nc.sbuf_tensor("negbigm", [128, 1], F32))
    negbig_b = E(nc.sbuf_tensor("negbig_b", [128, 1], BF16))
    neg80c = E(nc.sbuf_tensor("neg80c", [128, 1], F32))
    p0b = E(nc.sbuf_tensor("p0b", [128, NT], F32))
    b1fm = E(nc.sbuf_tensor("b1fm", [128, NF], F32))
    pb1fm = E(nc.sbuf_tensor("pb1fm", [128, NF], F32))
    lb1fm = E(nc.sbuf_tensor("lb1fm", [128, NF], F32))
    cvec_bc = E(nc.sbuf_tensor("cvec_bc", [128, D], F32))
    dvec_bc = E(nc.sbuf_tensor("dvec_bc", [128, D], F32))
    lbias_bc = E(nc.sbuf_tensor("lbias_bc", [128, D], F32))

    psum = [E(nc.psum_tensor(f"ps{i}", [128, 512], F32)) for i in range(6)]
    psumb = [E(nc.psum_tensor(f"psb{i}", [128, 1024], BF16)) for i in range(2)]
    psb_rd = {i: (VS, 0) for i in range(2)}

    def psb_produce(i, fn, waits=(), inc=True):
        s_, tok = psb_rd[i]
        w = list(waits)
        if tok > 0:
            w.append((s_, tok))
        return TS.op(fn, waits=w, inc=inc)

    def psb_consume(i, stream, fn, waits=()):
        tok = stream.op(fn, waits=waits)
        psb_rd[i] = (stream, tok)
        return tok
    ps_rd = {i: (VS, 0) for i in range(8)}

    def ps_produce(i, fn, waits=(), inc=True):
        s, tok = ps_rd[i]
        w = list(waits)
        if tok > 0:
            w.append((s, tok))
        return TS.op(fn, waits=w, inc=inc)

    def ps_consume(i, stream, fn, waits=()):
        tok = stream.op(fn, waits=waits)
        ps_rd[i] = (stream, tok)
        return tok

    def pbcast(dst_col, src11, bank, waits=()):
        w = src11.shape[-1]
        t = ps_produce(bank, lambda e: e.matmul(psum[bank][:, 0:w], ones_r[:],
                                                src11, start=True, stop=True),
                       waits=waits)
        return ps_consume(bank, VS, lambda e: e.tensor_copy(dst_col, psum[bank][:, 0:w]),
                          waits=[(TS, t)])

    def xpsum(dst11, src_col, bank, waits=()):
        t = ps_produce(bank, lambda e: e.matmul(psum[bank][0:1, 0:1], src_col,
                                                ones_c[:], start=True, stop=True),
                       waits=waits)
        return ps_consume(bank, VS, lambda e: e.tensor_copy(dst11, psum[bank][0:1, 0:1]),
                          waits=[(TS, t)])

    def emit_consts():
        g1 = GS.op(lambda e: e.iota(causal01[:], [[1, 128]], base=0,
                                    channel_multiplier=-1,
                                    allow_small_or_imprecise_dtypes=True))
        VS.op(lambda e: e.tensor_scalar(causal01[:], causal01[:], 0.0, None,
                                        op0=OP.is_gt), waits=[(GS, g1)])
        g2 = GS.op(lambda e: e.iota(iota64[:], [[1, XCAP]], base=0,
                                    channel_multiplier=0,
                                    allow_small_or_imprecise_dtypes=True))
        g3 = GS.op(lambda e: e.iota(eye_f[:], [[1, 128]], base=0,
                                    channel_multiplier=-1,
                                    allow_small_or_imprecise_dtypes=True))
        VS.op(lambda e: e.tensor_scalar(eye_f[:], eye_f[:], 0.0, None,
                                        op0=OP.is_equal), waits=[(GS, g3)])
        VS.op(lambda e: e.tensor_copy(eye_bf[:], eye_f[:]))
        VS.op(lambda e: e.memset(ones_r[:], 1.0))
        VS.op(lambda e: e.memset(ones_c[:], 1.0))
        VS.op(lambda e: e.memset(thrsb[:], 0.0))
        VS.op(lambda e: e.memset(negbig[:], NEG))
        VS.op(lambda e: e.memset(negbigm[:], -BIGM))
        VS.op(lambda e: e.memset(negbig_b[:], NEG))
        VS.op(lambda e: e.memset(neg80c[:], -80.0))
        _ = g2

    def emit_wprep(w_hbm, wb_hbm, pwb_hbm, n, thr_idx, sub_iters=24, full_iters=26, tag=""):
        width = n // 128
        k = max(1, int(n * 0.05))
        nchunks = max(1, width // 4096)
        cw = width // nchunks
        wv = w_hbm.rearrange("(p a) c -> p (a c)", p=128) if w_hbm.ndim == 2 else w_hbm
        wbv = wb_hbm.rearrange("(p a) c -> p (a c)", p=128) if wb_hbm.ndim == 2 else wb_hbm
        pwbv = pwb_hbm.rearrange("(p a) c -> p (a c)", p=128) if pwb_hbm.ndim == 2 else pwb_hbm

        with nc.sbuf_tensor("aw" + tag, [128, width], F32) as aw, \
             nc.sbuf_tensor("wchunk" + tag, [128, cw], F32) as wchunk, \
             nc.sbuf_tensor("bchunk" + tag, [128, cw], BF16) as bchunk, \
             nc.sbuf_tensor("mscr" + tag, [128, cw], BF16) as mscr, \
             nc.sbuf_tensor("cnt8" + tag, [128, nchunks], F32) as cnt8, \
             nc.sbuf_tensor("tneg" + tag, [128, 1], F32) as tneg, \
             nc.sbuf_tensor("taub" + tag, [128, 1], F32) as taub, \
             nc.sbuf_tensor("mscrb" + tag, [128, 4096], BF16) as mscrb, \
             nc.sbuf_tensor("cnt8b" + tag, [128, 8], F32) as cnt8b, \
             nc.sbuf_tensor("cnt1" + tag, [128, 1], F32) as cnt1, \
             nc.sbuf_tensor("st" + tag, [1, 16], F32) as st, \
             nc.sbuf_tensor("tau" + tag, [128, 1], F32) as tau:

            prev_st = None
            st_toks = []
            sa = 0
            for c in range(nchunks):
                lt = LD.dma(lambda e, c=c: e.dma_start(out=wchunk[:], in_=wv[:, c * cw:(c + 1) * cw]),
                            waits=[(VS, VS.n)] + ([(SS, sa)] if sa else []))
                sa = SS.op(lambda e, c=c: e.activation(aw[:, c * cw:(c + 1) * cw], wchunk[:],
                                                       ACT.Abs), waits=[lt])
                vb = VS.op(lambda e: e.tensor_copy(bchunk[:], wchunk[:]),
                           waits=[lt] + ([prev_st] if prev_st is not None else []))
                prev_st = STQ.dma(lambda e, c=c: e.dma_start(out=wbv[:, c * cw:(c + 1) * cw], in_=bchunk[:]),
                                  waits=[(VS, vb)])
                st_toks.append(prev_st)
            VS.wait([(SS, sa)])

            VS.op(lambda e: e.memset(st[:], 0.0))
            VS.op(lambda e: e.memset(st[0:1, 1:2], 64.0))
            VS.op(lambda e: e.memset(st[0:1, 3:4], 64.0))
            ntotf = float(n)

            def mid_of(lo_i, hi_i, mid_i):
                VS.op(lambda e: e.tensor_tensor(st[0:1, mid_i:mid_i + 1],
                                                st[0:1, lo_i:lo_i + 1],
                                                st[0:1, hi_i:hi_i + 1], op=OP.add))
                VS.op(lambda e: e.tensor_scalar(st[0:1, mid_i:mid_i + 1],
                                                st[0:1, mid_i:mid_i + 1], 0.5,
                                                None, op0=OP.mult))

            def update(lo_i, hi_i, mid_i, cnt_i, s0, s1, t_lo, t_hi):
                VS.op(lambda e, t_lo=t_lo: e.tensor_scalar(
                    st[0:1, s0:s0 + 1], st[0:1, cnt_i:cnt_i + 1], t_lo, None, op0=OP.is_ge))
                VS.op(lambda e: e.tensor_tensor(st[0:1, s1:s1 + 1], st[0:1, mid_i:mid_i + 1],
                                                st[0:1, lo_i:lo_i + 1], op=OP.subtract))
                VS.op(lambda e: e.scalar_tensor_tensor(
                    st[0:1, lo_i:lo_i + 1], st[0:1, s1:s1 + 1], st[0:1, s0:s0 + 1],
                    st[0:1, lo_i:lo_i + 1], op0=OP.mult, op1=OP.add))
                VS.op(lambda e, t_hi=t_hi: e.tensor_scalar(
                    st[0:1, s0:s0 + 1], st[0:1, cnt_i:cnt_i + 1], t_hi, None, op0=OP.is_lt))
                VS.op(lambda e: e.tensor_tensor(st[0:1, s1:s1 + 1], st[0:1, mid_i:mid_i + 1],
                                                st[0:1, hi_i:hi_i + 1], op=OP.subtract))
                VS.op(lambda e: e.scalar_tensor_tensor(
                    st[0:1, hi_i:hi_i + 1], st[0:1, s1:s1 + 1], st[0:1, s0:s0 + 1],
                    st[0:1, hi_i:hi_i + 1], op0=OP.mult, op1=OP.add))

            def sub_bisect(lo_i, hi_i, target, iters):
                step = 64
                aw_sub = aw.rearrange("p (a s) -> p a s", s=step)[:, :, 0]
                for _ in range(iters):
                    mid_of(lo_i, hi_i, 4)
                    pbcast(tau[:], st[0:1, 4:5], 5, waits=[(VS, VS.n)])
                    vtok = VS.op(lambda e, a=aw_sub: e.tensor_scalar(
                        mscr[:, 0:width // step], a, tau[:], 0.0,
                        op0=OP.is_gt, op1=OP.add, accum_out=cnt1[:]))
                    xpsum(st[0:1, 5:6], cnt1[:], 5, waits=[(VS, vtok)])
                    update(lo_i, hi_i, 4, 5, 6, 7, float(target), float(target))

            def joint_full(k_t, k_b, iters):
                for it_ in range(iters):
                    exact_t = it_ >= iters - 5
                    mid_of(0, 1, 4)
                    pbcast(tau[:], st[0:1, 4:5], 5, waits=[(VS, VS.n)])
                    VS.op(lambda e: e.tensor_scalar(tneg[:], tau[:], -1.0, None, op0=OP.mult))
                    mid_of(2, 3, 8)
                    pbcast(taub[:], st[0:1, 8:9], 5, waits=[(VS, VS.n)])
                    if not exact_t:
                        stok = 0
                        for c in range(nchunks):
                            stok = SS.op(lambda e, c=c: e.activation(
                                mscr[:], aw[:, c * cw:(c + 1) * cw], ACT.Sign, bias=tneg[:],
                                accum_out=cnt8[:, c:c + 1]), waits=[(VS, VS.n)])
                        for c in range(nchunks):
                            VS.op(lambda e, c=c: e.tensor_scalar(
                                mscrb[:], aw[:, c * cw:(c + 1) * cw], taub[:], 0.0,
                                op0=OP.is_gt, op1=OP.add, accum_out=cnt8b[:, c:c + 1]))
                        vtop = VS.op(lambda e: e.reduce_sum(cnt1[:], cnt8[:], axis=AX.X),
                                     waits=[(SS, stok)])
                        xpsum(st[0:1, 5:6], cnt1[:], 5, waits=[(VS, vtop)])
                        VS.op(lambda e: e.tensor_scalar(st[0:1, 5:6], st[0:1, 5:6], 0.5,
                                                        0.5 * ntotf, op0=OP.mult, op1=OP.add))
                    else:
                        for c in range(nchunks):
                            VS.op(lambda e, c=c: e.tensor_scalar(
                                mscr[:], aw[:, c * cw:(c + 1) * cw], tau[:], 0.0,
                                op0=OP.is_gt, op1=OP.add, accum_out=cnt8[:, c:c + 1]))
                        vtop = VS.op(lambda e: e.reduce_sum(cnt1[:], cnt8[:], axis=AX.X))
                        xpsum(st[0:1, 5:6], cnt1[:], 5, waits=[(VS, vtop)])
                        for c in range(nchunks):
                            VS.op(lambda e, c=c: e.tensor_scalar(
                                mscrb[:], aw[:, c * cw:(c + 1) * cw], taub[:], 0.0,
                                op0=OP.is_gt, op1=OP.add, accum_out=cnt8b[:, c:c + 1]))
                    vbot = VS.op(lambda e: e.reduce_sum(cnt1[:], cnt8b[:], axis=AX.X))
                    xpsum(st[0:1, 9:10], cnt1[:], 5, waits=[(VS, vbot)])
                    dz = 0.0 if exact_t else 0.6
                    update(0, 1, 4, 5, 6, 7, k_t + dz, k_t - dz)
                    update(2, 3, 8, 9, 10, 11, float(k_b), float(k_b))

            ssub = 64 if width >= 4096 else 1
            kb = n - k + 1
            if ssub > 1:
                sub_m = 10.0 * math.sqrt(max(k / ssub, 1.0))
                sub_bisect(0, 1, k / ssub + sub_m, sub_iters)
                sub_bisect(2, 3, max(k / ssub - sub_m, 0.5), sub_iters)
                VS.op(lambda e: e.tensor_copy(st[0:1, 1:2], st[0:1, 3:4]))
                VS.op(lambda e: e.memset(st[0:1, 2:3], 0.0))
                VS.op(lambda e: e.memset(st[0:1, 3:4], 64.0))
                sub_bisect(2, 3, kb / ssub + sub_m, sub_iters)
                VS.op(lambda e: e.memset(st[0:1, 8:9], 0.0))
                VS.op(lambda e: e.memset(st[0:1, 9:10], 64.0))
                sub_bisect(8, 9, max(kb / ssub - sub_m, 0.5), sub_iters)
                VS.op(lambda e: e.tensor_copy(st[0:1, 3:4], st[0:1, 9:10]))
                joint_full(float(k), float(kb), full_iters)
            else:
                joint_full(float(k), float(kb), 40)
            VS.op(lambda e, ti=thr_idx: e.tensor_copy(thrsb[0:1, 2 * ti:2 * ti + 1],
                                                      st[0:1, 0:1]))
            VS.op(lambda e, ti=thr_idx: e.tensor_copy(thrsb[0:1, 2 * ti + 1:2 * ti + 2],
                                                      st[0:1, 3:4]))
            pbcast(thrsb[:, 2 * thr_idx:2 * thr_idx + 2],
                   thrsb[0:1, 2 * thr_idx:2 * thr_idx + 2], 5, waits=[(VS, VS.n)])
            prev_st = None
            for c in range(nchunks):
                lt = LD.dma(lambda e, c=c: e.dma_start(out=bchunk[:], in_=wbv[:, c * cw:(c + 1) * cw]),
                            waits=[(VS, VS.n), st_toks[c], st_toks[-1]]
                                  + ([prev_st] if prev_st is not None else []))
                VS.op(lambda e, c=c: e.tensor_scalar(wchunk[:], aw[:, c * cw:(c + 1) * cw],
                                                     thrsb[:, 2 * thr_idx:2 * thr_idx + 1],
                                                     None, op0=OP.is_gt),
                      waits=[lt])
                VS.op(lambda e, c=c: e.scalar_tensor_tensor(
                    wchunk[:], aw[:, c * cw:(c + 1) * cw],
                    thrsb[:, 2 * thr_idx + 1:2 * thr_idx + 2], wchunk[:],
                    op0=OP.is_le, op1=OP.subtract))
                VS.op(lambda e: e.tensor_scalar(wchunk[:], wchunk[:], 0.8, None, op0=OP.mult))
                vp = VS.op(lambda e: e.scalar_tensor_tensor(mscr[:], bchunk[:], 1.0,
                                                            wchunk[:], op0=OP.mult,
                                                            op1=OP.mult))
                vp = VS.op(lambda e: e.tensor_tensor(mscr[:], mscr[:], bchunk[:], op=OP.add))
                prev_st = STQ.dma(lambda e, c=c: e.dma_start(out=pwbv[:, c * cw:(c + 1) * cw], in_=mscr[:]),
                                  waits=[(VS, vp)])
        barrier()

    def emit_bias_prep():
        with nc.sbuf_tensor("b1a", [128, 128], F32) as b1a, \
             nc.sbuf_tensor("ms", [128, NF], F32) as ms, \
             nc.sbuf_tensor("cnt1", [128, 1], F32) as cnt1, \
             nc.sbuf_tensor("st", [1, 8], F32) as st, \
             nc.sbuf_tensor("tau", [128, 1], F32) as tau, \
             nc.sbuf_tensor("b2r", [1, D], F32) as b2r, \
             nc.sbuf_tensor("pb2r", [1, D], F32) as pb2r, \
             nc.sbuf_tensor("lbr", [1, D], F32) as lbr:

            lt = LD.dma(lambda e: e.dma_start(
                out=b1a[0:NF, 0:128].rearrange("a b -> a b"),
                in_=B1.rearrange("(f p) -> f p", p=128)))
            tb1 = ps_produce(4, lambda e: e.transpose(
                psum[4][:, 0:NF], b1a[0:NF, 0:128], eye_f[0:NF, 0:NF]),
                waits=[lt])
            ps_consume(4, VS, lambda e: e.tensor_copy(b1fm[:], psum[4][:, 0:NF]),
                       waits=[(TS, tb1)])
            lt2 = LD.dma(lambda e: e.dma_start(
                out=b1a[0:NF, 0:128].rearrange("a b -> a b"),
                in_=Lb1.rearrange("(f p) -> f p", p=128)),
                waits=[(VS, VS.n)])
            tb2 = ps_produce(4, lambda e: e.transpose(
                psum[4][:, 0:NF], b1a[0:NF, 0:128], eye_f[0:NF, 0:NF]),
                waits=[lt2])
            ps_consume(4, VS, lambda e: e.tensor_copy(lb1fm[:], psum[4][:, 0:NF]),
                       waits=[(TS, tb2)])

            def small_select(src, n, iters=40):
                pass

            k1 = max(1, int(FF * 0.05))
            va = SS.op(lambda e: e.activation(b1a[:, 0:NF], b1fm[:], ACT.Abs),
                       waits=[(VS, VS.n)])
            VS.wait([(SS, va)])
            VS.op(lambda e: e.memset(st[:], 0.0))
            VS.op(lambda e: e.memset(st[0:1, 1:2], 64.0))
            VS.op(lambda e: e.memset(st[0:1, 3:4], 64.0))

            def bis(lo_i, hi_i, target, iters, src, w):
                for _ in range(iters):
                    VS.op(lambda e: e.tensor_tensor(st[0:1, 4:5], st[0:1, lo_i:lo_i + 1],
                                                    st[0:1, hi_i:hi_i + 1], op=OP.add))
                    VS.op(lambda e: e.tensor_scalar(st[0:1, 4:5], st[0:1, 4:5], 0.5,
                                                    None, op0=OP.mult))
                    pbcast(tau[:], st[0:1, 4:5], 5, waits=[(VS, VS.n)])
                    vtok = VS.op(lambda e: e.tensor_scalar(ms[:, 0:w], src[:, 0:w], tau[:],
                                                           0.0, op0=OP.is_gt, op1=OP.add,
                                                           accum_out=cnt1[:]))
                    xpsum(st[0:1, 5:6], cnt1[:], 5, waits=[(VS, vtok)])
                    VS.op(lambda e: e.tensor_scalar(st[0:1, 6:7], st[0:1, 5:6],
                                                    float(target), None, op0=OP.is_ge))
                    VS.op(lambda e: e.tensor_tensor(st[0:1, 7:8], st[0:1, 4:5],
                                                    st[0:1, lo_i:lo_i + 1], op=OP.subtract))
                    VS.op(lambda e: e.scalar_tensor_tensor(
                        st[0:1, lo_i:lo_i + 1], st[0:1, 7:8], st[0:1, 6:7],
                        st[0:1, lo_i:lo_i + 1], op0=OP.mult, op1=OP.add))
                    VS.op(lambda e: e.tensor_tensor(st[0:1, 7:8], st[0:1, 4:5],
                                                    st[0:1, hi_i:hi_i + 1], op=OP.subtract))
                    VS.op(lambda e: e.tensor_scalar(st[0:1, 6:7], st[0:1, 6:7], -1.0, 1.0,
                                                    op0=OP.mult, op1=OP.add))
                    VS.op(lambda e: e.scalar_tensor_tensor(
                        st[0:1, hi_i:hi_i + 1], st[0:1, 7:8], st[0:1, 6:7],
                        st[0:1, hi_i:hi_i + 1], op0=OP.mult, op1=OP.add))

            bis(0, 1, k1, 40, b1a, NF)
            bis(2, 3, FF - k1 + 1, 40, b1a, NF)
            b1a_n = b1a[:, 0:NF]
            pbcast(tau[:], st[0:1, 0:1], 5, waits=[(VS, VS.n)])
            VS.op(lambda e: e.tensor_scalar(ms[:], b1a[:, 0:NF], tau[:], None, op0=OP.is_gt))
            pbcast(tau[:], st[0:1, 3:4], 5, waits=[(VS, VS.n)])
            VS.op(lambda e: e.scalar_tensor_tensor(ms[:], b1a[:, 0:NF], tau[:], ms[:],
                                                   op0=OP.is_le, op1=OP.subtract))
            VS.op(lambda e: e.tensor_scalar(ms[:], ms[:], 0.8, None, op0=OP.mult))
            VS.op(lambda e: e.tensor_mul(ms[:], ms[:], b1fm[:]))
            VS.op(lambda e: e.tensor_add(pb1fm[:], ms[:], b1fm[:]))

            lt3 = LD.dma(lambda e: e.dma_start(out=b2r[:], in_=B2.rearrange("(o d) -> o d", o=1)))
            k2 = max(1, int(D * 0.05))
            va2 = SS.op(lambda e: e.activation(pb2r[:], b2r[:], ACT.Abs),
                        waits=[lt3, (VS, VS.n)])
            VS.wait([(SS, va2)])
            _ = (va, va2)
            VS.op(lambda e: e.memset(st[:], 0.0))
            VS.op(lambda e: e.memset(st[0:1, 1:2], 64.0))
            VS.op(lambda e: e.memset(st[0:1, 3:4], 64.0))

            def bis1(lo_i, hi_i, target, iters):
                for _ in range(iters):
                    VS.op(lambda e: e.tensor_tensor(st[0:1, 4:5], st[0:1, lo_i:lo_i + 1],
                                                    st[0:1, hi_i:hi_i + 1], op=OP.add))
                    VS.op(lambda e: e.tensor_scalar(st[0:1, 4:5], st[0:1, 4:5], 0.5,
                                                    None, op0=OP.mult))
                    VS.op(lambda e: e.tensor_scalar(lbr[:], pb2r[:], st[0:1, 4:5], 0.0,
                                                    op0=OP.is_gt, op1=OP.add,
                                                    accum_out=st[0:1, 5:6]))
                    VS.op(lambda e: e.tensor_scalar(st[0:1, 6:7], st[0:1, 5:6],
                                                    float(target), None, op0=OP.is_ge))
                    VS.op(lambda e: e.tensor_tensor(st[0:1, 7:8], st[0:1, 4:5],
                                                    st[0:1, lo_i:lo_i + 1], op=OP.subtract))
                    VS.op(lambda e: e.scalar_tensor_tensor(
                        st[0:1, lo_i:lo_i + 1], st[0:1, 7:8], st[0:1, 6:7],
                        st[0:1, lo_i:lo_i + 1], op0=OP.mult, op1=OP.add))
                    VS.op(lambda e: e.tensor_tensor(st[0:1, 7:8], st[0:1, 4:5],
                                                    st[0:1, hi_i:hi_i + 1], op=OP.subtract))
                    VS.op(lambda e: e.tensor_scalar(st[0:1, 6:7], st[0:1, 6:7], -1.0, 1.0,
                                                    op0=OP.mult, op1=OP.add))
                    VS.op(lambda e: e.scalar_tensor_tensor(
                        st[0:1, hi_i:hi_i + 1], st[0:1, 7:8], st[0:1, 6:7],
                        st[0:1, hi_i:hi_i + 1], op0=OP.mult, op1=OP.add))

            bis1(0, 1, k2, 40)
            bis1(2, 3, D - k2 + 1, 40)
            VS.op(lambda e: e.tensor_scalar(lbr[:], pb2r[:], st[0:1, 0:1], None, op0=OP.is_gt))
            VS.op(lambda e: e.scalar_tensor_tensor(lbr[:], pb2r[:], st[0:1, 3:4], lbr[:],
                                                   op0=OP.is_le, op1=OP.subtract))
            VS.op(lambda e: e.tensor_scalar(lbr[:], lbr[:], 0.8, None, op0=OP.mult))
            VS.op(lambda e: e.tensor_mul(lbr[:], lbr[:], b2r[:]))
            VS.op(lambda e: e.tensor_add(pb2r[:], lbr[:], b2r[:]))
            pbcast(cvec_bc[:, 0:512], b2r[0:1, 0:512], 5, waits=[(VS, VS.n)])
            pbcast(cvec_bc[:, 512:1024], b2r[0:1, 512:1024], 5)
            VS.op(lambda e: e.tensor_sub(pb2r[:], pb2r[:], b2r[:]))
            pbcast(dvec_bc[:, 0:512], pb2r[0:1, 0:512], 5, waits=[(VS, VS.n)])
            pbcast(dvec_bc[:, 512:1024], pb2r[0:1, 512:1024], 5)
            lt4 = LD.dma(lambda e: e.dma_start(out=b2r[:], in_=projB.rearrange("(o d) -> o d", o=1)),
                         waits=[(VS, VS.n), (TS, TS.n)])
            lt5 = LD.dma(lambda e: e.dma_start(out=pb2r[:], in_=Lb2.rearrange("(o d) -> o d", o=1)),
                         waits=[(VS, VS.n), (TS, TS.n)])
            vv = VS.op(lambda e: e.tensor_add(lbr[:], b2r[:], pb2r[:]),
                       waits=[lt4, lt5])
            pbcast(lbias_bc[:, 0:512], lbr[0:1, 0:512], 5, waits=[(VS, vv)])
            pbcast(lbias_bc[:, 512:1024], lbr[0:1, 512:1024], 5)
            _ = lt4
            if "thr" in debug:
                d = dbg_out("thr", [128, 8])
                STQ.dma(lambda e, d=d: e.dma_start(out=d, in_=thrsb[:]), waits=[(VS, VS.n)])
            if "biases" in debug:
                d = dbg_out("biases", [128, 2 * NF + 2 * D])
                vv2 = VS.op(lambda e: e.engine_nop(), inc=True)
                STQ.dma(lambda e, d=d: e.dma_start(out=d[:, 0:NF], in_=b1fm[:]), waits=[(VS, vv2)])
                STQ.dma(lambda e, d=d: e.dma_start(out=d[:, NF:2 * NF], in_=pb1fm[:]))
                STQ.dma(lambda e, d=d: e.dma_start(out=d[:, 2 * NF:2 * NF + D], in_=cvec_bc[:]))
                STQ.dma(lambda e, d=d: e.dma_start(out=d[:, 2 * NF + D:2 * NF + 2 * D], in_=dvec_bc[:]))
        barrier()

    def emit_pre(xnfm):
        with nc.sbuf_tensor("xt", [128, D], F32) as xt, \
             nc.sbuf_tensor("gam_bc", [128, D], F32) as gam_bc, \
             nc.sbuf_tensor("bet_bc", [128, D], F32) as bet_bc, \
             nc.sbuf_tensor("xc", [128, D], F32) as xc, \
             nc.sbuf_tensor("sqs", [128, D], F32) as sqs, \
             nc.sbuf_tensor("hfm", [128, D], BF16) as hfm, \
             nc.sbuf_tensor("mu", [128, 1], F32) as mu, \
             nc.sbuf_tensor("musq", [128, 1], F32) as musq, \
             nc.sbuf_tensor("ssq", [128, 1], F32) as ssq, \
             nc.sbuf_tensor("veps", [128, 1], F32) as veps, \
             nc.sbuf_tensor("rstd", [128, 1], F32) as rstd, \
             nc.sbuf_tensor("f2", [128, 1], F32) as f2, \
             nc.sbuf_tensor("sc8", [128, 8], F32) as sc8, \
             nc.sbuf_tensor("wrsb", [128, NP, 8], BF16) as wrsb, \
             nc.sbuf_tensor("grow", [1, D], F32) as grow, \
             nc.sbuf_tensor("brow", [1, D], F32) as brow, \
             nc.sbuf_tensor("brr", [1, 8], F32) as brr, \
             nc.sbuf_tensor("brbc", [128, 8], F32) as brbc:

            tg = LD.dma(lambda e: e.dma_start(out=grow[:], in_=LNG.rearrange("(o d) -> o d", o=1)))
            tb = LD.dma(lambda e: e.dma_start(out=brow[:], in_=LNB.rearrange("(o d) -> o d", o=1)))
            tr = LD.dma(lambda e: e.dma_start(out=brr[:], in_=BR.rearrange("(o d) -> o d", o=1)))
            tw = LDG.dma(lambda e: e.dma_start(out=wrsb[:], in_=WR.rearrange("(j p) e -> p j e", p=128)))
            pbcast(gam_bc[:, 0:512], grow[0:1, 0:512], 5, waits=[tg])
            pbcast(gam_bc[:, 512:1024], grow[0:1, 512:1024], 5)
            pbcast(bet_bc[:, 0:512], brow[0:1, 0:512], 5, waits=[tb])
            pbcast(bet_bc[:, 512:1024], brow[0:1, 512:1024], 5)
            pbcast(brbc[:], brr[0:1, :], 5, waits=[tr])

            sq_prev = 0
            tposx_prev = 0
            tposh_prev = 0
            for i in range(NT):
                lx = LD.dma(lambda e, i=i: e.dma_start(out=xt[:], in_=x0[i * 128:(i + 1) * 128, :]),
                            waits=[(VS, VS.n)] + ([(SS, sq_prev)] if sq_prev else []))
                VS.op(lambda e: e.reduce_sum(mu[:], xt[:], axis=AX.X), waits=[lx])
                VS.op(lambda e: e.tensor_scalar(mu[:], mu[:], 1.0 / D, None, op0=OP.mult))
                sq = SS.op(lambda e: e.activation(sqs[:], xt[:], ACT.Square, accum_out=ssq[:]),
                           waits=[lx, (VS, VS.n)])
                sq_prev = sq
                VS.op(lambda e: e.tensor_mul(musq[:], mu[:], mu[:]), waits=[(SS, sq)])
                VS.op(lambda e: e.scalar_tensor_tensor(veps[:], ssq[:], 1.0 / D, musq[:],
                                                       op0=OP.mult, op1=OP.subtract))
                VS.op(lambda e: e.tensor_scalar(veps[:], veps[:], EPS, None, op0=OP.add))
                s2 = SS.op(lambda e: e.activation(rstd[:], veps[:], ACT.Sqrt),
                           waits=[(VS, VS.n)])
                VS.op(lambda e: e.reciprocal(rstd[:], rstd[:]), waits=[(SS, s2)])
                vb = VS.op(lambda e: e.tensor_scalar(xc[:], xt[:], mu[:], rstd[:],
                                                     op0=OP.subtract, op1=OP.mult),
                           waits=([(TS, tposh_prev)] if tposh_prev else []))
                for j in range(NP):
                    bk = j % 2
                    t = ps_produce(bk, lambda e, j=j, bk=bk: e.transpose(
                        psum[bk][:, 0:128], xc[:, j * 128:(j + 1) * 128], eye_f[:]),
                        waits=[(VS, vb)])
                    ps_consume(bk, VS, lambda e, j=j, i=i, bk=bk: e.tensor_copy(
                        xnfm[:, j, i * 128:(i + 1) * 128], psum[bk][:, 0:128]),
                        waits=[(TS, t)])
                    tposx_prev = t
                VS.op(lambda e: e.reciprocal(f2[:], veps[:]))
                VS.op(lambda e: e.tensor_scalar(f2[:], f2[:], -EPS, 1.0,
                                                op0=OP.mult, op1=OP.add))
                VS.op(lambda e: e.tensor_scalar(f2[:], f2[:], EPS, None, op0=OP.add))
                s3 = SS.op(lambda e: e.activation(f2[:], f2[:], ACT.Sqrt), waits=[(VS, VS.n)])
                VS.op(lambda e: e.reciprocal(f2[:], f2[:]), waits=[(SS, s3)])
                VS.op(lambda e: e.scalar_tensor_tensor(sqs[:], xc[:], f2[:], gam_bc[:],
                                                       op0=OP.mult, op1=OP.mult))
                vh = VS.op(lambda e: e.tensor_tensor(xc[:], sqs[:], bet_bc[:], op=OP.add),
                           waits=[(TS, tposx_prev)])
                for j in range(NP):
                    bk = 2 + (j % 2)
                    t = ps_produce(bk, lambda e, j=j, bk=bk: e.transpose(
                        psum[bk][:, 0:128], xc[:, j * 128:(j + 1) * 128], eye_f[:]),
                        waits=[(VS, vh)])
                    ps_consume(bk, VS, lambda e, j=j, bk=bk: e.tensor_copy(
                        hfm[:, j * 128:(j + 1) * 128], psum[bk][:, 0:128]),
                        waits=[(TS, t)])
                    tposh_prev = t
                vhf = VS.n
                s_rd = ps_rd[4]
                for j in range(NP):
                    w = ([(VS, vhf), tw] if j == 0 else [])
                    if j == 0 and s_rd[1] > 0:
                        w.append(s_rd)
                    TS.op(lambda e, j=j: e.matmul(
                        psum[4][:, 0:8], hfm[:, j * 128:(j + 1) * 128], wrsb[:, j, :],
                        start=(j == 0), stop=(j == NP - 1)),
                        waits=w, inc=(j == NP - 1))
                tt = TS.n
                VS.op(lambda e: e.tensor_add(sc8[:], psum[4][:, 0:8], brbc[:]),
                      waits=[(TS, tt)])
                ps_rd[4] = (VS, VS.n)
                VS.op(lambda e: e.tensor_scalar(sc8[:], sc8[:], sc8[:, 0:1], None,
                                                op0=OP.subtract))
                s4 = SS.op(lambda e: e.activation(sc8[:], sc8[:], ACT.Exp, accum_out=f2[:]),
                           waits=[(VS, VS.n)])
                VS.op(lambda e, i=i: e.reciprocal(p0b[:, i:i + 1], f2[:]), waits=[(SS, s4)])

            if "xnfm" in debug:
                d = dbg_out("xnfm", [128, NP * T], BF16)
                STQ.dma(lambda e, d=d: e.dma_start(
                    out=d.rearrange("p (j t) -> p j t", j=NP), in_=xnfm[:]),
                    waits=[(VS, VS.n)])
            if "p0" in debug:
                d = dbg_out("p0", [128, NT])
                STQ.dma(lambda e, d=d: e.dma_start(out=d, in_=p0b[:]), waits=[(VS, VS.n)])
        barrier()


    def emit_attn(ffnA_spec=None):
        from contextlib import ExitStack
        actx = ExitStack()
        A = actx.enter_context
        qfm = A(nc.sbuf_tensor("qfm", [128, NP, T], BF16))
        kfm = A(nc.sbuf_tensor("kfm", [128, NP, T], BF16))
        vtm = A(nc.sbuf_tensor("vtm", [128, NT, D], BF16))
        if True:

            with nc.sbuf_tensor("wprj", [128, NP, D], BF16) as wprj, \
                 nc.sbuf_tensor("ptile", [128, 512], F32) as ptile, \
                 nc.sbuf_tensor("pabs", [128, 512], F32) as pabs, \
                 nc.sbuf_tensor("m8", [128, 8, 8], F32) as m8:
                for (wT_src, dstfm) in ((wqT, qfm), (wkT, kfm), (wvT, None)):
                    lw = LDG.dma(lambda e, wT_src=wT_src: e.dma_start(
                        out=wprj[:], in_=wT_src.rearrange("(j p) d -> p j d", p=128)),
                        waits=[(TS, TS.n)])
                    wsb = wprj
                    for i in range(NT):
                        for n in range(2):
                            bk = n % 2
                            for kk in range(NP):
                                w = [lw] if kk == 0 else []
                                if kk == 0:
                                    sprev, ptok = ps_rd[bk]
                                    if ptok > 0:
                                        w.append((sprev, ptok))
                                TS.op(lambda e, i=i, n=n, kk=kk, bk=bk, wsb=wsb: e.matmul(
                                    psum[bk][:], xnfm[:, kk, i * 128:(i + 1) * 128],
                                    wsb[:, kk, n * 512:(n + 1) * 512],
                                    start=(kk == 0), stop=(kk == NP - 1)),
                                    waits=w, inc=(kk == NP - 1))
                            tmm = TS.n
                            evp = ps_consume(bk, VS, lambda e, bk=bk: e.tensor_copy(ptile[:], psum[bk][:]),
                                             waits=[(TS, tmm)])
                            sab = SS.op(lambda e: e.activation(pabs[:], ptile[:], ACT.Abs),
                                        waits=[(VS, evp)])
                            VS.wait([(SS, sab)])
                            for hh in range(8):
                                VS.op(lambda e, hh=hh: e.max(out=m8[:, hh, :], in_=pabs[:, hh * 64:(hh + 1) * 64]))
                            p3 = pabs.rearrange("p (h d) -> p h d", h=8)
                            VS.op(lambda e, p3=p3: e.tensor_tensor(
                                p3, p3, m8[:, :, 1:2].to_broadcast([128, 8, 64]), op=OP.is_ge))
                            if dstfm is None:
                                VS.op(lambda e, i=i, n=n: e.tensor_tensor(
                                    vtm[:, i, n * 512:(n + 1) * 512], pabs[:], ptile[:], op=OP.mult))
                            else:
                                vsp = VS.op(lambda e: e.tensor_mul(ptile[:], pabs[:], ptile[:]))
                                for jj in range(4):
                                    tb = ps_produce(2 + jj % 2, lambda e, jj=jj: e.transpose(
                                        psum[2 + jj % 2][:, 0:128],
                                        ptile[:, jj * 128:(jj + 1) * 128], eye_f[:]),
                                        waits=[(VS, vsp)])
                                    ps_consume(2 + jj % 2, VS,
                                               lambda e, i=i, n=n, jj=jj, dstfm=dstfm: e.tensor_copy(
                                                   dstfm[:, n * 4 + jj, i * 128:(i + 1) * 128],
                                                   psum[2 + jj % 2][:, 0:128]),
                                               waits=[(TS, tb)])
            barrier()

            scP = [A(nc.sbuf_tensor(f"scp{p}", [128, T], BF16)) for p in range(2)]
            zmP = [A(nc.sbuf_tensor(f"zmp{p}", [128, T], BF16)) for p in range(2)]
            pmP = [A(nc.sbuf_tensor(f"pmp{p}", [128, T], mybir.dt.uint8)) for p in range(2)]
            czP = [A(nc.sbuf_tensor(f"czp{p}", [128, T], F32)) for p in range(2)]
            wbpP = [A(nc.sbuf_tensor(f"wbpp{p}", [128, T], BF16)) for p in range(2)]
            wtbP = [A(nc.sbuf_tensor(f"wtbp{p}", [128, T], BF16)) for p in range(2)]
            x8P = [A(nc.sbuf_tensor(f"x8p{p}", [128, XCAP], F32)) for p in range(2)]
            colP = [A(nc.sbuf_tensor(f"colp{p}", [128, 8], F32)) for p in range(2)]
            otileP = [A(nc.sbuf_tensor(f"otilep{p}", [128, 64], BF16)) for p in range(2)]
            ofmP = [A(nc.sbuf_tensor(f"ofmp{p}", [64, 128], BF16)) for p in range(2)]
            fmm = fact = None
            n_units = 0
            if ffnA_spec is not None:
                fmm, fact, n_units = make_ffnA(A, *ffnA_spec, banks=(2, 3))
            bands = [(h, qt) for h in range(H) for qt in range(NT)]
            fin = {0: None, 1: None}
            state = {}

            def front(i):
                h, qt = bands[i]
                par = i % 2
                sc = scP[par]
                pt, po = h // 2, (h % 2) * 64
                Wk = (qt + 1) * 128
                nch = (Wk + 511) // 512
                qsl = qfm[po:po + 64, pt, qt * 128:(qt + 1) * 128]
                f = fin[par]
                evtoks = []
                for cch in range(nch):
                    cw_ = min(512, Wk - cch * 512)
                    bk = par
                    sprev, ptok = ps_rd[bk]
                    w = [(sprev, ptok)] if ptok > 0 else []
                    TS.op(lambda e, cch=cch, cw_=cw_, bk=bk, qsl=qsl, pt=pt, po=po: e.matmul(
                        psum[bk][:, 0:cw_], qsl,
                        kfm[po:po + 64, pt, cch * 512:cch * 512 + cw_],
                        start=True, stop=True), waits=w, inc=True)
                    tmm = TS.n
                    ww = [(TS, tmm)]
                    if f is not None and cch == 0:
                        ww.append((VS, f["vs_sc"]))
                    ev = SS.op(lambda e, cch=cch, cw_=cw_, bk=bk, sc=sc: e.activation(
                        sc[:, cch * 512:cch * 512 + cw_], psum[bk][:, 0:cw_],
                        ACT.Identity, scale=INVSQ), waits=ww)
                    ps_rd[bk] = (SS, ev)
                    evtoks.append(ev)
                ctok = VS.op(lambda e, qt=qt, sc=sc: e.scalar_tensor_tensor(
                    sc[:, qt * 128:qt * 128 + 128], causal01[:], NEG,
                    sc[:, qt * 128:qt * 128 + 128], op0=OP.mult, op1=OP.add),
                    waits=[(SS, evtoks[-1])])
                sgn = None
                if qt > 0:
                    scol = colP[par][:, 7:8]
                    jk = wtbP[par]
                    sw = [(VS, ctok)]
                    if f is not None:
                        sw.append((TS, f["tav"]))
                    sgn = SS.op(lambda e, Wk=Wk, sc=sc, scol=scol, jk=jk: e.activation(
                        jk[:, 0:Wk], sc[:, 0:Wk], ACT.Sign, accum_out=scol),
                        waits=sw)
                return {"evlast": evtoks[-1], "ctok": ctok, "sgn": sgn}

            last_spill = {0: None, 1: None}

            def epilogue(f2):
                par, h, qt = f2["par"], f2["h"], f2["qt"]
                pt, po = h // 2, (h % 2) * 64
                Dcol = colP[par][:, 4:5]
                rDcol = colP[par][:, 5:6]
                otile, ofm = otileP[par], ofmP[par]
                avbank = 4 + par
                VS.op(lambda e, Dcol=Dcol, rDcol=rDcol: e.tensor_scalar_max(
                    rDcol, Dcol, 1e-30), waits=[(SS, f2["sexp"])])
                VS.op(lambda e, rDcol=rDcol: e.reciprocal(rDcol, rDcol))
                ov = ps_consume(avbank, VS, lambda e, otile=otile, rDcol=rDcol, avbank=avbank: e.tensor_scalar(
                    otile[:], psum[avbank][:, 0:64], rDcol, None, op0=OP.mult),
                    waits=[(TS, f2["tav"])])
                obb = f2["obb"]
                tb = psb_produce(obb, lambda e, otile=otile, obb=obb: e.transpose(
                    psumb[obb][0:64, 0:128], otile[:, 0:64], eye_bf[:]),
                    waits=[(VS, ov)])
                ow = [(TS, tb)]
                if last_spill[par] is not None:
                    ow.append(last_spill[par])
                oev = psb_consume(obb, SS, lambda e, ofm=ofm, obb=obb: e.activation(
                    ofm[:], psumb[obb][0:64, 0:128], ACT.Identity), waits=ow)
                last_spill[par] = STQ.dma(lambda e, pt=pt, po=po, qt=qt, ofm=ofm: e.dma_start(
                    out=s_attnfm[pt, po:po + 64, qt * 128:(qt + 1) * 128],
                    in_=ofm[:]), waits=[(SS, oev)])

            def back(i, st_):
                h, qt = bands[i]
                par = i % 2
                f = fin[par]
                if f is not None:
                    epilogue(f)
                sc, zm, pm, cz = scP[par], zmP[par], pmP[par], czP[par]
                wbp, wtb, x8 = wbpP[par], wtbP[par], x8P[par]
                pcol = colP[par][:, 0:1]
                mcol = colP[par][:, 1:2]
                dcol = colP[par][:, 2:3]
                taucol = colP[par][:, 3:4]
                Dcol = colP[par][:, 4:5]
                pt, po = h // 2, (h % 2) * 64
                Wk = (qt + 1) * 128
                R = EXTR[qt]
                ctok = st_["ctok"]
                sgn = st_["sgn"]
                wT = [(TS, f["ts_wT"])] if f is not None else []
                if qt == 0:
                    VS.op(lambda e, sc=sc, wbp=wbp: e.tensor_copy(wbp[:, 0:128], sc[:, 0:128]),
                          waits=wT)
                    for r in range(8):
                        VS.op(lambda e, r=r, wbp=wbp, x8=x8: e.max(
                            out=x8[:, r * 8:r * 8 + 8], in_=wbp[:, 0:128]))
                        VS.op(lambda e, r=r, wbp=wbp, x8=x8: e.match_replace(
                            out=wbp[:, 0:128], in_to_replace=x8[:, r * 8:r * 8 + 8],
                            in_values=wbp[:, 0:128], imm_value=NEG))
                    VS.op(lambda e, sc=sc, wbp=wbp, zm=zm: e.tensor_tensor(
                        zm[:, 0:128], sc[:, 0:128], wbp[:, 0:128], op=OP.not_equal))
                else:
                    zcol = colP[par][:, 6:7]
                    scol = colP[par][:, 7:8]
                    VS.op(lambda e, Wk=Wk, sc=sc, zm=zm, zcol=zcol: e.tensor_scalar(
                        zm[:, 0:Wk], sc[:, 0:Wk], 0.0, 0.0, op0=OP.is_equal,
                        op1=OP.add, accum_out=zcol))
                    VS.op(lambda e, Wk=Wk, zm=zm, cz=cz: e.tensor_tensor_scan(
                        cz[:, 0:Wk], zm[:, 0:Wk], zm[:, 0:Wk], 0.0,
                        op0=OP.add, op1=OP.bypass),
                        waits=[(SS, f["sexp"])] if f is not None else [])
                    VS.op(lambda e, pcol=pcol, scol=scol, zcol=zcol: e.tensor_sub(
                        pcol, scol, zcol), waits=[(SS, sgn)])
                    VS.op(lambda e, Wk=Wk, pcol=pcol: e.tensor_scalar(
                        pcol, pcol, 0.5, 0.5 * Wk, op0=OP.mult, op1=OP.add))
                    VS.op(lambda e, pcol=pcol, mcol=mcol: e.tensor_scalar(
                        mcol, pcol, -1.0, 64.0, op0=OP.mult, op1=OP.add))
                    VS.op(lambda e, Wk=Wk, zm=zm, cz=cz, mcol=mcol: e.scalar_tensor_tensor(
                        zm[:, 0:Wk], cz[:, 0:Wk], mcol, zm[:, 0:Wk],
                        op0=OP.is_le, op1=OP.mult))
                    if R > 0:
                        VS.op(lambda e, Wk=Wk, sc=sc, pm=pm: e.tensor_scalar(
                            pm[:, 0:Wk], sc[:, 0:Wk], 0.0, None, op0=OP.is_le))
                        sneg = SS.op(lambda e, Wk=Wk, sc=sc, wbp=wbp: e.activation(
                            wbp[:, 0:Wk], sc[:, 0:Wk], ACT.Identity, scale=-1.0),
                            waits=[(VS, VS.n)] + wT)
                        VS.op(lambda e, Wk=Wk, wbp=wbp, pm=pm: e.copy_predicated(
                            wbp[:, 0:Wk], pm[:, 0:Wk], negbig_b[:].to_broadcast([128, Wk])),
                            waits=[(SS, sneg)])
                        for r in range(R):
                            VS.op(lambda e, r=r, Wk=Wk, wbp=wbp, x8=x8: e.max(
                                out=x8[:, r * 8:r * 8 + 8], in_=wbp[:, 0:Wk]))
                            VS.op(lambda e, r=r, Wk=Wk, wbp=wbp, x8=x8: e.match_replace(
                                out=wbp[:, 0:Wk], in_to_replace=x8[:, r * 8:r * 8 + 8],
                                in_values=wbp[:, 0:Wk], imm_value=NEG))
                        VS.op(lambda e, pcol=pcol, dcol=dcol: e.tensor_scalar(
                            dcol, pcol, -64.0, None, op0=OP.add))
                        VS.op(lambda e, dcol=dcol: e.tensor_scalar_max(dcol, dcol, 0.0))
                        VS.op(lambda e, R=R, dcol=dcol: e.tensor_scalar_min(
                            dcol, dcol, float(8 * R - 1)))
                        VS.op(lambda e, R=R, x8=x8, dcol=dcol, taucol=taucol: e.scalar_tensor_tensor(
                            x8[:, 0:8 * R], iota64[:, 0:8 * R], dcol, x8[:, 0:8 * R],
                            op0=OP.is_equal, op1=OP.mult, accum_out=taucol))
                        VS.op(lambda e, taucol=taucol: e.tensor_scalar(
                            taucol, taucol, -1.0, None, op0=OP.mult))
                        VS.op(lambda e, taucol=taucol: e.tensor_scalar_max(taucol, taucol, TINY))
                    else:
                        VS.op(lambda e, taucol=taucol: e.memset(taucol, TINY))
                    VS.op(lambda e, Wk=Wk, sc=sc, zm=zm, taucol=taucol: e.scalar_tensor_tensor(
                        zm[:, 0:Wk], sc[:, 0:Wk], taucol, zm[:, 0:Wk],
                        op0=OP.is_ge, op1=OP.add))
                vw = []
                if qt == 0 and f is not None:
                    vw.append((SS, f["sexp"]))
                vm = VS.op(lambda e, Wk=Wk, sc=sc, zm=zm, cz=cz: e.scalar_tensor_tensor(
                    cz[:, 0:Wk], zm[:, 0:Wk], 80.0, sc[:, 0:Wk],
                    op0=OP.mult, op1=OP.add), waits=vw)
                sexp = SS.op(lambda e, Wk=Wk, cz=cz, wbp=wbp, Dcol=Dcol: e.activation(
                    wbp[:, 0:Wk], cz[:, 0:Wk], ACT.Exp, bias=neg80c[:],
                    accum_out=Dcol), waits=[(VS, vm)] + wT)
                wtev = []
                nkb = qt + 1
                ngrp = (nkb + 7) // 8
                wT_last = 0
                for g in range(ngrp):
                    bb = g % 2
                    blocks = list(range(g * 8, min((g + 1) * 8, nkb)))
                    tb = 0
                    for ii, kb in enumerate(blocks):
                        if ii == 0:
                            tb = psb_produce(bb, lambda e, kb=kb, bb=bb, wbp=wbp: e.transpose(
                                psumb[bb][:, (kb % 8) * 128:(kb % 8 + 1) * 128],
                                wbp[:, kb * 128:(kb + 1) * 128], eye_bf[:]),
                                waits=[(SS, sexp)])
                        else:
                            tb = TS.op(lambda e, kb=kb, bb=bb, wbp=wbp: e.transpose(
                                psumb[bb][:, (kb % 8) * 128:(kb % 8 + 1) * 128],
                                wbp[:, kb * 128:(kb + 1) * 128], eye_bf[:]))
                    wT_last = tb
                    wtev.append(psb_consume(bb, SS, lambda e, g=g, bb=bb, nb=len(blocks), wtb=wtb: e.activation(
                        wtb[:, g * 1024:g * 1024 + nb * 128], psumb[bb][:, 0:nb * 128],
                        ACT.Identity), waits=[(TS, tb)]))
                avbank = 4 + par
                sprev, ptok = ps_rd[avbank]
                for kb in range(nkb):
                    w = []
                    if kb == 0 and ptok > 0:
                        w.append((sprev, ptok))
                    if kb % 8 == 0:
                        w.append((SS, wtev[kb // 8]))
                    TS.op(lambda e, kb=kb, h=h, qt=qt, wtb=wtb, avbank=avbank: e.matmul(
                        psum[avbank][:, 0:64], wtb[:, kb * 128:(kb + 1) * 128],
                        vtm[:, kb, h * 64:(h + 1) * 64],
                        start=(kb == 0), stop=(kb == qt)),
                        waits=w, inc=(kb == qt))
                tav = TS.n
                if "band" in debug and i == 1:
                    dwb = dbg_out("band_w", [128, 256], BF16)
                    dzm = dbg_out("band_g", [128, 256], BF16)
                    dcl = dbg_out("band_col", [128, 8])
                    t1 = STQ.dma(lambda e, wbp=wbp: e.dma_start(out=dwb, in_=wbp[:, 0:256]),
                                 waits=[(SS, sexp)])
                    t2 = STQ.dma(lambda e, zm=zm: e.dma_start(out=dzm, in_=zm[:, 0:256]),
                                 waits=[(SS, sexp)])
                    t3 = STQ.dma(lambda e, par=par: e.dma_start(
                        out=dcl[:, 0:2], in_=colP[par][:, 0:2]), waits=[(SS, sexp)])
                    t4 = STQ.dma(lambda e, par=par: e.dma_start(
                        out=dcl[:, 3:5], in_=colP[par][:, 3:5]), waits=[(SS, sexp)])
                    t5 = STQ.dma(lambda e, par=par: e.dma_start(
                        out=dcl[:, 6:8], in_=colP[par][:, 6:8]), waits=[(SS, sexp)])
                    VS.wait([t1, t2, t3, t4, t5])
                    SS.wait([t1, t2, t3, t4, t5])
                return {"vs_sc": vm, "ts_wT": wT_last, "sexp": sexp, "tav": tav,
                        "obb": ngrp % 2, "par": par, "h": h, "qt": qt}

            for i in range(len(bands) + 1):
                if i < len(bands):
                    state[i] = front(i)
                if i >= 1:
                    fin[(i - 1) % 2] = back(i - 1, state.pop(i - 1))
                    if fmm is not None:
                        if i - 1 < n_units:
                            fmm(i - 1)
                        if i - 2 >= 0:
                            fact(i - 2)
            epilogue(fin[len(bands) % 2])
            epilogue(fin[(len(bands) + 1) % 2])
            if fmm is not None:
                for u in range(max(len(bands) - 1, 0), n_units):
                    fact(u)
        actx.close()
        barrier()


    def emit_wconv(w_hbm, wb_hbm, n, tag):
        width = n // 128
        nchunks = max(1, width // 4096)
        cw = width // nchunks
        wv = w_hbm.rearrange("(p a) c -> p (a c)", p=128)
        wbv = wb_hbm.rearrange("(p a) c -> p (a c)", p=128)
        with nc.sbuf_tensor("wcv" + tag, [128, cw], F32) as wchunk, \
             nc.sbuf_tensor("bcv" + tag, [128, cw], BF16) as bchunk:
            prev_st = None
            for c in range(nchunks):
                lt = LD.dma(lambda e, c=c: e.dma_start(out=wchunk[:], in_=wv[:, c * cw:(c + 1) * cw]),
                            waits=[(VS, VS.n)])
                vb = VS.op(lambda e: e.tensor_copy(bchunk[:], wchunk[:]),
                           waits=[lt] + ([prev_st] if prev_st is not None else []))
                prev_st = STQ.dma(lambda e, c=c: e.dma_start(out=wbv[:, c * cw:(c + 1) * cw], in_=bchunk[:]),
                                  waits=[(VS, vb)])
        barrier()

    def emit_wo_x1():
        with nc.sbuf_tensor("wob", [128, NP, D], BF16) as wob, \
             nc.sbuf_tensor("afm0", [128, NP, 128], BF16) as afm0, \
             nc.sbuf_tensor("afm1", [128, NP, 128], BF16) as afm1, \
             nc.sbuf_tensor("xt1a", [128, D], F32) as xt1a, \
             nc.sbuf_tensor("xt1b", [128, D], F32) as xt1b, \
             nc.sbuf_tensor("x1ta", [128, D], F32) as x1ta, \
             nc.sbuf_tensor("x1tb", [128, D], F32) as x1tb:
            afmP_, xtP_, x1tP_ = [afm0, afm1], [xt1a, xt1b], [x1ta, x1tb]
            lwo = LDG.dma(lambda e: e.dma_start(out=wob[:], in_=woT.rearrange("(j p) d -> p j d", p=128)))
            stP = [None, None]
            mmP = [0, 0]
            vvP = [0, 0]
            for i in range(NT):
                par = i % 2
                afm, xt1, x1t = afmP_[par], xtP_[par], x1tP_[par]
                la = LD.dma(lambda e, i=i, afm=afm: e.dma_start(
                    out=afm[:], in_=s_attnfm[:, :, i * 128:(i + 1) * 128].rearrange("j p t -> p j t")),
                    waits=[(TS, mmP[par])])
                lx = LD.dma(lambda e, i=i, xt1=xt1: e.dma_start(out=xt1[:], in_=x0[i * 128:(i + 1) * 128, :]),
                            waits=[(VS, vvP[par])] + ([stP[par]] if stP[par] is not None else []))
                for n in range(2):
                    bk = 2 * par + n % 2
                    sprev, ptok = ps_rd[bk]
                    for kk in range(NP):
                        w = []
                        if kk == 0:
                            w = [la, lwo]
                            if ptok > 0:
                                w.append((sprev, ptok))
                        TS.op(lambda e, n=n, kk=kk, bk=bk, afm=afm: e.matmul(
                            psum[bk][:], afm[:, kk, :], wob[:, kk, n * 512:(n + 1) * 512],
                            start=(kk == 0), stop=(kk == NP - 1)),
                            waits=w, inc=(kk == NP - 1))
                    tmm = TS.n
                    ps_consume(bk, VS, lambda e, n=n, bk=bk, x1t=x1t, xt1=xt1: e.tensor_tensor(
                        x1t[:, n * 512:(n + 1) * 512], psum[bk][:],
                        xt1[:, n * 512:(n + 1) * 512], op=OP.add),
                        waits=[(TS, tmm), lx] + ([stP[par]] if stP[par] is not None else []))
                mmP[par] = TS.n
                vvP[par] = VS.n
                stP[par] = STQ.dma(lambda e, i=i, x1t=x1t: e.dma_start(
                    out=s_x1[i * 128:(i + 1) * 128, :], in_=x1t[:]), waits=[(VS, vvP[par])])
        barrier()

    def make_ffnA(A, wb_list, bfm_list, act_hbm, gelu_fn, xsrc, tag, banks=(2, 3)):
        nexp = len(wb_list)
        w1c2 = A(nc.sbuf_tensor("w1c" + tag, [128, 2, NP, 128], BF16))
        pw1c2 = A(nc.sbuf_tensor("pw1c" + tag, [128, 2, NP, 128], BF16))
        actt2 = A(nc.sbuf_tensor("actt" + tag, [128, 2, 512], BF16))
        prev_stores = {0: [], 1: []}
        lastmm = {0: 0, 1: 0}
        ltoks_cur = {}
        upf = 4 * nexp

        mmtoks = {}

        def mm_part(u):
            ft, r = divmod(u, upf)
            nch, ei = divmod(r, nexp)
            fp = ft % 2
            wcs = [w1c2[:, fp], pw1c2[:, fp]][:nexp]
            if r == 0:
                ltoks_cur.clear()
                for e2 in range(nexp):
                    ltoks_cur[e2] = LDG.dma(lambda e, ft=ft, e2=e2, wcs=wcs: e.dma_start(
                        out=wcs[e2],
                        in_=wb_list[e2][:, ft * 128:(ft + 1) * 128].rearrange(
                            "(j p) f -> p j f", p=128)),
                        waits=[(TS, lastmm[fp])])
            bk = banks[u % len(banks)]
            sprev, ptok = ps_rd[bk]
            for kk in range(NP):
                w = []
                if kk == 0:
                    w = [ltoks_cur[ei]]
                    if ptok > 0:
                        w.append((sprev, ptok))
                TS.op(lambda e, nch=nch, ei=ei, kk=kk, bk=bk, wcs=wcs: e.matmul(
                    psum[bk][:], wcs[ei][:, kk, :],
                    xsrc[:, kk, nch * 512:(nch + 1) * 512],
                    start=(kk == 0), stop=(kk == NP - 1)),
                    waits=w, inc=(kk == NP - 1))
            mmtoks[u] = TS.n
            if r == upf - 1:
                lastmm[fp] = TS.n

        def act_part(u):
            ft, r = divmod(u, upf)
            nch, ei = divmod(r, nexp)
            bk = banks[u % len(banks)]
            ap = u % 2
            sg = SS.op(lambda e, ft=ft, ei=ei, bk=bk, ap=ap: e.activation(
                actt2[:, ap], psum[bk][:], gelu_fn, bias=bfm_list[ei][:, ft:ft + 1]),
                waits=[(TS, mmtoks.pop(u))] + prev_stores[ap])
            ps_rd[bk] = (SS, sg)
            prev_stores[ap] = []
            for tt4 in range(4):
                prev_stores[ap].append(STQ.dma(
                    lambda e, ft=ft, ei=ei, nch=nch, tt4=tt4, ap=ap: e.dma_start(
                        out=act_hbm[ei, ft, nch * 4 + tt4, :, :],
                        in_=actt2[:, ap, tt4 * 128:(tt4 + 1) * 128]),
                    waits=[(SS, sg)]))
        return mm_part, act_part, NF * upf

    def emit_ffnA(wb_list, bfm_list, act_hbm, gelu_fn, xsrc, tag):
        from contextlib import ExitStack
        fctx = ExitStack()
        mm_part, act_part, n_units = make_ffnA(
            fctx.enter_context, wb_list, bfm_list, act_hbm,
            gelu_fn, xsrc, tag, banks=(0, 1, 2, 3))
        for u in range(n_units):
            mm_part(u)
            act_part(u)
        fctx.close()
        barrier()

    def emit_ffnB0():
        with nc.sbuf_tensor("w2r", [128, NF, D], BF16) as w2r, \
             nc.sbuf_tensor("pw2r", [128, NF, D], BF16) as pw2r, \
             nc.sbuf_tensor("acts", [128, 2, NF, 128], BF16) as acts, \
             nc.sbuf_tensor("y0t", [128, 512], F32) as y0t, \
             nc.sbuf_tensor("x1t2", [128, 2, D], F32) as x1t2, \
             nc.sbuf_tensor("p0c", [128, 1], F32) as p0c, \
             nc.sbuf_tensor("q0c", [128, 1], F32) as q0c, \
             nc.sbuf_tensor("cvt", [128, D], F32) as cvt:
            lw2 = LDG.dma(lambda e: e.dma_start(
                out=w2r[:], in_=s_w2b.rearrange("(j p) d -> p j d", p=128)))
            lpw2 = LDG.dma(lambda e: e.dma_start(
                out=pw2r[:], in_=s_pw2b.rearrange("(j p) d -> p j d", p=128)))
            stP = [None, None]
            mmP = [0, 0]
            vvP = [0, 0]
            for i in range(NT):
                par = i % 2
                acts_, x1t_ = acts[:], x1t2[:, par]
                la = LD.dma(lambda e, i=i, acts_=acts_: e.dma_start(
                    out=acts_, in_=s_act[:, :, i, :, :].rearrange("e f p t -> p e f t")),
                    waits=[(TS, max(mmP))])
                lx = LD.dma(lambda e, i=i, x1t_=x1t_: e.dma_start(
                    out=x1t_, in_=s_x1[i * 128:(i + 1) * 128, :]),
                    waits=[(VS, vvP[par])] + ([stP[par]] if stP[par] is not None else []))
                VS.op(lambda e, i=i: e.tensor_copy(p0c[:], p0b[:, i:i + 1]))
                VS.op(lambda e: e.tensor_scalar(q0c[:], p0c[:], -1.0, 1.0, op0=OP.mult, op1=OP.add))
                VS.op(lambda e: e.scalar_tensor_tensor(cvt[:], dvec_bc[:], q0c[:], cvec_bc[:],
                                                       op0=OP.mult, op1=OP.add))
                for n in range(2):
                    for (ei, wsb) in ((0, w2r), (1, pw2r)):
                        bk = 2 * par + ei
                        sprev, ptok = ps_rd[bk]
                        for kk in range(NF):
                            w = []
                            if kk == 0:
                                w = [la, lw2 if ei == 0 else lpw2]
                                if ptok > 0:
                                    w.append((sprev, ptok))
                            TS.op(lambda e, n=n, ei=ei, kk=kk, bk=bk, wsb=wsb, acts_=acts_: e.matmul(
                                psum[bk][:], acts_[:, ei, kk, :], wsb[:, kk, n * 512:(n + 1) * 512],
                                start=(kk == 0), stop=(kk == NF - 1)),
                                waits=w, inc=(kk == NF - 1))
                        if ei == 0:
                            t0 = TS.n
                        else:
                            t1 = TS.n
                    b0, b1 = 2 * par, 2 * par + 1
                    ev0 = ps_consume(b0, VS, lambda e, b0=b0: e.tensor_copy(y0t[:], psum[b0][:]),
                                     waits=[(TS, t0)])
                    vv = ps_consume(b1, VS, lambda e, n=n, b1=b1: e.tensor_sub(y0t[:], y0t[:], psum[b1][:]),
                                    waits=[(TS, t1)])
                    VS.op(lambda e, n=n, b1=b1: e.scalar_tensor_tensor(
                        y0t[:], y0t[:], p0c[:], psum[b1][:], op0=OP.mult, op1=OP.add))
                    ps_rd[b1] = (VS, VS.n)
                    VS.op(lambda e, n=n: e.tensor_tensor(
                        y0t[:], y0t[:], cvt[:, n * 512:(n + 1) * 512], op=OP.add))
                    VS.op(lambda e, n=n, x1t_=x1t_: e.tensor_tensor(
                        x1t_[:, n * 512:(n + 1) * 512], y0t[:],
                        x1t_[:, n * 512:(n + 1) * 512], op=OP.add), waits=[lx])
                mmP[par] = TS.n
                vvP[par] = VS.n
                stP[par] = STQ.dma(lambda e, i=i, x1t_=x1t_: e.dma_start(
                    out=s_x1[i * 128:(i + 1) * 128, :], in_=x1t_), waits=[(VS, vvP[par])])
        barrier()

    def emit_l1_ln(xnfm):
        with nc.sbuf_tensor("xtl", [128, D], F32) as xt, \
             nc.sbuf_tensor("xcl", [128, D], F32) as xc, \
             nc.sbuf_tensor("sql", [128, D], F32) as sqs, \
             nc.sbuf_tensor("mul_", [128, 1], F32) as mu, \
             nc.sbuf_tensor("musql", [128, 1], F32) as musq, \
             nc.sbuf_tensor("ssql", [128, 1], F32) as ssq, \
             nc.sbuf_tensor("vepsl", [128, 1], F32) as veps, \
             nc.sbuf_tensor("rstdl", [128, 1], F32) as rstd:
            sq_prev = 0
            tposx_prev = 0
            for i in range(NT):
                lx = LD.dma(lambda e, i=i: e.dma_start(out=xt[:], in_=s_x1[i * 128:(i + 1) * 128, :]),
                            waits=[(VS, VS.n)] + ([(SS, sq_prev)] if sq_prev else []))
                VS.op(lambda e: e.reduce_sum(mu[:], xt[:], axis=AX.X), waits=[lx])
                VS.op(lambda e: e.tensor_scalar(mu[:], mu[:], 1.0 / D, None, op0=OP.mult))
                sq = SS.op(lambda e: e.activation(sqs[:], xt[:], ACT.Square, accum_out=ssq[:]),
                           waits=[lx, (VS, VS.n)])
                sq_prev = sq
                VS.op(lambda e: e.tensor_mul(musq[:], mu[:], mu[:]), waits=[(SS, sq)])
                VS.op(lambda e: e.scalar_tensor_tensor(veps[:], ssq[:], 1.0 / D, musq[:],
                                                       op0=OP.mult, op1=OP.subtract))
                VS.op(lambda e: e.tensor_scalar(veps[:], veps[:], EPS, None, op0=OP.add))
                s2 = SS.op(lambda e: e.activation(rstd[:], veps[:], ACT.Sqrt),
                           waits=[(VS, VS.n)])
                VS.op(lambda e: e.reciprocal(rstd[:], rstd[:]), waits=[(SS, s2)])
                vb = VS.op(lambda e: e.tensor_scalar(xc[:], xt[:], mu[:], rstd[:],
                                                     op0=OP.subtract, op1=OP.mult),
                           waits=([(TS, tposx_prev)] if tposx_prev else []))
                for j in range(NP):
                    bk = j % 2
                    t = ps_produce(bk, lambda e, j=j, bk=bk: e.transpose(
                        psum[bk][:, 0:128], xc[:, j * 128:(j + 1) * 128], eye_f[:]),
                        waits=[(VS, vb)])
                    ps_consume(bk, VS, lambda e, j=j, i=i, bk=bk: e.tensor_copy(
                        xnfm[:, j, i * 128:(i + 1) * 128], psum[bk][:, 0:128]),
                        waits=[(TS, t)])
                    tposx_prev = t
        barrier()

    BIG1 = 400.0

    def emit_l1_attn(xnfm):
        from contextlib import ExitStack
        actx = ExitStack()
        A = actx.enter_context
        qfm = A(nc.sbuf_tensor("qfm1", [128, NP, T], BF16))
        kfm = A(nc.sbuf_tensor("kfm1", [128, NP, T], BF16))
        vtm = A(nc.sbuf_tensor("vtm1", [128, NT, D], BF16))
        qb = A(nc.sbuf_tensor("qb1", [64, H, 32], F32))
        kb = A(nc.sbuf_tensor("kb1", [64, H, 32], F32))
        bmask = A(nc.sbuf_tensor("bmask1", [32, H, 32], BF16))
        cmat = A(nc.sbuf_tensor("cmat1", [32, 32], F32))
        m8c = A(nc.sbuf_tensor("m8c1", [32, 8], F32))
        em = A(nc.sbuf_tensor("em1", [32, T], BF16))
        emv = A(nc.sbuf_tensor("emv1", [32, T], F32))
        em2 = A(nc.sbuf_tensor("em21", [32, T], BF16))
        eyem = A(nc.sbuf_tensor("eyem1", [32, 32], F32))

        g1 = GS.op(lambda e: e.iota(emv[:], [[1, T]], base=0, channel_multiplier=-64,
                                    allow_small_or_imprecise_dtypes=True))
        VS.op(lambda e: e.tensor_scalar(em2[:], emv[:], 0.0, None, op0=OP.is_ge),
              waits=[(GS, g1)])
        VS.op(lambda e: e.tensor_scalar(em[:], emv[:], 63.0, None, op0=OP.is_le))
        VS.op(lambda e: e.tensor_mul(em[:], em[:], em2[:]))
        g2 = GS.op(lambda e: e.iota(eyem[:], [[1, 32]], base=0, channel_multiplier=-1,
                                    allow_small_or_imprecise_dtypes=True))
        VS.op(lambda e: e.tensor_scalar(eyem[:], eyem[:], 0.0, None, op0=OP.is_equal),
              waits=[(GS, g2)])

        with nc.sbuf_tensor("wprj1", [128, NP, D], BF16) as wprj:
            for (off, dstfm) in ((0, qfm), (D, kfm), (2 * D, None)):
                lw = LDG.dma(lambda e, off=off: e.dma_start(
                    out=wprj[:], in_=qkvT[:, off:off + D].rearrange("(j p) d -> p j d", p=128)),
                    waits=[(TS, TS.n)])
                if dstfm is None:
                    for i in range(NT):
                        for n in range(2):
                            bk = n % 2
                            sprev, ptok = ps_rd[bk]
                            for kk in range(NP):
                                w = []
                                if kk == 0:
                                    w = [lw]
                                    if ptok > 0:
                                        w.append((sprev, ptok))
                                TS.op(lambda e, i=i, n=n, kk=kk, bk=bk: e.matmul(
                                    psum[bk][:], xnfm[:, kk, i * 128:(i + 1) * 128],
                                    wprj[:, kk, n * 512:(n + 1) * 512],
                                    start=(kk == 0), stop=(kk == NP - 1)),
                                    waits=w, inc=(kk == NP - 1))
                            tmm = TS.n
                            ps_consume(bk, VS, lambda e, i=i, n=n, bk=bk: e.tensor_copy(
                                vtm[:, i, n * 512:(n + 1) * 512], psum[bk][:]),
                                waits=[(TS, tmm)])
                else:
                    for j in range(NP):
                        for ch in range(4):
                            bk = ch % 2
                            sprev, ptok = ps_rd[bk]
                            for kk in range(NP):
                                w = []
                                if kk == 0:
                                    w = [lw]
                                    if ptok > 0:
                                        w.append((sprev, ptok))
                                TS.op(lambda e, j=j, ch=ch, kk=kk, bk=bk: e.matmul(
                                    psum[bk][:], wprj[:, kk, j * 128:(j + 1) * 128],
                                    xnfm[:, kk, ch * 512:(ch + 1) * 512],
                                    start=(kk == 0), stop=(kk == NP - 1)),
                                    waits=w, inc=(kk == NP - 1))
                            tmm = TS.n
                            ps_consume(bk, VS, lambda e, j=j, ch=ch, bk=bk, dstfm=dstfm: e.tensor_copy(
                                dstfm[:, j, ch * 512:(ch + 1) * 512], psum[bk][:]),
                                waits=[(TS, tmm)])
        barrier()

        for h in range(H):
            pt, po = h // 2, (h % 2) * 64
            v1 = VS.op(lambda e, h=h, pt=pt, po=po: e.reduce_sum(
                qb[:, h, :], qfm[po:po + 64, pt, :].rearrange("p (m w) -> p m w", w=64),
                axis=AX.X))
            v2 = VS.op(lambda e, h=h, pt=pt, po=po: e.reduce_sum(
                kb[:, h, :], kfm[po:po + 64, pt, :].rearrange("p (m w) -> p m w", w=64),
                axis=AX.X))
            sprev, ptok = ps_rd[5]
            w = [(VS, v2)]
            if ptok > 0:
                w.append((sprev, ptok))
            TS.op(lambda e, h=h: e.matmul(psum[5][0:32, 0:32], qb[:, h, :], kb[:, h, :],
                                          start=True, stop=True), waits=w, inc=True)
            tc = TS.n
            ev = ps_consume(5, SS, lambda e: e.activation(cmat[:], psum[5][0:32, 0:32],
                                                          ACT.Identity, scale=0.125 / 4096.0),
                            waits=[(TS, tc), (VS, VS.n)])
            VS.op(lambda e: e.max(out=m8c[:], in_=cmat[:]), waits=[(SS, ev)])
            VS.op(lambda e, h=h: e.scalar_tensor_tensor(
                bmask[:, h, :], cmat[:], m8c[:, 1:2], eyem[:],
                op0=OP.is_ge, op1=OP.max))
        barrier()

        sc1P = [A(nc.sbuf_tensor(f"sc1P{p}", [128, T], BF16)) for p in range(2)]
        vmP = [A(nc.sbuf_tensor(f"vmP{p}", [128, T], F32)) for p in range(2)]
        wbp1P = [A(nc.sbuf_tensor(f"wbp1P{p}", [128, T], BF16)) for p in range(2)]
        wtb1P = [A(nc.sbuf_tensor(f"wtb1P{p}", [128, T], BF16)) for p in range(2)]
        M2P = [A(nc.sbuf_tensor(f"M2P{p}", [128, 32], F32)) for p in range(2)]
        col1P = [A(nc.sbuf_tensor(f"col1P{p}", [128, 4], F32)) for p in range(2)]
        otile1P = [A(nc.sbuf_tensor(f"otile1P{p}", [128, 64], BF16)) for p in range(2)]
        ofm1P = [A(nc.sbuf_tensor(f"ofm1P{p}", [64, 128], BF16)) for p in range(2)]
        bands1 = [(h, qt) for h in range(H) for qt in range(NT)]
        fin1 = {0: None, 1: None}
        st1 = {}

        def front1(i):
            h, qt = bands1[i]
            par = i % 2
            sc, M2 = sc1P[par], M2P[par]
            pt, po = h // 2, (h % 2) * 64
            qsl = qfm[po:po + 64, pt, qt * 128:(qt + 1) * 128]
            f = fin1[par]
            mbank = 4 + par
            sprev, ptok = ps_rd[mbank]
            w = [(VS, VS.n)]
            if ptok > 0:
                w.append((sprev, ptok))
            TS.op(lambda e, qt=qt, h=h, mbank=mbank: e.matmul(
                psum[mbank][:, 448:480], em[:, qt * 128:(qt + 1) * 128], bmask[:, h, :],
                start=True, stop=True), waits=w, inc=True)
            tm2 = TS.n
            evlast = 0
            vtoks = {}
            for cch in range(4):
                bk = 2 * par + cch % 2
                sprev, ptok = ps_rd[bk]
                w = [(sprev, ptok)] if ptok > 0 else []
                TS.op(lambda e, cch=cch, bk=bk, qsl=qsl, pt=pt, po=po: e.matmul(
                    psum[bk][:], qsl, kfm[po:po + 64, pt, cch * 512:(cch + 1) * 512],
                    start=True, stop=True), waits=w, inc=True)
                tmm = TS.n
                if cch < 2:
                    ww = [(TS, tmm)]
                    if f is not None and cch == 0:
                        ww.append((VS, f["vs_sc"]))
                    evlast = SS.op(lambda e, cch=cch, bk=bk, sc=sc: e.activation(
                        sc[:, cch * 512:(cch + 1) * 512], psum[bk][:],
                        ACT.Identity, scale=0.125), waits=ww)
                    ps_rd[bk] = (SS, evlast)
                else:
                    vtoks[cch] = tmm
            return {"evlast": evlast, "tm2": tm2, "vtoks": vtoks}

        last_spill1 = {0: None, 1: None}

        def epilogue1(f2):
            par, h, qt = f2["par"], f2["h"], f2["qt"]
            pt, po = h // 2, (h % 2) * 64
            Dcol = col1P[par][:, 1:2]
            rDcol = col1P[par][:, 2:3]
            otile, ofm = otile1P[par], ofm1P[par]
            avbank = 4 + par
            VS.op(lambda e, Dcol=Dcol, rDcol=rDcol: e.tensor_scalar_max(
                rDcol, Dcol, 1e-30), waits=[(SS, f2["ss_exp"])])
            VS.op(lambda e, rDcol=rDcol: e.reciprocal(rDcol, rDcol))
            ov = ps_consume(avbank, VS, lambda e, otile=otile, rDcol=rDcol, avbank=avbank: e.tensor_scalar(
                otile[:], psum[avbank][:, 0:64], rDcol, None, op0=OP.mult),
                waits=[(TS, f2["tav"])])
            tb = psb_produce(0, lambda e, otile=otile: e.transpose(
                psumb[0][0:64, 0:128], otile[:, 0:64], eye_bf[:]),
                waits=[(VS, ov)])
            ow = [(TS, tb)]
            if last_spill1[par] is not None:
                ow.append(last_spill1[par])
            oev = psb_consume(0, SS, lambda e, ofm=ofm: e.activation(
                ofm[:], psumb[0][0:64, 0:128], ACT.Identity), waits=ow)
            last_spill1[par] = STQ.dma(lambda e, pt=pt, po=po, qt=qt, ofm=ofm: e.dma_start(
                out=s_attnfm1.rearrange("(j p) t -> j p t", p=128)[
                    pt, po:po + 64, qt * 128:(qt + 1) * 128],
                in_=ofm[:]), waits=[(SS, oev)])

        def back1(i, st_):
            h, qt = bands1[i]
            par = i % 2
            f = fin1[par]
            sc, M2, vm = sc1P[par], M2P[par], vmP[par]
            wbp, wtb = wbp1P[par], wtb1P[par]
            mxc = col1P[par][:, 0:1]
            Dcol = col1P[par][:, 1:2]
            pt, po = h // 2, (h % 2) * 64
            for cch, tmm in st_["vtoks"].items():
                bk = 2 * par + cch % 2
                vev = VS.op(lambda e, cch=cch, bk=bk, sc=sc: e.tensor_scalar(
                    sc[:, cch * 512:(cch + 1) * 512], psum[bk][:], 0.125,
                    None, op0=OP.mult), waits=[(TS, tmm)])
                ps_rd[bk] = (VS, vev)
            mbank = 4 + par
            VS.op(lambda e, M2=M2, mbank=mbank: e.tensor_copy(
                M2[:], psum[mbank][:, 448:480]), waits=[(TS, st_["tm2"])])
            ps_rd[mbank] = (VS, VS.op(lambda e, M2=M2: e.tensor_scalar(
                M2[:], M2[:], 1e30, -1e30, op0=OP.mult, op1=OP.add)))
            if f is not None:
                epilogue1(f)
            sc3 = sc.rearrange("p (m b) -> p m b", b=64)
            vm3 = vm.rearrange("p (m b) -> p m b", b=64)
            vw = [(SS, st_["evlast"])]
            if f is not None:
                vw.append((SS, f["ss_exp"]))
            vmt = VS.op(lambda e, sc3=sc3, vm3=vm3, M2=M2: e.scalar_tensor_tensor(
                vm3, M2[:].unsqueeze(2).to_broadcast([128, 32, 64]), 1.0, sc3,
                op0=OP.mult, op1=OP.add), waits=vw)
            VS.op(lambda e, vm=vm, mxc=mxc: e.reduce_max(mxc, vm[:], axis=AX.X))
            vneg = VS.op(lambda e, mxc=mxc: e.tensor_scalar(mxc, mxc, -1.0, None, op0=OP.mult))
            ew = [(VS, vneg)]
            if f is not None:
                ew.append(f["vs_wbp"])
            sexp = SS.op(lambda e, vm=vm, wbp=wbp, Dcol=Dcol, mxc=mxc: e.activation(
                wbp[:], vm[:], ACT.Exp, bias=mxc, accum_out=Dcol), waits=ew)
            wtev = []
            for g in range(2):
                bb = g % 2
                tb = 0
                for ii in range(8):
                    kbi = g * 8 + ii
                    if ii == 0:
                        tb = psb_produce(bb, lambda e, kbi=kbi, bb=bb, wbp=wbp: e.transpose(
                            psumb[bb][:, (kbi % 8) * 128:(kbi % 8 + 1) * 128],
                            wbp[:, kbi * 128:(kbi + 1) * 128], eye_bf[:]),
                            waits=[(SS, sexp)])
                    else:
                        tb = TS.op(lambda e, kbi=kbi, bb=bb, wbp=wbp: e.transpose(
                            psumb[bb][:, (kbi % 8) * 128:(kbi % 8 + 1) * 128],
                            wbp[:, kbi * 128:(kbi + 1) * 128], eye_bf[:]))
                wtev.append((SS, psb_consume(bb, SS, lambda e, g=g, bb=bb, wtb=wtb: e.activation(
                    wtb[:, g * 1024:(g + 1) * 1024], psumb[bb][:, 0:1024],
                    ACT.Identity), waits=[(TS, tb)])))
            avbank = 4 + par
            sprev, ptok = ps_rd[avbank]
            for kbi in range(NT):
                w = []
                if kbi == 0 and ptok > 0:
                    w.append((sprev, ptok))
                if kbi % 8 == 0:
                    w.append(wtev[kbi // 8])
                TS.op(lambda e, kbi=kbi, h=h, wtb=wtb, avbank=avbank: e.matmul(
                    psum[avbank][:, 0:64], wtb[:, kbi * 128:(kbi + 1) * 128],
                    vtm[:, kbi, h * 64:(h + 1) * 64],
                    start=(kbi == 0), stop=(kbi == NT - 1)),
                    waits=w, inc=(kbi == NT - 1))
            tav = TS.n
            return {"vs_sc": vmt, "vs_wbp": wtev[-1],
                    "ss_exp": sexp, "tav": tav, "par": par, "h": h, "qt": qt}

        for i in range(len(bands1) + 1):
            if i < len(bands1):
                st1[i] = front1(i)
            if i >= 1:
                fin1[(i - 1) % 2] = back1(i - 1, st1.pop(i - 1))
        epilogue1(fin1[len(bands1) % 2])
        epilogue1(fin1[(len(bands1) + 1) % 2])
        actx.close()
        barrier()

    def emit_l1_proj():
        with nc.sbuf_tensor("prjb", [128, NP, D], BF16) as prjb, \
             nc.sbuf_tensor("afml1", [128, 2, NP, 128], BF16) as afm, \
             nc.sbuf_tensor("xt2", [128, 2, D], F32) as xt2, \
             nc.sbuf_tensor("x2t", [128, 2, D], F32) as x2t:
            lwo = LDG.dma(lambda e: e.dma_start(out=prjb[:], in_=projT.rearrange("(j p) d -> p j d", p=128)))
            stP = [None, None]
            mmP = [0, 0]
            vvP = [0, 0]
            for i in range(NT):
                par = i % 2
                afm_, xt2_, x2t_ = afm[:, par], xt2[:, par], x2t[:, par]
                la = LD.dma(lambda e, i=i, afm_=afm_: e.dma_start(
                    out=afm_, in_=s_attnfm1.rearrange("(j p) t -> p j t", p=128)[:, :, i * 128:(i + 1) * 128]),
                    waits=[(TS, mmP[par])])
                lx = LD.dma(lambda e, i=i, xt2_=xt2_: e.dma_start(out=xt2_, in_=s_x1[i * 128:(i + 1) * 128, :]),
                            waits=[(VS, vvP[par])] + ([stP[par]] if stP[par] is not None else []))
                for n in range(2):
                    bk = 2 * par + n % 2
                    sprev, ptok = ps_rd[bk]
                    for kk in range(NP):
                        w = []
                        if kk == 0:
                            w = [la, lwo]
                            if ptok > 0:
                                w.append((sprev, ptok))
                        TS.op(lambda e, n=n, kk=kk, bk=bk, afm_=afm_: e.matmul(
                            psum[bk][:], afm_[:, kk, :], prjb[:, kk, n * 512:(n + 1) * 512],
                            start=(kk == 0), stop=(kk == NP - 1)),
                            waits=w, inc=(kk == NP - 1))
                    tmm = TS.n
                    ps_consume(bk, VS, lambda e, n=n, bk=bk, x2t_=x2t_, xt2_=xt2_: e.tensor_tensor(
                        x2t_[:, n * 512:(n + 1) * 512], psum[bk][:],
                        xt2_[:, n * 512:(n + 1) * 512], op=OP.add),
                        waits=[(TS, tmm), lx] + ([stP[par]] if stP[par] is not None else []))
                mmP[par] = TS.n
                vvP[par] = VS.n
                stP[par] = STQ.dma(lambda e, i=i, x2t_=x2t_: e.dma_start(
                    out=s_x2[i * 128:(i + 1) * 128, :], in_=x2t_), waits=[(VS, vvP[par])])
        barrier()

    def emit_ffnB1():
        with nc.sbuf_tensor("w2r1", [128, NF, D], BF16) as w2r, \
             nc.sbuf_tensor("acts1", [128, 2, NF, 128], BF16) as acts, \
             nc.sbuf_tensor("yt1", [128, 512], F32) as yt, \
             nc.sbuf_tensor("x2t2", [128, 2, D], F32) as x2t2:
            lw2 = LDG.dma(lambda e: e.dma_start(
                out=w2r[:], in_=s_lw2b.rearrange("(j p) d -> p j d", p=128)))
            stP = [None, None]
            mmP = [0, 0]
            vvP = [0, 0]
            for i in range(NT):
                par = i % 2
                acts_, x2t_ = acts[:, par], x2t2[:, par]
                la = LD.dma(lambda e, i=i, acts_=acts_: e.dma_start(
                    out=acts_, in_=s_act1[0, :, i, :, :].rearrange("f p t -> p f t")),
                    waits=[(TS, mmP[par])])
                lx = LD.dma(lambda e, i=i, x2t_=x2t_: e.dma_start(out=x2t_, in_=s_x2[i * 128:(i + 1) * 128, :]),
                            waits=[(VS, vvP[par])] + ([stP[par]] if stP[par] is not None else []))
                for n in range(2):
                    bk = 2 * par + n % 2
                    sprev, ptok = ps_rd[bk]
                    for kk in range(NF):
                        w = []
                        if kk == 0:
                            w = [la, lw2]
                            if ptok > 0:
                                w.append((sprev, ptok))
                        TS.op(lambda e, n=n, kk=kk, bk=bk, acts_=acts_: e.matmul(
                            psum[bk][:], acts_[:, kk, :], w2r[:, kk, n * 512:(n + 1) * 512],
                            start=(kk == 0), stop=(kk == NF - 1)),
                            waits=w, inc=(kk == NF - 1))
                    tmm = TS.n
                    ps_consume(bk, VS, lambda e, n=n, bk=bk: e.tensor_tensor(
                        yt[:], psum[bk][:], lbias_bc[:, n * 512:(n + 1) * 512], op=OP.add),
                        waits=[(TS, tmm), lx])
                    VS.op(lambda e, n=n, x2t_=x2t_: e.tensor_tensor(
                        x2t_[:, n * 512:(n + 1) * 512], yt[:],
                        x2t_[:, n * 512:(n + 1) * 512], op=OP.add))
                mmP[par] = TS.n
                vvP[par] = VS.n
                stP[par] = STQ.dma(lambda e, i=i, x2t_=x2t_: e.dma_start(
                    out=out[i * 128:(i + 1) * 128, :], in_=x2t_), waits=[(VS, vvP[par])])
        barrier()

    emit_consts()
    emit_wprep(W1, s_w1b, s_pw1b, D * FF, 0, tag="w1")
    emit_wprep(W2, s_w2b, s_pw2b, FF * D, 1, tag="w2")
    emit_bias_prep()
    xnfm = E(nc.sbuf_tensor("xnfm", [128, NP, T], BF16))
    emit_pre(xnfm)
    emit_attn(ffnA_spec=([s_w1b, s_pw1b], [b1fm, pb1fm], s_act,
                         ACT.Gelu_apprx_tanh, xnfm, "f0"))
    if "attnfm" in debug:
        d = dbg_out("attnfm", [NP, 128, T], BF16)
        STQ.dma(lambda e, d=d: e.dma_start(out=d, in_=s_attnfm))
        barrier()
    emit_wconv(Lw1, s_lw1b, D * FF, "l1")
    emit_wconv(Lw2, s_lw2b, FF * D, "l2")
    emit_wo_x1()
    emit_ffnB0()
    if "x1" in debug:
        d = dbg_out("x1", [T, D], F32)
        STQ.dma(lambda e, d=d: e.dma_start(out=d, in_=s_x1))
        barrier()
    emit_l1_ln(xnfm)
    emit_l1_attn(xnfm)
    emit_l1_proj()
    emit_ffnA([s_lw1b], [lb1fm], s_act1, ACT.Gelu, xnfm, "f1")
    emit_ffnB1()

    SY.wait(STQ.all_tokens())

    with nc.Block() as block:
        block.tensor(lambda e: TS.realize(e))
        block.vector(lambda e: VS.realize(e))
        block.scalar(lambda e: SS.realize(e))
        block.gpsimd(lambda e: GS.realize(e))
        block.sync(lambda e: SY.realize(e))

    ctx.close()
    return nc, dbg


def host_inputs(inputs, core):
    x = np.ascontiguousarray(inputs["L0_attn_x"][core])
    m = {
        "x0": x,
        "wqT": np.ascontiguousarray(inputs["L0_attn_wq"].T),
        "wkT": np.ascontiguousarray(inputs["L0_attn_wk"].T),
        "wvT": np.ascontiguousarray(inputs["L0_attn_wv"].T),
        "woT": np.ascontiguousarray(inputs["L0_attn_wo"].T),
        "W1": np.ascontiguousarray(inputs["L0_ffn_W1"]),
        "B1": np.ascontiguousarray(inputs["L0_ffn_b1"]),
        "W2": np.ascontiguousarray(inputs["L0_ffn_W2"]),
        "B2": np.ascontiguousarray(inputs["L0_ffn_b2"]),
        "LNG": np.ascontiguousarray(inputs["L0_ffn_ln_gamma"]),
        "LNB": np.ascontiguousarray(inputs["L0_ffn_ln_beta"]),
        "WR": np.ascontiguousarray(inputs["L0_ffn_Wr"]),
        "BR": np.ascontiguousarray(inputs["L0_ffn_br"]),
        "qkvT": np.ascontiguousarray(inputs["L1_attn_qkv_w"].T),
        "projT": np.ascontiguousarray(inputs["L1_attn_proj_w"].T),
        "projB": np.ascontiguousarray(inputs["L1_attn_proj_b"]),
        "Lw1": np.ascontiguousarray(inputs["L1_ffn_w1"]),
        "Lb1": np.ascontiguousarray(inputs["L1_ffn_b1"]),
        "Lw2": np.ascontiguousarray(inputs["L1_ffn_w2"]),
        "Lb2": np.ascontiguousarray(inputs["L1_ffn_b2"]),
    }
    return {k: np.asarray(v, dtype=np.float32) for k, v in m.items()}


def kernel(**inputs):
    nc, _ = build_program()
    in_maps = [host_inputs(inputs, c) for c in range(8)]
    res = run_bass_kernel_spmd(nc, in_maps, core_ids=list(range(8)))
    return np.stack([r["out"] for r in res.results], axis=0)


# version 15
# speedup vs baseline: 1.0702x; 1.0266x over previous
import math
import numpy as np
import concourse.bass as bass
import concourse.mybir as mybir
from concourse.bass_utils import run_bass_kernel_spmd

F32 = mybir.dt.float32
BF16 = mybir.dt.bfloat16
OP = mybir.AluOpType
ACT = mybir.ActivationFunctionType
AX = mybir.AxisListType

T = 2048
D = 1024
NT = T // 128
NP = D // 128
H = 16
HD = 64
FF = 4096
NF = FF // 128
NEG = -1.0e30
BIGM = 80.0
TINY = 1e-38
EPS = 1e-5
INVSQ = 1.0 / math.sqrt(HD)


def _extract_rounds(qt):
    mu = 0.032 * 128.0 * (qt + 1)
    ub = mu + 6.0 * math.sqrt(mu)
    return max(0, int(math.ceil((ub - 64.0) / 8.0)))


EXTR = [0] + [_extract_rounds(qt) for qt in range(1, NT)]
XCAP = 64


class Stream:
    def __init__(self, name, auto_serial):
        self.name = name
        self.auto = auto_serial
        self.n = 0
        self.ops = []
        self.sem = None

    def op(self, fn, waits=(), inc=True):
        prev = self.n if self.auto else 0
        if inc:
            self.n += 1
        self.ops.append((fn, tuple(waits), inc, prev))
        return self.n

    def wait(self, waits):
        self.ops.append((None, tuple(waits), False, 0))

    def realize(self, eng):
        for fn, waits, inc, serial in self.ops:
            if serial:
                eng.wait_ge(self.sem, serial)
            for s, val in waits:
                if val > 0:
                    eng.wait_ge(getattr(s, "sem", s), val)
            if fn is None:
                continue
            inst = fn(eng)
            if inc:
                inst.then_inc(self.sem, 1)


class DmaQueue:

    def __init__(self, stream, ring=12):
        self.stream = stream
        self.ring = ring
        self.sems = []
        self.uses = [0] * ring
        self.count = 0

    def dma(self, fn, waits=()):
        slot = self.count % self.ring
        self.count += 1
        prior = self.uses[slot]
        self.uses[slot] += 1
        sem = None

        q = self

        def emit(eng, fn=fn, slot=slot, prior=prior):
            if prior:
                eng.wait_ge(q.sems[slot], 16 * prior)
            inst = fn(eng)
            inst.then_inc(q.sems[slot], 16)
            return inst

        self.stream.op(emit, waits=waits, inc=False)
        return (_QSem(self, slot), 16 * (prior + 1))

    def all_tokens(self):
        return [(_QSem(self, i), 16 * u) for i, u in enumerate(self.uses) if u > 0]


class _QSem:

    def __init__(self, q, slot):
        self.q = q
        self.slot = slot

    @property
    def sem(self):
        return self.q.sems[self.slot]


def build_program(debug=()):
    nc = bass.Bass("TRN2", target_bir_lowering=False, debug=False)

    def din(name, shape):
        return nc.dram_tensor(name, shape, F32, kind="ExternalInput").ap()

    x0 = din("x0", [T, D])
    wqT = din("wqT", [D, D]); wkT = din("wkT", [D, D])
    wvT = din("wvT", [D, D]); woT = din("woT", [D, D])
    W1 = din("W1", [D, FF]); B1 = din("B1", [FF])
    W2 = din("W2", [FF, D]); B2 = din("B2", [D])
    LNG = din("LNG", [D]); LNB = din("LNB", [D])
    WR = din("WR", [D, 8]); BR = din("BR", [8])
    qkvT = din("qkvT", [D, 3 * D])
    projT = din("projT", [D, D]); projB = din("projB", [D])
    Lw1 = din("Lw1", [D, FF]); Lb1 = din("Lb1", [FF])
    Lw2 = din("Lw2", [FF, D]); Lb2 = din("Lb2", [D])
    out = nc.dram_tensor("out", [T, D], F32, kind="ExternalOutput").ap()

    s_w1b = nc.dram_tensor("s_w1b", [D, FF], BF16).ap()
    s_pw1b = nc.dram_tensor("s_pw1b", [D, FF], BF16).ap()
    s_w2b = nc.dram_tensor("s_w2b", [FF, D], BF16).ap()
    s_pw2b = nc.dram_tensor("s_pw2b", [FF, D], BF16).ap()
    s_lw1b = nc.dram_tensor("s_lw1b", [D, FF], BF16).ap()
    s_lw2b = nc.dram_tensor("s_lw2b", [FF, D], BF16).ap()
    s_act = nc.dram_tensor("s_act", [2, NF, NT, 128, 128], BF16).ap()
    s_act1 = nc.dram_tensor("s_act1", [1, NF, NT, 128, 128], BF16).ap()
    s_attnfm = nc.dram_tensor("s_attnfm", [NP, 128, T], BF16).ap()
    s_attnfm1 = nc.dram_tensor("s_attnfm1", [D, T], BF16).ap()
    s_x1 = nc.dram_tensor("s_x1", [T, D], F32).ap()
    s_x2 = nc.dram_tensor("s_x2", [T, D], F32).ap()

    dbg = {}

    def dbg_out(name, shape, dtype=F32):
        if name in debug:
            dbg[name] = nc.dram_tensor("dbg_" + name, shape, dtype,
                                       kind="ExternalOutput").ap()
        return dbg.get(name)

    TS = Stream("tensor", auto_serial=False)
    VS = Stream("vector", auto_serial=True)
    SS = Stream("scalar", auto_serial=True)
    GS = Stream("gpsimd", auto_serial=True)
    SY = Stream("sync", auto_serial=False)
    LD = DmaQueue(SY)
    STQ = DmaQueue(SS)
    LDG = DmaQueue(GS)
    streams = [TS, VS, SS, GS, SY]

    from contextlib import ExitStack
    ctx = ExitStack()
    E = ctx.enter_context
    for s, nm in [(TS, "semt"), (VS, "semv"), (SS, "sems"), (GS, "semg"), (SY, "semy")]:
        s.sem = E(nc.semaphore(nm))
    LD.sems = [E(nc.semaphore(f"semld{i}")) for i in range(LD.ring)]
    STQ.sems = [E(nc.semaphore(f"semst{i}")) for i in range(STQ.ring)]
    LDG.sems = [E(nc.semaphore(f"semldg{i}")) for i in range(LDG.ring)]

    def barrier():
        snap = [(s, s.n) for s in streams]
        qtoks = LD.all_tokens() + STQ.all_tokens() + LDG.all_tokens()
        for s in streams:
            w = [(o, v) for o, v in snap if o is not s and v > 0] + qtoks
            s.wait(w)

    eye_bf = E(nc.sbuf_tensor("eye_bf", [128, 128], BF16))
    eye_f = E(nc.sbuf_tensor("eye_f", [128, 128], F32))
    causal01 = E(nc.sbuf_tensor("causal01", [128, 128], F32))
    iota64 = E(nc.sbuf_tensor("iota64", [128, XCAP], F32))
    ones_r = E(nc.sbuf_tensor("ones_r", [1, 128], F32))
    ones_c = E(nc.sbuf_tensor("ones_c", [128, 1], F32))
    thrsb = E(nc.sbuf_tensor("thrsb", [128, 8], F32))
    negbig = E(nc.sbuf_tensor("negbig", [128, 1], F32))
    negbigm = E(nc.sbuf_tensor("negbigm", [128, 1], F32))
    negbig_b = E(nc.sbuf_tensor("negbig_b", [128, 1], BF16))
    neg80c = E(nc.sbuf_tensor("neg80c", [128, 1], F32))
    p0b = E(nc.sbuf_tensor("p0b", [128, NT], F32))
    b1fm = E(nc.sbuf_tensor("b1fm", [128, NF], F32))
    pb1fm = E(nc.sbuf_tensor("pb1fm", [128, NF], F32))
    lb1fm = E(nc.sbuf_tensor("lb1fm", [128, NF], F32))
    cvec_bc = E(nc.sbuf_tensor("cvec_bc", [128, D], F32))
    dvec_bc = E(nc.sbuf_tensor("dvec_bc", [128, D], F32))
    lbias_bc = E(nc.sbuf_tensor("lbias_bc", [128, D], F32))

    psum = [E(nc.psum_tensor(f"ps{i}", [128, 512], F32)) for i in range(6)]
    psumb = [E(nc.psum_tensor(f"psb{i}", [128, 1024], BF16)) for i in range(2)]
    psb_rd = {i: (VS, 0) for i in range(2)}

    def psb_produce(i, fn, waits=(), inc=True):
        s_, tok = psb_rd[i]
        w = list(waits)
        if tok > 0:
            w.append((s_, tok))
        return TS.op(fn, waits=w, inc=inc)

    def psb_consume(i, stream, fn, waits=()):
        tok = stream.op(fn, waits=waits)
        psb_rd[i] = (stream, tok)
        return tok
    ps_rd = {i: (VS, 0) for i in range(8)}

    def ps_produce(i, fn, waits=(), inc=True):
        s, tok = ps_rd[i]
        w = list(waits)
        if tok > 0:
            w.append((s, tok))
        return TS.op(fn, waits=w, inc=inc)

    def ps_consume(i, stream, fn, waits=()):
        tok = stream.op(fn, waits=waits)
        ps_rd[i] = (stream, tok)
        return tok

    def pbcast(dst_col, src11, bank, waits=()):
        w = src11.shape[-1]
        t = ps_produce(bank, lambda e: e.matmul(psum[bank][:, 0:w], ones_r[:],
                                                src11, start=True, stop=True),
                       waits=waits)
        return ps_consume(bank, VS, lambda e: e.tensor_copy(dst_col, psum[bank][:, 0:w]),
                          waits=[(TS, t)])

    def xpsum(dst11, src_col, bank, waits=()):
        t = ps_produce(bank, lambda e: e.matmul(psum[bank][0:1, 0:1], src_col,
                                                ones_c[:], start=True, stop=True),
                       waits=waits)
        return ps_consume(bank, VS, lambda e: e.tensor_copy(dst11, psum[bank][0:1, 0:1]),
                          waits=[(TS, t)])

    def emit_consts():
        g1 = GS.op(lambda e: e.iota(causal01[:], [[1, 128]], base=0,
                                    channel_multiplier=-1,
                                    allow_small_or_imprecise_dtypes=True))
        VS.op(lambda e: e.tensor_scalar(causal01[:], causal01[:], 0.0, None,
                                        op0=OP.is_gt), waits=[(GS, g1)])
        g2 = GS.op(lambda e: e.iota(iota64[:], [[1, XCAP]], base=0,
                                    channel_multiplier=0,
                                    allow_small_or_imprecise_dtypes=True))
        g3 = GS.op(lambda e: e.iota(eye_f[:], [[1, 128]], base=0,
                                    channel_multiplier=-1,
                                    allow_small_or_imprecise_dtypes=True))
        VS.op(lambda e: e.tensor_scalar(eye_f[:], eye_f[:], 0.0, None,
                                        op0=OP.is_equal), waits=[(GS, g3)])
        VS.op(lambda e: e.tensor_copy(eye_bf[:], eye_f[:]))
        VS.op(lambda e: e.memset(ones_r[:], 1.0))
        VS.op(lambda e: e.memset(ones_c[:], 1.0))
        VS.op(lambda e: e.memset(thrsb[:], 0.0))
        VS.op(lambda e: e.memset(negbig[:], NEG))
        VS.op(lambda e: e.memset(negbigm[:], -BIGM))
        VS.op(lambda e: e.memset(negbig_b[:], NEG))
        VS.op(lambda e: e.memset(neg80c[:], -80.0))
        _ = g2

    def emit_wprep(w_hbm, wb_hbm, pwb_hbm, n, thr_idx, sub_iters=24, full_iters=26, tag=""):
        width = n // 128
        k = max(1, int(n * 0.05))
        nchunks = max(1, width // 4096)
        cw = width // nchunks
        wv = w_hbm.rearrange("(p a) c -> p (a c)", p=128) if w_hbm.ndim == 2 else w_hbm
        wbv = wb_hbm.rearrange("(p a) c -> p (a c)", p=128) if wb_hbm.ndim == 2 else wb_hbm
        pwbv = pwb_hbm.rearrange("(p a) c -> p (a c)", p=128) if pwb_hbm.ndim == 2 else pwb_hbm

        with nc.sbuf_tensor("aw" + tag, [128, width], F32) as aw, \
             nc.sbuf_tensor("wchunk" + tag, [128, cw], F32) as wchunk, \
             nc.sbuf_tensor("bchunk" + tag, [128, cw], BF16) as bchunk, \
             nc.sbuf_tensor("mscr" + tag, [128, cw], BF16) as mscr, \
             nc.sbuf_tensor("cnt8" + tag, [128, nchunks], F32) as cnt8, \
             nc.sbuf_tensor("tneg" + tag, [128, 1], F32) as tneg, \
             nc.sbuf_tensor("taub" + tag, [128, 1], F32) as taub, \
             nc.sbuf_tensor("mscrb" + tag, [128, 4096], BF16) as mscrb, \
             nc.sbuf_tensor("cnt8b" + tag, [128, 8], F32) as cnt8b, \
             nc.sbuf_tensor("cnt1" + tag, [128, 1], F32) as cnt1, \
             nc.sbuf_tensor("st" + tag, [1, 16], F32) as st, \
             nc.sbuf_tensor("tau" + tag, [128, 1], F32) as tau:

            prev_st = None
            st_toks = []
            sa = 0
            for c in range(nchunks):
                lt = LD.dma(lambda e, c=c: e.dma_start(out=wchunk[:], in_=wv[:, c * cw:(c + 1) * cw]),
                            waits=[(VS, VS.n)] + ([(SS, sa)] if sa else []))
                sa = SS.op(lambda e, c=c: e.activation(aw[:, c * cw:(c + 1) * cw], wchunk[:],
                                                       ACT.Abs), waits=[lt])
                vb = VS.op(lambda e: e.tensor_copy(bchunk[:], wchunk[:]),
                           waits=[lt] + ([prev_st] if prev_st is not None else []))
                prev_st = STQ.dma(lambda e, c=c: e.dma_start(out=wbv[:, c * cw:(c + 1) * cw], in_=bchunk[:]),
                                  waits=[(VS, vb)])
                st_toks.append(prev_st)
            VS.wait([(SS, sa)])

            VS.op(lambda e: e.memset(st[:], 0.0))
            VS.op(lambda e: e.memset(st[0:1, 1:2], 64.0))
            VS.op(lambda e: e.memset(st[0:1, 3:4], 64.0))
            ntotf = float(n)

            def mid_of(lo_i, hi_i, mid_i):
                VS.op(lambda e: e.tensor_tensor(st[0:1, mid_i:mid_i + 1],
                                                st[0:1, lo_i:lo_i + 1],
                                                st[0:1, hi_i:hi_i + 1], op=OP.add))
                VS.op(lambda e: e.tensor_scalar(st[0:1, mid_i:mid_i + 1],
                                                st[0:1, mid_i:mid_i + 1], 0.5,
                                                None, op0=OP.mult))

            def update(lo_i, hi_i, mid_i, cnt_i, s0, s1, t_lo, t_hi):
                VS.op(lambda e, t_lo=t_lo: e.tensor_scalar(
                    st[0:1, s0:s0 + 1], st[0:1, cnt_i:cnt_i + 1], t_lo, None, op0=OP.is_ge))
                VS.op(lambda e: e.tensor_tensor(st[0:1, s1:s1 + 1], st[0:1, mid_i:mid_i + 1],
                                                st[0:1, lo_i:lo_i + 1], op=OP.subtract))
                VS.op(lambda e: e.scalar_tensor_tensor(
                    st[0:1, lo_i:lo_i + 1], st[0:1, s1:s1 + 1], st[0:1, s0:s0 + 1],
                    st[0:1, lo_i:lo_i + 1], op0=OP.mult, op1=OP.add))
                VS.op(lambda e, t_hi=t_hi: e.tensor_scalar(
                    st[0:1, s0:s0 + 1], st[0:1, cnt_i:cnt_i + 1], t_hi, None, op0=OP.is_lt))
                VS.op(lambda e: e.tensor_tensor(st[0:1, s1:s1 + 1], st[0:1, mid_i:mid_i + 1],
                                                st[0:1, hi_i:hi_i + 1], op=OP.subtract))
                VS.op(lambda e: e.scalar_tensor_tensor(
                    st[0:1, hi_i:hi_i + 1], st[0:1, s1:s1 + 1], st[0:1, s0:s0 + 1],
                    st[0:1, hi_i:hi_i + 1], op0=OP.mult, op1=OP.add))

            def sub_bisect(lo_i, hi_i, target, iters):
                step = 64
                aw_sub = aw.rearrange("p (a s) -> p a s", s=step)[:, :, 0]
                for _ in range(iters):
                    mid_of(lo_i, hi_i, 4)
                    pbcast(tau[:], st[0:1, 4:5], 5, waits=[(VS, VS.n)])
                    vtok = VS.op(lambda e, a=aw_sub: e.tensor_scalar(
                        mscr[:, 0:width // step], a, tau[:], 0.0,
                        op0=OP.is_gt, op1=OP.add, accum_out=cnt1[:]))
                    xpsum(st[0:1, 5:6], cnt1[:], 5, waits=[(VS, vtok)])
                    update(lo_i, hi_i, 4, 5, 6, 7, float(target), float(target))

            def joint_full(k_t, k_b, iters):
                for it_ in range(iters):
                    exact_t = it_ >= iters - 5
                    mid_of(0, 1, 4)
                    pbcast(tau[:], st[0:1, 4:5], 5, waits=[(VS, VS.n)])
                    VS.op(lambda e: e.tensor_scalar(tneg[:], tau[:], -1.0, None, op0=OP.mult))
                    mid_of(2, 3, 8)
                    pbcast(taub[:], st[0:1, 8:9], 5, waits=[(VS, VS.n)])
                    if not exact_t:
                        stok = 0
                        for c in range(nchunks):
                            stok = SS.op(lambda e, c=c: e.activation(
                                mscr[:], aw[:, c * cw:(c + 1) * cw], ACT.Sign, bias=tneg[:],
                                accum_out=cnt8[:, c:c + 1]), waits=[(VS, VS.n)])
                        for c in range(nchunks):
                            VS.op(lambda e, c=c: e.tensor_scalar(
                                mscrb[:], aw[:, c * cw:(c + 1) * cw], taub[:], 0.0,
                                op0=OP.is_gt, op1=OP.add, accum_out=cnt8b[:, c:c + 1]))
                        vtop = VS.op(lambda e: e.reduce_sum(cnt1[:], cnt8[:], axis=AX.X),
                                     waits=[(SS, stok)])
                        xpsum(st[0:1, 5:6], cnt1[:], 5, waits=[(VS, vtop)])
                        VS.op(lambda e: e.tensor_scalar(st[0:1, 5:6], st[0:1, 5:6], 0.5,
                                                        0.5 * ntotf, op0=OP.mult, op1=OP.add))
                    else:
                        for c in range(nchunks):
                            VS.op(lambda e, c=c: e.tensor_scalar(
                                mscr[:], aw[:, c * cw:(c + 1) * cw], tau[:], 0.0,
                                op0=OP.is_gt, op1=OP.add, accum_out=cnt8[:, c:c + 1]))
                        vtop = VS.op(lambda e: e.reduce_sum(cnt1[:], cnt8[:], axis=AX.X))
                        xpsum(st[0:1, 5:6], cnt1[:], 5, waits=[(VS, vtop)])
                        for c in range(nchunks):
                            VS.op(lambda e, c=c: e.tensor_scalar(
                                mscrb[:], aw[:, c * cw:(c + 1) * cw], taub[:], 0.0,
                                op0=OP.is_gt, op1=OP.add, accum_out=cnt8b[:, c:c + 1]))
                    vbot = VS.op(lambda e: e.reduce_sum(cnt1[:], cnt8b[:], axis=AX.X))
                    xpsum(st[0:1, 9:10], cnt1[:], 5, waits=[(VS, vbot)])
                    dz = 0.0 if exact_t else 0.6
                    update(0, 1, 4, 5, 6, 7, k_t + dz, k_t - dz)
                    update(2, 3, 8, 9, 10, 11, float(k_b), float(k_b))

            ssub = 64 if width >= 4096 else 1
            kb = n - k + 1
            if ssub > 1:
                sub_m = 10.0 * math.sqrt(max(k / ssub, 1.0))
                sub_bisect(0, 1, k / ssub + sub_m, sub_iters)
                sub_bisect(2, 3, max(k / ssub - sub_m, 0.5), sub_iters)
                VS.op(lambda e: e.tensor_copy(st[0:1, 1:2], st[0:1, 3:4]))
                VS.op(lambda e: e.memset(st[0:1, 2:3], 0.0))
                VS.op(lambda e: e.memset(st[0:1, 3:4], 64.0))
                sub_bisect(2, 3, kb / ssub + sub_m, sub_iters)
                VS.op(lambda e: e.memset(st[0:1, 8:9], 0.0))
                VS.op(lambda e: e.memset(st[0:1, 9:10], 64.0))
                sub_bisect(8, 9, max(kb / ssub - sub_m, 0.5), sub_iters)
                VS.op(lambda e: e.tensor_copy(st[0:1, 3:4], st[0:1, 9:10]))
                joint_full(float(k), float(kb), full_iters)
            else:
                joint_full(float(k), float(kb), 40)
            VS.op(lambda e, ti=thr_idx: e.tensor_copy(thrsb[0:1, 2 * ti:2 * ti + 1],
                                                      st[0:1, 0:1]))
            VS.op(lambda e, ti=thr_idx: e.tensor_copy(thrsb[0:1, 2 * ti + 1:2 * ti + 2],
                                                      st[0:1, 3:4]))
            pbcast(thrsb[:, 2 * thr_idx:2 * thr_idx + 2],
                   thrsb[0:1, 2 * thr_idx:2 * thr_idx + 2], 5, waits=[(VS, VS.n)])
            prev_st = None
            for c in range(nchunks):
                lt = LD.dma(lambda e, c=c: e.dma_start(out=bchunk[:], in_=wbv[:, c * cw:(c + 1) * cw]),
                            waits=[(VS, VS.n), st_toks[c], st_toks[-1]]
                                  + ([prev_st] if prev_st is not None else []))
                VS.op(lambda e, c=c: e.tensor_scalar(wchunk[:], aw[:, c * cw:(c + 1) * cw],
                                                     thrsb[:, 2 * thr_idx:2 * thr_idx + 1],
                                                     None, op0=OP.is_gt),
                      waits=[lt])
                VS.op(lambda e, c=c: e.scalar_tensor_tensor(
                    wchunk[:], aw[:, c * cw:(c + 1) * cw],
                    thrsb[:, 2 * thr_idx + 1:2 * thr_idx + 2], wchunk[:],
                    op0=OP.is_le, op1=OP.subtract))
                VS.op(lambda e: e.tensor_scalar(wchunk[:], wchunk[:], 0.8, None, op0=OP.mult))
                vp = VS.op(lambda e: e.scalar_tensor_tensor(mscr[:], bchunk[:], 1.0,
                                                            wchunk[:], op0=OP.mult,
                                                            op1=OP.mult))
                vp = VS.op(lambda e: e.tensor_tensor(mscr[:], mscr[:], bchunk[:], op=OP.add))
                prev_st = STQ.dma(lambda e, c=c: e.dma_start(out=pwbv[:, c * cw:(c + 1) * cw], in_=mscr[:]),
                                  waits=[(VS, vp)])
        barrier()

    def emit_bias_prep():
        with nc.sbuf_tensor("b1a", [128, 128], F32) as b1a, \
             nc.sbuf_tensor("ms", [128, NF], F32) as ms, \
             nc.sbuf_tensor("cnt1", [128, 1], F32) as cnt1, \
             nc.sbuf_tensor("st", [1, 8], F32) as st, \
             nc.sbuf_tensor("tau", [128, 1], F32) as tau, \
             nc.sbuf_tensor("b2r", [1, D], F32) as b2r, \
             nc.sbuf_tensor("pb2r", [1, D], F32) as pb2r, \
             nc.sbuf_tensor("lbr", [1, D], F32) as lbr:

            lt = LD.dma(lambda e: e.dma_start(
                out=b1a[0:NF, 0:128].rearrange("a b -> a b"),
                in_=B1.rearrange("(f p) -> f p", p=128)))
            tb1 = ps_produce(4, lambda e: e.transpose(
                psum[4][:, 0:NF], b1a[0:NF, 0:128], eye_f[0:NF, 0:NF]),
                waits=[lt])
            ps_consume(4, VS, lambda e: e.tensor_copy(b1fm[:], psum[4][:, 0:NF]),
                       waits=[(TS, tb1)])
            lt2 = LD.dma(lambda e: e.dma_start(
                out=b1a[0:NF, 0:128].rearrange("a b -> a b"),
                in_=Lb1.rearrange("(f p) -> f p", p=128)),
                waits=[(VS, VS.n)])
            tb2 = ps_produce(4, lambda e: e.transpose(
                psum[4][:, 0:NF], b1a[0:NF, 0:128], eye_f[0:NF, 0:NF]),
                waits=[lt2])
            ps_consume(4, VS, lambda e: e.tensor_copy(lb1fm[:], psum[4][:, 0:NF]),
                       waits=[(TS, tb2)])

            def small_select(src, n, iters=40):
                pass

            k1 = max(1, int(FF * 0.05))
            va = SS.op(lambda e: e.activation(b1a[:, 0:NF], b1fm[:], ACT.Abs),
                       waits=[(VS, VS.n)])
            VS.wait([(SS, va)])
            VS.op(lambda e: e.memset(st[:], 0.0))
            VS.op(lambda e: e.memset(st[0:1, 1:2], 64.0))
            VS.op(lambda e: e.memset(st[0:1, 3:4], 64.0))

            def bis(lo_i, hi_i, target, iters, src, w):
                for _ in range(iters):
                    VS.op(lambda e: e.tensor_tensor(st[0:1, 4:5], st[0:1, lo_i:lo_i + 1],
                                                    st[0:1, hi_i:hi_i + 1], op=OP.add))
                    VS.op(lambda e: e.tensor_scalar(st[0:1, 4:5], st[0:1, 4:5], 0.5,
                                                    None, op0=OP.mult))
                    pbcast(tau[:], st[0:1, 4:5], 5, waits=[(VS, VS.n)])
                    vtok = VS.op(lambda e: e.tensor_scalar(ms[:, 0:w], src[:, 0:w], tau[:],
                                                           0.0, op0=OP.is_gt, op1=OP.add,
                                                           accum_out=cnt1[:]))
                    xpsum(st[0:1, 5:6], cnt1[:], 5, waits=[(VS, vtok)])
                    VS.op(lambda e: e.tensor_scalar(st[0:1, 6:7], st[0:1, 5:6],
                                                    float(target), None, op0=OP.is_ge))
                    VS.op(lambda e: e.tensor_tensor(st[0:1, 7:8], st[0:1, 4:5],
                                                    st[0:1, lo_i:lo_i + 1], op=OP.subtract))
                    VS.op(lambda e: e.scalar_tensor_tensor(
                        st[0:1, lo_i:lo_i + 1], st[0:1, 7:8], st[0:1, 6:7],
                        st[0:1, lo_i:lo_i + 1], op0=OP.mult, op1=OP.add))
                    VS.op(lambda e: e.tensor_tensor(st[0:1, 7:8], st[0:1, 4:5],
                                                    st[0:1, hi_i:hi_i + 1], op=OP.subtract))
                    VS.op(lambda e: e.tensor_scalar(st[0:1, 6:7], st[0:1, 6:7], -1.0, 1.0,
                                                    op0=OP.mult, op1=OP.add))
                    VS.op(lambda e: e.scalar_tensor_tensor(
                        st[0:1, hi_i:hi_i + 1], st[0:1, 7:8], st[0:1, 6:7],
                        st[0:1, hi_i:hi_i + 1], op0=OP.mult, op1=OP.add))

            bis(0, 1, k1, 40, b1a, NF)
            bis(2, 3, FF - k1 + 1, 40, b1a, NF)
            b1a_n = b1a[:, 0:NF]
            pbcast(tau[:], st[0:1, 0:1], 5, waits=[(VS, VS.n)])
            VS.op(lambda e: e.tensor_scalar(ms[:], b1a[:, 0:NF], tau[:], None, op0=OP.is_gt))
            pbcast(tau[:], st[0:1, 3:4], 5, waits=[(VS, VS.n)])
            VS.op(lambda e: e.scalar_tensor_tensor(ms[:], b1a[:, 0:NF], tau[:], ms[:],
                                                   op0=OP.is_le, op1=OP.subtract))
            VS.op(lambda e: e.tensor_scalar(ms[:], ms[:], 0.8, None, op0=OP.mult))
            VS.op(lambda e: e.tensor_mul(ms[:], ms[:], b1fm[:]))
            VS.op(lambda e: e.tensor_add(pb1fm[:], ms[:], b1fm[:]))

            lt3 = LD.dma(lambda e: e.dma_start(out=b2r[:], in_=B2.rearrange("(o d) -> o d", o=1)))
            k2 = max(1, int(D * 0.05))
            va2 = SS.op(lambda e: e.activation(pb2r[:], b2r[:], ACT.Abs),
                        waits=[lt3, (VS, VS.n)])
            VS.wait([(SS, va2)])
            _ = (va, va2)
            VS.op(lambda e: e.memset(st[:], 0.0))
            VS.op(lambda e: e.memset(st[0:1, 1:2], 64.0))
            VS.op(lambda e: e.memset(st[0:1, 3:4], 64.0))

            def bis1(lo_i, hi_i, target, iters):
                for _ in range(iters):
                    VS.op(lambda e: e.tensor_tensor(st[0:1, 4:5], st[0:1, lo_i:lo_i + 1],
                                                    st[0:1, hi_i:hi_i + 1], op=OP.add))
                    VS.op(lambda e: e.tensor_scalar(st[0:1, 4:5], st[0:1, 4:5], 0.5,
                                                    None, op0=OP.mult))
                    VS.op(lambda e: e.tensor_scalar(lbr[:], pb2r[:], st[0:1, 4:5], 0.0,
                                                    op0=OP.is_gt, op1=OP.add,
                                                    accum_out=st[0:1, 5:6]))
                    VS.op(lambda e: e.tensor_scalar(st[0:1, 6:7], st[0:1, 5:6],
                                                    float(target), None, op0=OP.is_ge))
                    VS.op(lambda e: e.tensor_tensor(st[0:1, 7:8], st[0:1, 4:5],
                                                    st[0:1, lo_i:lo_i + 1], op=OP.subtract))
                    VS.op(lambda e: e.scalar_tensor_tensor(
                        st[0:1, lo_i:lo_i + 1], st[0:1, 7:8], st[0:1, 6:7],
                        st[0:1, lo_i:lo_i + 1], op0=OP.mult, op1=OP.add))
                    VS.op(lambda e: e.tensor_tensor(st[0:1, 7:8], st[0:1, 4:5],
                                                    st[0:1, hi_i:hi_i + 1], op=OP.subtract))
                    VS.op(lambda e: e.tensor_scalar(st[0:1, 6:7], st[0:1, 6:7], -1.0, 1.0,
                                                    op0=OP.mult, op1=OP.add))
                    VS.op(lambda e: e.scalar_tensor_tensor(
                        st[0:1, hi_i:hi_i + 1], st[0:1, 7:8], st[0:1, 6:7],
                        st[0:1, hi_i:hi_i + 1], op0=OP.mult, op1=OP.add))

            bis1(0, 1, k2, 40)
            bis1(2, 3, D - k2 + 1, 40)
            VS.op(lambda e: e.tensor_scalar(lbr[:], pb2r[:], st[0:1, 0:1], None, op0=OP.is_gt))
            VS.op(lambda e: e.scalar_tensor_tensor(lbr[:], pb2r[:], st[0:1, 3:4], lbr[:],
                                                   op0=OP.is_le, op1=OP.subtract))
            VS.op(lambda e: e.tensor_scalar(lbr[:], lbr[:], 0.8, None, op0=OP.mult))
            VS.op(lambda e: e.tensor_mul(lbr[:], lbr[:], b2r[:]))
            VS.op(lambda e: e.tensor_add(pb2r[:], lbr[:], b2r[:]))
            pbcast(cvec_bc[:, 0:512], b2r[0:1, 0:512], 5, waits=[(VS, VS.n)])
            pbcast(cvec_bc[:, 512:1024], b2r[0:1, 512:1024], 5)
            VS.op(lambda e: e.tensor_sub(pb2r[:], pb2r[:], b2r[:]))
            pbcast(dvec_bc[:, 0:512], pb2r[0:1, 0:512], 5, waits=[(VS, VS.n)])
            pbcast(dvec_bc[:, 512:1024], pb2r[0:1, 512:1024], 5)
            lt4 = LD.dma(lambda e: e.dma_start(out=b2r[:], in_=projB.rearrange("(o d) -> o d", o=1)),
                         waits=[(VS, VS.n), (TS, TS.n)])
            lt5 = LD.dma(lambda e: e.dma_start(out=pb2r[:], in_=Lb2.rearrange("(o d) -> o d", o=1)),
                         waits=[(VS, VS.n), (TS, TS.n)])
            vv = VS.op(lambda e: e.tensor_add(lbr[:], b2r[:], pb2r[:]),
                       waits=[lt4, lt5])
            pbcast(lbias_bc[:, 0:512], lbr[0:1, 0:512], 5, waits=[(VS, vv)])
            pbcast(lbias_bc[:, 512:1024], lbr[0:1, 512:1024], 5)
            _ = lt4
            if "thr" in debug:
                d = dbg_out("thr", [128, 8])
                STQ.dma(lambda e, d=d: e.dma_start(out=d, in_=thrsb[:]), waits=[(VS, VS.n)])
            if "biases" in debug:
                d = dbg_out("biases", [128, 2 * NF + 2 * D])
                vv2 = VS.op(lambda e: e.engine_nop(), inc=True)
                STQ.dma(lambda e, d=d: e.dma_start(out=d[:, 0:NF], in_=b1fm[:]), waits=[(VS, vv2)])
                STQ.dma(lambda e, d=d: e.dma_start(out=d[:, NF:2 * NF], in_=pb1fm[:]))
                STQ.dma(lambda e, d=d: e.dma_start(out=d[:, 2 * NF:2 * NF + D], in_=cvec_bc[:]))
                STQ.dma(lambda e, d=d: e.dma_start(out=d[:, 2 * NF + D:2 * NF + 2 * D], in_=dvec_bc[:]))
        barrier()

    def emit_pre(xnfm):
        with nc.sbuf_tensor("xt", [128, D], F32) as xt, \
             nc.sbuf_tensor("gam_bc", [128, D], F32) as gam_bc, \
             nc.sbuf_tensor("bet_bc", [128, D], F32) as bet_bc, \
             nc.sbuf_tensor("xc", [128, D], F32) as xc, \
             nc.sbuf_tensor("sqs", [128, D], F32) as sqs, \
             nc.sbuf_tensor("hfm", [128, D], BF16) as hfm, \
             nc.sbuf_tensor("mu", [128, 1], F32) as mu, \
             nc.sbuf_tensor("musq", [128, 1], F32) as musq, \
             nc.sbuf_tensor("ssq", [128, 1], F32) as ssq, \
             nc.sbuf_tensor("veps", [128, 1], F32) as veps, \
             nc.sbuf_tensor("rstd", [128, 1], F32) as rstd, \
             nc.sbuf_tensor("f2", [128, 1], F32) as f2, \
             nc.sbuf_tensor("sc8", [128, 8], F32) as sc8, \
             nc.sbuf_tensor("wrsb", [128, NP, 8], BF16) as wrsb, \
             nc.sbuf_tensor("grow", [1, D], F32) as grow, \
             nc.sbuf_tensor("brow", [1, D], F32) as brow, \
             nc.sbuf_tensor("brr", [1, 8], F32) as brr, \
             nc.sbuf_tensor("brbc", [128, 8], F32) as brbc:

            tg = LD.dma(lambda e: e.dma_start(out=grow[:], in_=LNG.rearrange("(o d) -> o d", o=1)))
            tb = LD.dma(lambda e: e.dma_start(out=brow[:], in_=LNB.rearrange("(o d) -> o d", o=1)))
            tr = LD.dma(lambda e: e.dma_start(out=brr[:], in_=BR.rearrange("(o d) -> o d", o=1)))
            tw = LDG.dma(lambda e: e.dma_start(out=wrsb[:], in_=WR.rearrange("(j p) e -> p j e", p=128)))
            pbcast(gam_bc[:, 0:512], grow[0:1, 0:512], 5, waits=[tg])
            pbcast(gam_bc[:, 512:1024], grow[0:1, 512:1024], 5)
            pbcast(bet_bc[:, 0:512], brow[0:1, 0:512], 5, waits=[tb])
            pbcast(bet_bc[:, 512:1024], brow[0:1, 512:1024], 5)
            pbcast(brbc[:], brr[0:1, :], 5, waits=[tr])

            sq_prev = 0
            tposx_prev = 0
            tposh_prev = 0
            for i in range(NT):
                lx = LD.dma(lambda e, i=i: e.dma_start(out=xt[:], in_=x0[i * 128:(i + 1) * 128, :]),
                            waits=[(VS, VS.n)] + ([(SS, sq_prev)] if sq_prev else []))
                VS.op(lambda e: e.reduce_sum(mu[:], xt[:], axis=AX.X), waits=[lx])
                VS.op(lambda e: e.tensor_scalar(mu[:], mu[:], 1.0 / D, None, op0=OP.mult))
                sq = SS.op(lambda e: e.activation(sqs[:], xt[:], ACT.Square, accum_out=ssq[:]),
                           waits=[lx, (VS, VS.n)])
                sq_prev = sq
                VS.op(lambda e: e.tensor_mul(musq[:], mu[:], mu[:]), waits=[(SS, sq)])
                VS.op(lambda e: e.scalar_tensor_tensor(veps[:], ssq[:], 1.0 / D, musq[:],
                                                       op0=OP.mult, op1=OP.subtract))
                VS.op(lambda e: e.tensor_scalar(veps[:], veps[:], EPS, None, op0=OP.add))
                s2 = SS.op(lambda e: e.activation(rstd[:], veps[:], ACT.Sqrt),
                           waits=[(VS, VS.n)])
                VS.op(lambda e: e.reciprocal(rstd[:], rstd[:]), waits=[(SS, s2)])
                vb = VS.op(lambda e: e.tensor_scalar(xc[:], xt[:], mu[:], rstd[:],
                                                     op0=OP.subtract, op1=OP.mult),
                           waits=([(TS, tposh_prev)] if tposh_prev else []))
                for j in range(NP):
                    bk = j % 2
                    t = ps_produce(bk, lambda e, j=j, bk=bk: e.transpose(
                        psum[bk][:, 0:128], xc[:, j * 128:(j + 1) * 128], eye_f[:]),
                        waits=[(VS, vb)])
                    ps_consume(bk, VS, lambda e, j=j, i=i, bk=bk: e.tensor_copy(
                        xnfm[:, j, i * 128:(i + 1) * 128], psum[bk][:, 0:128]),
                        waits=[(TS, t)])
                    tposx_prev = t
                VS.op(lambda e: e.reciprocal(f2[:], veps[:]))
                VS.op(lambda e: e.tensor_scalar(f2[:], f2[:], -EPS, 1.0,
                                                op0=OP.mult, op1=OP.add))
                VS.op(lambda e: e.tensor_scalar(f2[:], f2[:], EPS, None, op0=OP.add))
                s3 = SS.op(lambda e: e.activation(f2[:], f2[:], ACT.Sqrt), waits=[(VS, VS.n)])
                VS.op(lambda e: e.reciprocal(f2[:], f2[:]), waits=[(SS, s3)])
                VS.op(lambda e: e.scalar_tensor_tensor(sqs[:], xc[:], f2[:], gam_bc[:],
                                                       op0=OP.mult, op1=OP.mult))
                vh = VS.op(lambda e: e.tensor_tensor(xc[:], sqs[:], bet_bc[:], op=OP.add),
                           waits=[(TS, tposx_prev)])
                for j in range(NP):
                    bk = 2 + (j % 2)
                    t = ps_produce(bk, lambda e, j=j, bk=bk: e.transpose(
                        psum[bk][:, 0:128], xc[:, j * 128:(j + 1) * 128], eye_f[:]),
                        waits=[(VS, vh)])
                    ps_consume(bk, VS, lambda e, j=j, bk=bk: e.tensor_copy(
                        hfm[:, j * 128:(j + 1) * 128], psum[bk][:, 0:128]),
                        waits=[(TS, t)])
                    tposh_prev = t
                vhf = VS.n
                s_rd = ps_rd[4]
                for j in range(NP):
                    w = ([(VS, vhf), tw] if j == 0 else [])
                    if j == 0 and s_rd[1] > 0:
                        w.append(s_rd)
                    TS.op(lambda e, j=j: e.matmul(
                        psum[4][:, 0:8], hfm[:, j * 128:(j + 1) * 128], wrsb[:, j, :],
                        start=(j == 0), stop=(j == NP - 1)),
                        waits=w, inc=(j == NP - 1))
                tt = TS.n
                VS.op(lambda e: e.tensor_add(sc8[:], psum[4][:, 0:8], brbc[:]),
                      waits=[(TS, tt)])
                ps_rd[4] = (VS, VS.n)
                VS.op(lambda e: e.tensor_scalar(sc8[:], sc8[:], sc8[:, 0:1], None,
                                                op0=OP.subtract))
                s4 = SS.op(lambda e: e.activation(sc8[:], sc8[:], ACT.Exp, accum_out=f2[:]),
                           waits=[(VS, VS.n)])
                VS.op(lambda e, i=i: e.reciprocal(p0b[:, i:i + 1], f2[:]), waits=[(SS, s4)])

            if "xnfm" in debug:
                d = dbg_out("xnfm", [128, NP * T], BF16)
                STQ.dma(lambda e, d=d: e.dma_start(
                    out=d.rearrange("p (j t) -> p j t", j=NP), in_=xnfm[:]),
                    waits=[(VS, VS.n)])
            if "p0" in debug:
                d = dbg_out("p0", [128, NT])
                STQ.dma(lambda e, d=d: e.dma_start(out=d, in_=p0b[:]), waits=[(VS, VS.n)])
        barrier()


    def emit_attn(ffnA_spec=None):
        from contextlib import ExitStack
        actx = ExitStack()
        A = actx.enter_context
        qfm = A(nc.sbuf_tensor("qfm", [128, NP, T], BF16))
        kfm = A(nc.sbuf_tensor("kfm", [128, NP, T], BF16))
        vtm = A(nc.sbuf_tensor("vtm", [128, NT, D], BF16))
        if True:

            with nc.sbuf_tensor("wprj", [128, NP, D], BF16) as wprj, \
                 nc.sbuf_tensor("ptile", [128, 512], F32) as ptile, \
                 nc.sbuf_tensor("pabs", [128, 512], F32) as pabs, \
                 nc.sbuf_tensor("m8", [128, 8, 8], F32) as m8:
                for (wT_src, dstfm) in ((wqT, qfm), (wkT, kfm), (wvT, None)):
                    lw = LDG.dma(lambda e, wT_src=wT_src: e.dma_start(
                        out=wprj[:], in_=wT_src.rearrange("(j p) d -> p j d", p=128)),
                        waits=[(TS, TS.n)])
                    wsb = wprj
                    for i in range(NT):
                        for n in range(2):
                            bk = n % 2
                            for kk in range(NP):
                                w = [lw] if kk == 0 else []
                                if kk == 0:
                                    sprev, ptok = ps_rd[bk]
                                    if ptok > 0:
                                        w.append((sprev, ptok))
                                TS.op(lambda e, i=i, n=n, kk=kk, bk=bk, wsb=wsb: e.matmul(
                                    psum[bk][:], xnfm[:, kk, i * 128:(i + 1) * 128],
                                    wsb[:, kk, n * 512:(n + 1) * 512],
                                    start=(kk == 0), stop=(kk == NP - 1)),
                                    waits=w, inc=(kk == NP - 1))
                            tmm = TS.n
                            evp = ps_consume(bk, VS, lambda e, bk=bk: e.tensor_copy(ptile[:], psum[bk][:]),
                                             waits=[(TS, tmm)])
                            sab = SS.op(lambda e: e.activation(pabs[:], ptile[:], ACT.Abs),
                                        waits=[(VS, evp)])
                            VS.wait([(SS, sab)])
                            for hh in range(8):
                                VS.op(lambda e, hh=hh: e.max(out=m8[:, hh, :], in_=pabs[:, hh * 64:(hh + 1) * 64]))
                            p3 = pabs.rearrange("p (h d) -> p h d", h=8)
                            VS.op(lambda e, p3=p3: e.tensor_tensor(
                                p3, p3, m8[:, :, 1:2].to_broadcast([128, 8, 64]), op=OP.is_ge))
                            if dstfm is None:
                                VS.op(lambda e, i=i, n=n: e.tensor_tensor(
                                    vtm[:, i, n * 512:(n + 1) * 512], pabs[:], ptile[:], op=OP.mult))
                            else:
                                vsp = VS.op(lambda e: e.tensor_mul(ptile[:], pabs[:], ptile[:]))
                                for jj in range(4):
                                    tb = ps_produce(2 + jj % 2, lambda e, jj=jj: e.transpose(
                                        psum[2 + jj % 2][:, 0:128],
                                        ptile[:, jj * 128:(jj + 1) * 128], eye_f[:]),
                                        waits=[(VS, vsp)])
                                    ps_consume(2 + jj % 2, VS,
                                               lambda e, i=i, n=n, jj=jj, dstfm=dstfm: e.tensor_copy(
                                                   dstfm[:, n * 4 + jj, i * 128:(i + 1) * 128],
                                                   psum[2 + jj % 2][:, 0:128]),
                                               waits=[(TS, tb)])
            barrier()

            scP = [A(nc.sbuf_tensor(f"scp{p}", [128, T], BF16)) for p in range(2)]
            zmP = [A(nc.sbuf_tensor(f"zmp{p}", [128, T], BF16)) for p in range(2)]
            pmP = [A(nc.sbuf_tensor(f"pmp{p}", [128, T], mybir.dt.uint8)) for p in range(2)]
            czP = [A(nc.sbuf_tensor(f"czp{p}", [128, T], F32)) for p in range(2)]
            wbpP = [A(nc.sbuf_tensor(f"wbpp{p}", [128, T], BF16)) for p in range(2)]
            wtbP = [A(nc.sbuf_tensor(f"wtbp{p}", [128, T], BF16)) for p in range(2)]
            x8P = [A(nc.sbuf_tensor(f"x8p{p}", [128, XCAP], F32)) for p in range(2)]
            colP = [A(nc.sbuf_tensor(f"colp{p}", [128, 8], F32)) for p in range(2)]
            otileP = [A(nc.sbuf_tensor(f"otilep{p}", [128, 64], BF16)) for p in range(2)]
            ofmP = [A(nc.sbuf_tensor(f"ofmp{p}", [64, 128], BF16)) for p in range(2)]
            fmm = fact = None
            n_units = 0
            if ffnA_spec is not None:
                fmm, fact, n_units = make_ffnA(A, *ffnA_spec, banks=(2, 3),
                                               raw_spill=True)
            bands = [(h, qt) for h in range(H) for qt in range(NT)]
            fin = {0: None, 1: None}
            state = {}

            def front(i):
                h, qt = bands[i]
                par = i % 2
                sc = scP[par]
                pt, po = h // 2, (h % 2) * 64
                Wk = (qt + 1) * 128
                nch = (Wk + 511) // 512
                qsl = qfm[po:po + 64, pt, qt * 128:(qt + 1) * 128]
                f = fin[par]
                evtoks = []
                for cch in range(nch):
                    cw_ = min(512, Wk - cch * 512)
                    bk = par
                    sprev, ptok = ps_rd[bk]
                    w = [(sprev, ptok)] if ptok > 0 else []
                    TS.op(lambda e, cch=cch, cw_=cw_, bk=bk, qsl=qsl, pt=pt, po=po: e.matmul(
                        psum[bk][:, 0:cw_], qsl,
                        kfm[po:po + 64, pt, cch * 512:cch * 512 + cw_],
                        start=True, stop=True), waits=w, inc=True)
                    tmm = TS.n
                    ww = [(TS, tmm)]
                    if f is not None and cch == 0:
                        ww.append((VS, f["vs_sc"]))
                    ev = SS.op(lambda e, cch=cch, cw_=cw_, bk=bk, sc=sc: e.activation(
                        sc[:, cch * 512:cch * 512 + cw_], psum[bk][:, 0:cw_],
                        ACT.Identity, scale=INVSQ), waits=ww)
                    ps_rd[bk] = (SS, ev)
                    evtoks.append(ev)
                ctok = VS.op(lambda e, qt=qt, sc=sc: e.scalar_tensor_tensor(
                    sc[:, qt * 128:qt * 128 + 128], causal01[:], NEG,
                    sc[:, qt * 128:qt * 128 + 128], op0=OP.mult, op1=OP.add),
                    waits=[(SS, evtoks[-1])])
                sgn = None
                if qt > 0:
                    scol = colP[par][:, 7:8]
                    jk = wtbP[par]
                    sw = [(VS, ctok)]
                    if f is not None:
                        sw.append((TS, f["tav"]))
                    sgn = SS.op(lambda e, Wk=Wk, sc=sc, scol=scol, jk=jk: e.activation(
                        jk[:, 0:Wk], sc[:, 0:Wk], ACT.Sign, accum_out=scol),
                        waits=sw)
                return {"evlast": evtoks[-1], "ctok": ctok, "sgn": sgn}

            last_spill = {0: None, 1: None}

            def epilogue(f2):
                par, h, qt = f2["par"], f2["h"], f2["qt"]
                pt, po = h // 2, (h % 2) * 64
                Dcol = colP[par][:, 4:5]
                rDcol = colP[par][:, 5:6]
                otile, ofm = otileP[par], ofmP[par]
                avbank = 4 + par
                VS.op(lambda e, Dcol=Dcol, rDcol=rDcol: e.tensor_scalar_max(
                    rDcol, Dcol, 1e-30), waits=[(SS, f2["sexp"])])
                VS.op(lambda e, rDcol=rDcol: e.reciprocal(rDcol, rDcol))
                ov = ps_consume(avbank, VS, lambda e, otile=otile, rDcol=rDcol, avbank=avbank: e.tensor_scalar(
                    otile[:], psum[avbank][:, 0:64], rDcol, None, op0=OP.mult),
                    waits=[(TS, f2["tav"])])
                obb = f2["obb"]
                tb = psb_produce(obb, lambda e, otile=otile, obb=obb: e.transpose(
                    psumb[obb][0:64, 0:128], otile[:, 0:64], eye_bf[:]),
                    waits=[(VS, ov)])
                ow = [(TS, tb)]
                if last_spill[par] is not None:
                    ow.append(last_spill[par])
                oev = psb_consume(obb, SS, lambda e, ofm=ofm, obb=obb: e.activation(
                    ofm[:], psumb[obb][0:64, 0:128], ACT.Identity), waits=ow)
                last_spill[par] = STQ.dma(lambda e, pt=pt, po=po, qt=qt, ofm=ofm: e.dma_start(
                    out=s_attnfm[pt, po:po + 64, qt * 128:(qt + 1) * 128],
                    in_=ofm[:]), waits=[(SS, oev)])

            def back(i, st_):
                h, qt = bands[i]
                par = i % 2
                f = fin[par]
                if f is not None:
                    epilogue(f)
                sc, zm, pm, cz = scP[par], zmP[par], pmP[par], czP[par]
                wbp, wtb, x8 = wbpP[par], wtbP[par], x8P[par]
                pcol = colP[par][:, 0:1]
                mcol = colP[par][:, 1:2]
                dcol = colP[par][:, 2:3]
                taucol = colP[par][:, 3:4]
                Dcol = colP[par][:, 4:5]
                pt, po = h // 2, (h % 2) * 64
                Wk = (qt + 1) * 128
                R = EXTR[qt]
                ctok = st_["ctok"]
                sgn = st_["sgn"]
                wT = [(TS, f["ts_wT"])] if f is not None else []
                if qt == 0:
                    VS.op(lambda e, sc=sc, wbp=wbp: e.tensor_copy(wbp[:, 0:128], sc[:, 0:128]),
                          waits=wT)
                    for r in range(8):
                        VS.op(lambda e, r=r, wbp=wbp, x8=x8: e.max(
                            out=x8[:, r * 8:r * 8 + 8], in_=wbp[:, 0:128]))
                        VS.op(lambda e, r=r, wbp=wbp, x8=x8: e.match_replace(
                            out=wbp[:, 0:128], in_to_replace=x8[:, r * 8:r * 8 + 8],
                            in_values=wbp[:, 0:128], imm_value=NEG))
                    VS.op(lambda e, sc=sc, wbp=wbp, zm=zm: e.tensor_tensor(
                        zm[:, 0:128], sc[:, 0:128], wbp[:, 0:128], op=OP.not_equal))
                else:
                    zcol = colP[par][:, 6:7]
                    scol = colP[par][:, 7:8]
                    VS.op(lambda e, Wk=Wk, sc=sc, zm=zm, zcol=zcol: e.tensor_scalar(
                        zm[:, 0:Wk], sc[:, 0:Wk], 0.0, 0.0, op0=OP.is_equal,
                        op1=OP.add, accum_out=zcol))
                    VS.op(lambda e, Wk=Wk, zm=zm, cz=cz: e.tensor_tensor_scan(
                        cz[:, 0:Wk], zm[:, 0:Wk], zm[:, 0:Wk], 0.0,
                        op0=OP.add, op1=OP.bypass),
                        waits=[(SS, f["sexp"])] if f is not None else [])
                    VS.op(lambda e, pcol=pcol, scol=scol, zcol=zcol: e.tensor_sub(
                        pcol, scol, zcol), waits=[(SS, sgn)])
                    VS.op(lambda e, Wk=Wk, pcol=pcol: e.tensor_scalar(
                        pcol, pcol, 0.5, 0.5 * Wk, op0=OP.mult, op1=OP.add))
                    VS.op(lambda e, pcol=pcol, mcol=mcol: e.tensor_scalar(
                        mcol, pcol, -1.0, 64.0, op0=OP.mult, op1=OP.add))
                    VS.op(lambda e, Wk=Wk, zm=zm, cz=cz, mcol=mcol: e.scalar_tensor_tensor(
                        zm[:, 0:Wk], cz[:, 0:Wk], mcol, zm[:, 0:Wk],
                        op0=OP.is_le, op1=OP.mult))
                    if R > 0:
                        VS.op(lambda e, Wk=Wk, sc=sc, pm=pm: e.tensor_scalar(
                            pm[:, 0:Wk], sc[:, 0:Wk], 0.0, None, op0=OP.is_le))
                        sneg = SS.op(lambda e, Wk=Wk, sc=sc, wbp=wbp: e.activation(
                            wbp[:, 0:Wk], sc[:, 0:Wk], ACT.Identity, scale=-1.0),
                            waits=[(VS, VS.n)] + wT)
                        VS.op(lambda e, Wk=Wk, wbp=wbp, pm=pm: e.copy_predicated(
                            wbp[:, 0:Wk], pm[:, 0:Wk], negbig_b[:].to_broadcast([128, Wk])),
                            waits=[(SS, sneg)])
                        for r in range(R):
                            VS.op(lambda e, r=r, Wk=Wk, wbp=wbp, x8=x8: e.max(
                                out=x8[:, r * 8:r * 8 + 8], in_=wbp[:, 0:Wk]))
                            VS.op(lambda e, r=r, Wk=Wk, wbp=wbp, x8=x8: e.match_replace(
                                out=wbp[:, 0:Wk], in_to_replace=x8[:, r * 8:r * 8 + 8],
                                in_values=wbp[:, 0:Wk], imm_value=NEG))
                        VS.op(lambda e, pcol=pcol, dcol=dcol: e.tensor_scalar(
                            dcol, pcol, -64.0, None, op0=OP.add))
                        VS.op(lambda e, dcol=dcol: e.tensor_scalar_max(dcol, dcol, 0.0))
                        VS.op(lambda e, R=R, dcol=dcol: e.tensor_scalar_min(
                            dcol, dcol, float(8 * R - 1)))
                        VS.op(lambda e, R=R, x8=x8, dcol=dcol, taucol=taucol: e.scalar_tensor_tensor(
                            x8[:, 0:8 * R], iota64[:, 0:8 * R], dcol, x8[:, 0:8 * R],
                            op0=OP.is_equal, op1=OP.mult, accum_out=taucol))
                        VS.op(lambda e, taucol=taucol: e.tensor_scalar(
                            taucol, taucol, -1.0, None, op0=OP.mult))
                        VS.op(lambda e, taucol=taucol: e.tensor_scalar_max(taucol, taucol, TINY))
                    else:
                        VS.op(lambda e, taucol=taucol: e.memset(taucol, TINY))
                    VS.op(lambda e, Wk=Wk, sc=sc, zm=zm, taucol=taucol: e.scalar_tensor_tensor(
                        zm[:, 0:Wk], sc[:, 0:Wk], taucol, zm[:, 0:Wk],
                        op0=OP.is_ge, op1=OP.add))
                vw = []
                if qt == 0 and f is not None:
                    vw.append((SS, f["sexp"]))
                vm = VS.op(lambda e, Wk=Wk, sc=sc, zm=zm, cz=cz: e.scalar_tensor_tensor(
                    cz[:, 0:Wk], zm[:, 0:Wk], 80.0, sc[:, 0:Wk],
                    op0=OP.mult, op1=OP.add), waits=vw)
                sexp = SS.op(lambda e, Wk=Wk, cz=cz, wbp=wbp, Dcol=Dcol: e.activation(
                    wbp[:, 0:Wk], cz[:, 0:Wk], ACT.Exp, bias=neg80c[:],
                    accum_out=Dcol), waits=[(VS, vm)] + wT)
                wtev = []
                nkb = qt + 1
                ngrp = (nkb + 7) // 8
                wT_last = 0
                for g in range(ngrp):
                    bb = g % 2
                    blocks = list(range(g * 8, min((g + 1) * 8, nkb)))
                    tb = 0
                    for ii, kb in enumerate(blocks):
                        if ii == 0:
                            tb = psb_produce(bb, lambda e, kb=kb, bb=bb, wbp=wbp: e.transpose(
                                psumb[bb][:, (kb % 8) * 128:(kb % 8 + 1) * 128],
                                wbp[:, kb * 128:(kb + 1) * 128], eye_bf[:]),
                                waits=[(SS, sexp)])
                        else:
                            tb = TS.op(lambda e, kb=kb, bb=bb, wbp=wbp: e.transpose(
                                psumb[bb][:, (kb % 8) * 128:(kb % 8 + 1) * 128],
                                wbp[:, kb * 128:(kb + 1) * 128], eye_bf[:]))
                    wT_last = tb
                    wtev.append(psb_consume(bb, SS, lambda e, g=g, bb=bb, nb=len(blocks), wtb=wtb: e.activation(
                        wtb[:, g * 1024:g * 1024 + nb * 128], psumb[bb][:, 0:nb * 128],
                        ACT.Identity), waits=[(TS, tb)]))
                avbank = 4 + par
                sprev, ptok = ps_rd[avbank]
                for kb in range(nkb):
                    w = []
                    if kb == 0 and ptok > 0:
                        w.append((sprev, ptok))
                    if kb % 8 == 0:
                        w.append((SS, wtev[kb // 8]))
                    TS.op(lambda e, kb=kb, h=h, qt=qt, wtb=wtb, avbank=avbank: e.matmul(
                        psum[avbank][:, 0:64], wtb[:, kb * 128:(kb + 1) * 128],
                        vtm[:, kb, h * 64:(h + 1) * 64],
                        start=(kb == 0), stop=(kb == qt)),
                        waits=w, inc=(kb == qt))
                tav = TS.n
                if "band" in debug and i == 1:
                    dwb = dbg_out("band_w", [128, 256], BF16)
                    dzm = dbg_out("band_g", [128, 256], BF16)
                    dcl = dbg_out("band_col", [128, 8])
                    t1 = STQ.dma(lambda e, wbp=wbp: e.dma_start(out=dwb, in_=wbp[:, 0:256]),
                                 waits=[(SS, sexp)])
                    t2 = STQ.dma(lambda e, zm=zm: e.dma_start(out=dzm, in_=zm[:, 0:256]),
                                 waits=[(SS, sexp)])
                    t3 = STQ.dma(lambda e, par=par: e.dma_start(
                        out=dcl[:, 0:2], in_=colP[par][:, 0:2]), waits=[(SS, sexp)])
                    t4 = STQ.dma(lambda e, par=par: e.dma_start(
                        out=dcl[:, 3:5], in_=colP[par][:, 3:5]), waits=[(SS, sexp)])
                    t5 = STQ.dma(lambda e, par=par: e.dma_start(
                        out=dcl[:, 6:8], in_=colP[par][:, 6:8]), waits=[(SS, sexp)])
                    VS.wait([t1, t2, t3, t4, t5])
                    SS.wait([t1, t2, t3, t4, t5])
                return {"vs_sc": vm, "ts_wT": wT_last, "sexp": sexp, "tav": tav,
                        "obb": ngrp % 2, "par": par, "h": h, "qt": qt}

            for i in range(len(bands) + 1):
                if i < len(bands):
                    state[i] = front(i)
                if i >= 1:
                    fin[(i - 1) % 2] = back(i - 1, state.pop(i - 1))
                    if fmm is not None:
                        if i - 1 < n_units:
                            fmm(i - 1)
                        if i - 2 >= 0:
                            fact(i - 2)
            epilogue(fin[len(bands) % 2])
            epilogue(fin[(len(bands) + 1) % 2])
            if fmm is not None:
                for u in range(max(len(bands) - 1, 0), n_units):
                    fact(u)
        actx.close()
        barrier()


    def emit_wconv(w_hbm, wb_hbm, n, tag):
        width = n // 128
        nchunks = max(1, width // 4096)
        cw = width // nchunks
        wv = w_hbm.rearrange("(p a) c -> p (a c)", p=128)
        wbv = wb_hbm.rearrange("(p a) c -> p (a c)", p=128)
        with nc.sbuf_tensor("wcv" + tag, [128, cw], F32) as wchunk, \
             nc.sbuf_tensor("bcv" + tag, [128, cw], BF16) as bchunk:
            prev_st = None
            for c in range(nchunks):
                lt = LD.dma(lambda e, c=c: e.dma_start(out=wchunk[:], in_=wv[:, c * cw:(c + 1) * cw]),
                            waits=[(VS, VS.n)])
                vb = VS.op(lambda e: e.tensor_copy(bchunk[:], wchunk[:]),
                           waits=[lt] + ([prev_st] if prev_st is not None else []))
                prev_st = STQ.dma(lambda e, c=c: e.dma_start(out=wbv[:, c * cw:(c + 1) * cw], in_=bchunk[:]),
                                  waits=[(VS, vb)])
        barrier()

    def emit_wo_x1():
        with nc.sbuf_tensor("wob", [128, NP, D], BF16) as wob, \
             nc.sbuf_tensor("afm0", [128, NP, 128], BF16) as afm0, \
             nc.sbuf_tensor("afm1", [128, NP, 128], BF16) as afm1, \
             nc.sbuf_tensor("xt1a", [128, D], F32) as xt1a, \
             nc.sbuf_tensor("xt1b", [128, D], F32) as xt1b, \
             nc.sbuf_tensor("x1ta", [128, D], F32) as x1ta, \
             nc.sbuf_tensor("x1tb", [128, D], F32) as x1tb:
            afmP_, xtP_, x1tP_ = [afm0, afm1], [xt1a, xt1b], [x1ta, x1tb]
            lwo = LDG.dma(lambda e: e.dma_start(out=wob[:], in_=woT.rearrange("(j p) d -> p j d", p=128)))
            stP = [None, None]
            mmP = [0, 0]
            vvP = [0, 0]
            for i in range(NT):
                par = i % 2
                afm, xt1, x1t = afmP_[par], xtP_[par], x1tP_[par]
                la = LD.dma(lambda e, i=i, afm=afm: e.dma_start(
                    out=afm[:], in_=s_attnfm[:, :, i * 128:(i + 1) * 128].rearrange("j p t -> p j t")),
                    waits=[(TS, mmP[par])])
                lx = LD.dma(lambda e, i=i, xt1=xt1: e.dma_start(out=xt1[:], in_=x0[i * 128:(i + 1) * 128, :]),
                            waits=[(VS, vvP[par])] + ([stP[par]] if stP[par] is not None else []))
                for n in range(2):
                    bk = 2 * par + n % 2
                    sprev, ptok = ps_rd[bk]
                    for kk in range(NP):
                        w = []
                        if kk == 0:
                            w = [la, lwo]
                            if ptok > 0:
                                w.append((sprev, ptok))
                        TS.op(lambda e, n=n, kk=kk, bk=bk, afm=afm: e.matmul(
                            psum[bk][:], afm[:, kk, :], wob[:, kk, n * 512:(n + 1) * 512],
                            start=(kk == 0), stop=(kk == NP - 1)),
                            waits=w, inc=(kk == NP - 1))
                    tmm = TS.n
                    ps_consume(bk, VS, lambda e, n=n, bk=bk, x1t=x1t, xt1=xt1: e.tensor_tensor(
                        x1t[:, n * 512:(n + 1) * 512], psum[bk][:],
                        xt1[:, n * 512:(n + 1) * 512], op=OP.add),
                        waits=[(TS, tmm), lx] + ([stP[par]] if stP[par] is not None else []))
                mmP[par] = TS.n
                vvP[par] = VS.n
                stP[par] = STQ.dma(lambda e, i=i, x1t=x1t: e.dma_start(
                    out=s_x1[i * 128:(i + 1) * 128, :], in_=x1t[:]), waits=[(VS, vvP[par])])
        barrier()

    def make_ffnA(A, wb_list, bfm_list, act_hbm, gelu_fn, xsrc, tag, banks=(2, 3),
                  raw_spill=False):
        nexp = len(wb_list)
        w1c2 = A(nc.sbuf_tensor("w1c" + tag, [128, 2, NP, 128], BF16))
        pw1c2 = A(nc.sbuf_tensor("pw1c" + tag, [128, 2, NP, 128], BF16))
        actt2 = A(nc.sbuf_tensor("actt" + tag, [128, 2, 512], BF16))
        prev_stores = {0: [], 1: []}
        lastmm = {0: 0, 1: 0}
        ltoks_cur = {}
        upf = 4 * nexp

        mmtoks = {}

        def mm_part(u):
            ft, r = divmod(u, upf)
            nch, ei = divmod(r, nexp)
            fp = ft % 2
            wcs = [w1c2[:, fp], pw1c2[:, fp]][:nexp]
            if r == 0:
                ltoks_cur.clear()
                for e2 in range(nexp):
                    ltoks_cur[e2] = LDG.dma(lambda e, ft=ft, e2=e2, wcs=wcs: e.dma_start(
                        out=wcs[e2],
                        in_=wb_list[e2][:, ft * 128:(ft + 1) * 128].rearrange(
                            "(j p) f -> p j f", p=128)),
                        waits=[(TS, lastmm[fp])])
            bk = banks[u % len(banks)]
            sprev, ptok = ps_rd[bk]
            for kk in range(NP):
                w = []
                if kk == 0:
                    w = [ltoks_cur[ei]]
                    if ptok > 0:
                        w.append((sprev, ptok))
                TS.op(lambda e, nch=nch, ei=ei, kk=kk, bk=bk, wcs=wcs: e.matmul(
                    psum[bk][:], wcs[ei][:, kk, :],
                    xsrc[:, kk, nch * 512:(nch + 1) * 512],
                    start=(kk == 0), stop=(kk == NP - 1)),
                    waits=w, inc=(kk == NP - 1))
            mmtoks[u] = TS.n
            if r == upf - 1:
                lastmm[fp] = TS.n

        def act_part(u):
            ft, r = divmod(u, upf)
            nch, ei = divmod(r, nexp)
            bk = banks[u % len(banks)]
            ap = u % 2
            fn = ACT.Identity if raw_spill else gelu_fn
            sg = SS.op(lambda e, ft=ft, ei=ei, bk=bk, ap=ap, fn=fn: e.activation(
                actt2[:, ap], psum[bk][:], fn, bias=bfm_list[ei][:, ft:ft + 1]),
                waits=[(TS, mmtoks.pop(u))] + prev_stores[ap])
            ps_rd[bk] = (SS, sg)
            prev_stores[ap] = []
            for tt4 in range(4):
                prev_stores[ap].append(STQ.dma(
                    lambda e, ft=ft, ei=ei, nch=nch, tt4=tt4, ap=ap: e.dma_start(
                        out=act_hbm[ei, ft, nch * 4 + tt4, :, :],
                        in_=actt2[:, ap, tt4 * 128:(tt4 + 1) * 128]),
                    waits=[(SS, sg)]))
        return mm_part, act_part, NF * upf

    def emit_ffnA(wb_list, bfm_list, act_hbm, gelu_fn, xsrc, tag):
        from contextlib import ExitStack
        fctx = ExitStack()
        mm_part, act_part, n_units = make_ffnA(
            fctx.enter_context, wb_list, bfm_list, act_hbm,
            gelu_fn, xsrc, tag, banks=(0, 1, 2, 3))
        for u in range(n_units):
            mm_part(u)
            act_part(u)
        fctx.close()
        barrier()

    def emit_ffnB0():
        with nc.sbuf_tensor("w2r", [128, NF, D], BF16) as w2r, \
             nc.sbuf_tensor("pw2r", [128, NF, D], BF16) as pw2r, \
             nc.sbuf_tensor("acts", [128, 2, NF, 128], BF16) as acts, \
             nc.sbuf_tensor("y0t", [128, 512], F32) as y0t, \
             nc.sbuf_tensor("x1t2", [128, 2, D], F32) as x1t2, \
             nc.sbuf_tensor("p0c", [128, 1], F32) as p0c, \
             nc.sbuf_tensor("q0c", [128, 1], F32) as q0c, \
             nc.sbuf_tensor("cvt", [128, D], F32) as cvt:
            lw2 = LDG.dma(lambda e: e.dma_start(
                out=w2r[:], in_=s_w2b.rearrange("(j p) d -> p j d", p=128)))
            lpw2 = LDG.dma(lambda e: e.dma_start(
                out=pw2r[:], in_=s_pw2b.rearrange("(j p) d -> p j d", p=128)))
            stP = [None, None]
            mmP = [0, 0]
            vvP = [0, 0]
            for i in range(NT):
                par = i % 2
                acts_, x1t_ = acts[:], x1t2[:, par]
                la = LD.dma(lambda e, i=i, acts_=acts_: e.dma_start(
                    out=acts_, in_=s_act[:, :, i, :, :].rearrange("e f p t -> p e f t")),
                    waits=[(TS, max(mmP))])
                gl = 0
                for ei2 in range(2):
                    gl = SS.op(lambda e, ei2=ei2: e.activation(
                        acts[:, ei2], acts[:, ei2], ACT.Gelu_apprx_tanh),
                        waits=[la] if ei2 == 0 else [])
                la = (SS, gl)
                lx = LD.dma(lambda e, i=i, x1t_=x1t_: e.dma_start(
                    out=x1t_, in_=s_x1[i * 128:(i + 1) * 128, :]),
                    waits=[(VS, vvP[par])] + ([stP[par]] if stP[par] is not None else []))
                VS.op(lambda e, i=i: e.tensor_copy(p0c[:], p0b[:, i:i + 1]))
                VS.op(lambda e: e.tensor_scalar(q0c[:], p0c[:], -1.0, 1.0, op0=OP.mult, op1=OP.add))
                VS.op(lambda e: e.scalar_tensor_tensor(cvt[:], dvec_bc[:], q0c[:], cvec_bc[:],
                                                       op0=OP.mult, op1=OP.add))
                for n in range(2):
                    for (ei, wsb) in ((0, w2r), (1, pw2r)):
                        bk = 2 * par + ei
                        sprev, ptok = ps_rd[bk]
                        for kk in range(NF):
                            w = []
                            if kk == 0:
                                w = [la, lw2 if ei == 0 else lpw2]
                                if ptok > 0:
                                    w.append((sprev, ptok))
                            TS.op(lambda e, n=n, ei=ei, kk=kk, bk=bk, wsb=wsb, acts_=acts_: e.matmul(
                                psum[bk][:], acts_[:, ei, kk, :], wsb[:, kk, n * 512:(n + 1) * 512],
                                start=(kk == 0), stop=(kk == NF - 1)),
                                waits=w, inc=(kk == NF - 1))
                        if ei == 0:
                            t0 = TS.n
                        else:
                            t1 = TS.n
                    b0, b1 = 2 * par, 2 * par + 1
                    ev0 = ps_consume(b0, VS, lambda e, b0=b0: e.tensor_copy(y0t[:], psum[b0][:]),
                                     waits=[(TS, t0)])
                    vv = ps_consume(b1, VS, lambda e, n=n, b1=b1: e.tensor_sub(y0t[:], y0t[:], psum[b1][:]),
                                    waits=[(TS, t1)])
                    VS.op(lambda e, n=n, b1=b1: e.scalar_tensor_tensor(
                        y0t[:], y0t[:], p0c[:], psum[b1][:], op0=OP.mult, op1=OP.add))
                    ps_rd[b1] = (VS, VS.n)
                    VS.op(lambda e, n=n: e.tensor_tensor(
                        y0t[:], y0t[:], cvt[:, n * 512:(n + 1) * 512], op=OP.add))
                    VS.op(lambda e, n=n, x1t_=x1t_: e.tensor_tensor(
                        x1t_[:, n * 512:(n + 1) * 512], y0t[:],
                        x1t_[:, n * 512:(n + 1) * 512], op=OP.add), waits=[lx])
                mmP[par] = TS.n
                vvP[par] = VS.n
                stP[par] = STQ.dma(lambda e, i=i, x1t_=x1t_: e.dma_start(
                    out=s_x1[i * 128:(i + 1) * 128, :], in_=x1t_), waits=[(VS, vvP[par])])
        barrier()

    def emit_l1_ln(xnfm):
        with nc.sbuf_tensor("xtl", [128, D], F32) as xt, \
             nc.sbuf_tensor("xcl", [128, D], F32) as xc, \
             nc.sbuf_tensor("sql", [128, D], F32) as sqs, \
             nc.sbuf_tensor("mul_", [128, 1], F32) as mu, \
             nc.sbuf_tensor("musql", [128, 1], F32) as musq, \
             nc.sbuf_tensor("ssql", [128, 1], F32) as ssq, \
             nc.sbuf_tensor("vepsl", [128, 1], F32) as veps, \
             nc.sbuf_tensor("rstdl", [128, 1], F32) as rstd:
            sq_prev = 0
            tposx_prev = 0
            for i in range(NT):
                lx = LD.dma(lambda e, i=i: e.dma_start(out=xt[:], in_=s_x1[i * 128:(i + 1) * 128, :]),
                            waits=[(VS, VS.n)] + ([(SS, sq_prev)] if sq_prev else []))
                VS.op(lambda e: e.reduce_sum(mu[:], xt[:], axis=AX.X), waits=[lx])
                VS.op(lambda e: e.tensor_scalar(mu[:], mu[:], 1.0 / D, None, op0=OP.mult))
                sq = SS.op(lambda e: e.activation(sqs[:], xt[:], ACT.Square, accum_out=ssq[:]),
                           waits=[lx, (VS, VS.n)])
                sq_prev = sq
                VS.op(lambda e: e.tensor_mul(musq[:], mu[:], mu[:]), waits=[(SS, sq)])
                VS.op(lambda e: e.scalar_tensor_tensor(veps[:], ssq[:], 1.0 / D, musq[:],
                                                       op0=OP.mult, op1=OP.subtract))
                VS.op(lambda e: e.tensor_scalar(veps[:], veps[:], EPS, None, op0=OP.add))
                s2 = SS.op(lambda e: e.activation(rstd[:], veps[:], ACT.Sqrt),
                           waits=[(VS, VS.n)])
                VS.op(lambda e: e.reciprocal(rstd[:], rstd[:]), waits=[(SS, s2)])
                vb = VS.op(lambda e: e.tensor_scalar(xc[:], xt[:], mu[:], rstd[:],
                                                     op0=OP.subtract, op1=OP.mult),
                           waits=([(TS, tposx_prev)] if tposx_prev else []))
                for j in range(NP):
                    bk = j % 2
                    t = ps_produce(bk, lambda e, j=j, bk=bk: e.transpose(
                        psum[bk][:, 0:128], xc[:, j * 128:(j + 1) * 128], eye_f[:]),
                        waits=[(VS, vb)])
                    ps_consume(bk, VS, lambda e, j=j, i=i, bk=bk: e.tensor_copy(
                        xnfm[:, j, i * 128:(i + 1) * 128], psum[bk][:, 0:128]),
                        waits=[(TS, t)])
                    tposx_prev = t
        barrier()

    BIG1 = 400.0

    def emit_l1_attn(xnfm):
        from contextlib import ExitStack
        actx = ExitStack()
        A = actx.enter_context
        qfm = A(nc.sbuf_tensor("qfm1", [128, NP, T], BF16))
        kfm = A(nc.sbuf_tensor("kfm1", [128, NP, T], BF16))
        vtm = A(nc.sbuf_tensor("vtm1", [128, NT, D], BF16))
        qb = A(nc.sbuf_tensor("qb1", [64, H, 32], F32))
        kb = A(nc.sbuf_tensor("kb1", [64, H, 32], F32))
        bmask = A(nc.sbuf_tensor("bmask1", [32, H, 32], BF16))
        cmat = A(nc.sbuf_tensor("cmat1", [32, 32], F32))
        m8c = A(nc.sbuf_tensor("m8c1", [32, 8], F32))
        em = A(nc.sbuf_tensor("em1", [32, T], BF16))
        emv = A(nc.sbuf_tensor("emv1", [32, T], F32))
        em2 = A(nc.sbuf_tensor("em21", [32, T], BF16))
        eyem = A(nc.sbuf_tensor("eyem1", [32, 32], F32))

        g1 = GS.op(lambda e: e.iota(emv[:], [[1, T]], base=0, channel_multiplier=-64,
                                    allow_small_or_imprecise_dtypes=True))
        VS.op(lambda e: e.tensor_scalar(em2[:], emv[:], 0.0, None, op0=OP.is_ge),
              waits=[(GS, g1)])
        VS.op(lambda e: e.tensor_scalar(em[:], emv[:], 63.0, None, op0=OP.is_le))
        VS.op(lambda e: e.tensor_mul(em[:], em[:], em2[:]))
        g2 = GS.op(lambda e: e.iota(eyem[:], [[1, 32]], base=0, channel_multiplier=-1,
                                    allow_small_or_imprecise_dtypes=True))
        VS.op(lambda e: e.tensor_scalar(eyem[:], eyem[:], 0.0, None, op0=OP.is_equal),
              waits=[(GS, g2)])

        with nc.sbuf_tensor("wprj1", [128, NP, D], BF16) as wprj:
            for (off, dstfm) in ((0, qfm), (D, kfm), (2 * D, None)):
                lw = LDG.dma(lambda e, off=off: e.dma_start(
                    out=wprj[:], in_=qkvT[:, off:off + D].rearrange("(j p) d -> p j d", p=128)),
                    waits=[(TS, TS.n)])
                if dstfm is None:
                    for i in range(NT):
                        for n in range(2):
                            bk = n % 2
                            sprev, ptok = ps_rd[bk]
                            for kk in range(NP):
                                w = []
                                if kk == 0:
                                    w = [lw]
                                    if ptok > 0:
                                        w.append((sprev, ptok))
                                TS.op(lambda e, i=i, n=n, kk=kk, bk=bk: e.matmul(
                                    psum[bk][:], xnfm[:, kk, i * 128:(i + 1) * 128],
                                    wprj[:, kk, n * 512:(n + 1) * 512],
                                    start=(kk == 0), stop=(kk == NP - 1)),
                                    waits=w, inc=(kk == NP - 1))
                            tmm = TS.n
                            ps_consume(bk, VS, lambda e, i=i, n=n, bk=bk: e.tensor_copy(
                                vtm[:, i, n * 512:(n + 1) * 512], psum[bk][:]),
                                waits=[(TS, tmm)])
                else:
                    for j in range(NP):
                        for ch in range(4):
                            bk = ch % 2
                            sprev, ptok = ps_rd[bk]
                            for kk in range(NP):
                                w = []
                                if kk == 0:
                                    w = [lw]
                                    if ptok > 0:
                                        w.append((sprev, ptok))
                                TS.op(lambda e, j=j, ch=ch, kk=kk, bk=bk: e.matmul(
                                    psum[bk][:], wprj[:, kk, j * 128:(j + 1) * 128],
                                    xnfm[:, kk, ch * 512:(ch + 1) * 512],
                                    start=(kk == 0), stop=(kk == NP - 1)),
                                    waits=w, inc=(kk == NP - 1))
                            tmm = TS.n
                            ps_consume(bk, VS, lambda e, j=j, ch=ch, bk=bk, dstfm=dstfm: e.tensor_copy(
                                dstfm[:, j, ch * 512:(ch + 1) * 512], psum[bk][:]),
                                waits=[(TS, tmm)])
        barrier()

        for h in range(H):
            pt, po = h // 2, (h % 2) * 64
            v1 = VS.op(lambda e, h=h, pt=pt, po=po: e.reduce_sum(
                qb[:, h, :], qfm[po:po + 64, pt, :].rearrange("p (m w) -> p m w", w=64),
                axis=AX.X))
            v2 = VS.op(lambda e, h=h, pt=pt, po=po: e.reduce_sum(
                kb[:, h, :], kfm[po:po + 64, pt, :].rearrange("p (m w) -> p m w", w=64),
                axis=AX.X))
            sprev, ptok = ps_rd[5]
            w = [(VS, v2)]
            if ptok > 0:
                w.append((sprev, ptok))
            TS.op(lambda e, h=h: e.matmul(psum[5][0:32, 0:32], qb[:, h, :], kb[:, h, :],
                                          start=True, stop=True), waits=w, inc=True)
            tc = TS.n
            ev = ps_consume(5, SS, lambda e: e.activation(cmat[:], psum[5][0:32, 0:32],
                                                          ACT.Identity, scale=0.125 / 4096.0),
                            waits=[(TS, tc), (VS, VS.n)])
            VS.op(lambda e: e.max(out=m8c[:], in_=cmat[:]), waits=[(SS, ev)])
            VS.op(lambda e, h=h: e.scalar_tensor_tensor(
                bmask[:, h, :], cmat[:], m8c[:, 1:2], eyem[:],
                op0=OP.is_ge, op1=OP.max))
        barrier()

        sc1P = [A(nc.sbuf_tensor(f"sc1P{p}", [128, T], BF16)) for p in range(2)]
        vmP = [A(nc.sbuf_tensor(f"vmP{p}", [128, T], F32)) for p in range(2)]
        wbp1P = [A(nc.sbuf_tensor(f"wbp1P{p}", [128, T], BF16)) for p in range(2)]
        wtb1P = [A(nc.sbuf_tensor(f"wtb1P{p}", [128, T], BF16)) for p in range(2)]
        M2P = [A(nc.sbuf_tensor(f"M2P{p}", [128, 32], F32)) for p in range(2)]
        col1P = [A(nc.sbuf_tensor(f"col1P{p}", [128, 4], F32)) for p in range(2)]
        otile1P = [A(nc.sbuf_tensor(f"otile1P{p}", [128, 64], BF16)) for p in range(2)]
        ofm1P = [A(nc.sbuf_tensor(f"ofm1P{p}", [64, 128], BF16)) for p in range(2)]
        bands1 = [(h, qt) for h in range(H) for qt in range(NT)]
        fin1 = {0: None, 1: None}
        st1 = {}

        def front1(i):
            h, qt = bands1[i]
            par = i % 2
            sc, M2 = sc1P[par], M2P[par]
            pt, po = h // 2, (h % 2) * 64
            qsl = qfm[po:po + 64, pt, qt * 128:(qt + 1) * 128]
            f = fin1[par]
            mbank = 4 + par
            sprev, ptok = ps_rd[mbank]
            w = [(VS, VS.n)]
            if ptok > 0:
                w.append((sprev, ptok))
            TS.op(lambda e, qt=qt, h=h, mbank=mbank: e.matmul(
                psum[mbank][:, 448:480], em[:, qt * 128:(qt + 1) * 128], bmask[:, h, :],
                start=True, stop=True), waits=w, inc=True)
            tm2 = TS.n
            evlast = 0
            vtoks = {}
            for cch in range(4):
                bk = 2 * par + cch % 2
                sprev, ptok = ps_rd[bk]
                w = [(sprev, ptok)] if ptok > 0 else []
                TS.op(lambda e, cch=cch, bk=bk, qsl=qsl, pt=pt, po=po: e.matmul(
                    psum[bk][:], qsl, kfm[po:po + 64, pt, cch * 512:(cch + 1) * 512],
                    start=True, stop=True), waits=w, inc=True)
                tmm = TS.n
                if cch < 2:
                    ww = [(TS, tmm)]
                    if f is not None and cch == 0:
                        ww.append((VS, f["vs_sc"]))
                    evlast = SS.op(lambda e, cch=cch, bk=bk, sc=sc: e.activation(
                        sc[:, cch * 512:(cch + 1) * 512], psum[bk][:],
                        ACT.Identity, scale=0.125), waits=ww)
                    ps_rd[bk] = (SS, evlast)
                else:
                    vtoks[cch] = tmm
            return {"evlast": evlast, "tm2": tm2, "vtoks": vtoks}

        last_spill1 = {0: None, 1: None}

        def epilogue1(f2):
            par, h, qt = f2["par"], f2["h"], f2["qt"]
            pt, po = h // 2, (h % 2) * 64
            Dcol = col1P[par][:, 1:2]
            rDcol = col1P[par][:, 2:3]
            otile, ofm = otile1P[par], ofm1P[par]
            avbank = 4 + par
            VS.op(lambda e, Dcol=Dcol, rDcol=rDcol: e.tensor_scalar_max(
                rDcol, Dcol, 1e-30), waits=[(SS, f2["ss_exp"])])
            VS.op(lambda e, rDcol=rDcol: e.reciprocal(rDcol, rDcol))
            ov = ps_consume(avbank, VS, lambda e, otile=otile, rDcol=rDcol, avbank=avbank: e.tensor_scalar(
                otile[:], psum[avbank][:, 0:64], rDcol, None, op0=OP.mult),
                waits=[(TS, f2["tav"])])
            tb = psb_produce(0, lambda e, otile=otile: e.transpose(
                psumb[0][0:64, 0:128], otile[:, 0:64], eye_bf[:]),
                waits=[(VS, ov)])
            ow = [(TS, tb)]
            if last_spill1[par] is not None:
                ow.append(last_spill1[par])
            oev = psb_consume(0, SS, lambda e, ofm=ofm: e.activation(
                ofm[:], psumb[0][0:64, 0:128], ACT.Identity), waits=ow)
            last_spill1[par] = STQ.dma(lambda e, pt=pt, po=po, qt=qt, ofm=ofm: e.dma_start(
                out=s_attnfm1.rearrange("(j p) t -> j p t", p=128)[
                    pt, po:po + 64, qt * 128:(qt + 1) * 128],
                in_=ofm[:]), waits=[(SS, oev)])

        def back1(i, st_):
            h, qt = bands1[i]
            par = i % 2
            f = fin1[par]
            sc, M2, vm = sc1P[par], M2P[par], vmP[par]
            wbp, wtb = wbp1P[par], wtb1P[par]
            mxc = col1P[par][:, 0:1]
            Dcol = col1P[par][:, 1:2]
            pt, po = h // 2, (h % 2) * 64
            for cch, tmm in st_["vtoks"].items():
                bk = 2 * par + cch % 2
                vev = VS.op(lambda e, cch=cch, bk=bk, sc=sc: e.tensor_scalar(
                    sc[:, cch * 512:(cch + 1) * 512], psum[bk][:], 0.125,
                    None, op0=OP.mult), waits=[(TS, tmm)])
                ps_rd[bk] = (VS, vev)
            mbank = 4 + par
            VS.op(lambda e, M2=M2, mbank=mbank: e.tensor_copy(
                M2[:], psum[mbank][:, 448:480]), waits=[(TS, st_["tm2"])])
            ps_rd[mbank] = (VS, VS.op(lambda e, M2=M2: e.tensor_scalar(
                M2[:], M2[:], 1e30, -1e30, op0=OP.mult, op1=OP.add)))
            if f is not None:
                epilogue1(f)
            sc3 = sc.rearrange("p (m b) -> p m b", b=64)
            vm3 = vm.rearrange("p (m b) -> p m b", b=64)
            vw = [(SS, st_["evlast"])]
            if f is not None:
                vw.append((SS, f["ss_exp"]))
            vmt = VS.op(lambda e, sc3=sc3, vm3=vm3, M2=M2: e.scalar_tensor_tensor(
                vm3, M2[:].unsqueeze(2).to_broadcast([128, 32, 64]), 1.0, sc3,
                op0=OP.mult, op1=OP.add), waits=vw)
            VS.op(lambda e, vm=vm, mxc=mxc: e.reduce_max(mxc, vm[:], axis=AX.X))
            vneg = VS.op(lambda e, mxc=mxc: e.tensor_scalar(mxc, mxc, -1.0, None, op0=OP.mult))
            ew = [(VS, vneg)]
            if f is not None:
                ew.append(f["vs_wbp"])
            sexp = SS.op(lambda e, vm=vm, wbp=wbp, Dcol=Dcol, mxc=mxc: e.activation(
                wbp[:], vm[:], ACT.Exp, bias=mxc, accum_out=Dcol), waits=ew)
            wtev = []
            for g in range(2):
                bb = g % 2
                tb = 0
                for ii in range(8):
                    kbi = g * 8 + ii
                    if ii == 0:
                        tb = psb_produce(bb, lambda e, kbi=kbi, bb=bb, wbp=wbp: e.transpose(
                            psumb[bb][:, (kbi % 8) * 128:(kbi % 8 + 1) * 128],
                            wbp[:, kbi * 128:(kbi + 1) * 128], eye_bf[:]),
                            waits=[(SS, sexp)])
                    else:
                        tb = TS.op(lambda e, kbi=kbi, bb=bb, wbp=wbp: e.transpose(
                            psumb[bb][:, (kbi % 8) * 128:(kbi % 8 + 1) * 128],
                            wbp[:, kbi * 128:(kbi + 1) * 128], eye_bf[:]))
                wtev.append((SS, psb_consume(bb, SS, lambda e, g=g, bb=bb, wtb=wtb: e.activation(
                    wtb[:, g * 1024:(g + 1) * 1024], psumb[bb][:, 0:1024],
                    ACT.Identity), waits=[(TS, tb)])))
            avbank = 4 + par
            sprev, ptok = ps_rd[avbank]
            for kbi in range(NT):
                w = []
                if kbi == 0 and ptok > 0:
                    w.append((sprev, ptok))
                if kbi % 8 == 0:
                    w.append(wtev[kbi // 8])
                TS.op(lambda e, kbi=kbi, h=h, wtb=wtb, avbank=avbank: e.matmul(
                    psum[avbank][:, 0:64], wtb[:, kbi * 128:(kbi + 1) * 128],
                    vtm[:, kbi, h * 64:(h + 1) * 64],
                    start=(kbi == 0), stop=(kbi == NT - 1)),
                    waits=w, inc=(kbi == NT - 1))
            tav = TS.n
            return {"vs_sc": vmt, "vs_wbp": wtev[-1],
                    "ss_exp": sexp, "tav": tav, "par": par, "h": h, "qt": qt}

        for i in range(len(bands1) + 1):
            if i < len(bands1):
                st1[i] = front1(i)
            if i >= 1:
                fin1[(i - 1) % 2] = back1(i - 1, st1.pop(i - 1))
        epilogue1(fin1[len(bands1) % 2])
        epilogue1(fin1[(len(bands1) + 1) % 2])
        actx.close()
        barrier()

    def emit_l1_proj():
        with nc.sbuf_tensor("prjb", [128, NP, D], BF16) as prjb, \
             nc.sbuf_tensor("afml1", [128, 2, NP, 128], BF16) as afm, \
             nc.sbuf_tensor("xt2", [128, 2, D], F32) as xt2, \
             nc.sbuf_tensor("x2t", [128, 2, D], F32) as x2t:
            lwo = LDG.dma(lambda e: e.dma_start(out=prjb[:], in_=projT.rearrange("(j p) d -> p j d", p=128)))
            stP = [None, None]
            mmP = [0, 0]
            vvP = [0, 0]
            for i in range(NT):
                par = i % 2
                afm_, xt2_, x2t_ = afm[:, par], xt2[:, par], x2t[:, par]
                la = LD.dma(lambda e, i=i, afm_=afm_: e.dma_start(
                    out=afm_, in_=s_attnfm1.rearrange("(j p) t -> p j t", p=128)[:, :, i * 128:(i + 1) * 128]),
                    waits=[(TS, mmP[par])])
                lx = LD.dma(lambda e, i=i, xt2_=xt2_: e.dma_start(out=xt2_, in_=s_x1[i * 128:(i + 1) * 128, :]),
                            waits=[(VS, vvP[par])] + ([stP[par]] if stP[par] is not None else []))
                for n in range(2):
                    bk = 2 * par + n % 2
                    sprev, ptok = ps_rd[bk]
                    for kk in range(NP):
                        w = []
                        if kk == 0:
                            w = [la, lwo]
                            if ptok > 0:
                                w.append((sprev, ptok))
                        TS.op(lambda e, n=n, kk=kk, bk=bk, afm_=afm_: e.matmul(
                            psum[bk][:], afm_[:, kk, :], prjb[:, kk, n * 512:(n + 1) * 512],
                            start=(kk == 0), stop=(kk == NP - 1)),
                            waits=w, inc=(kk == NP - 1))
                    tmm = TS.n
                    ps_consume(bk, VS, lambda e, n=n, bk=bk, x2t_=x2t_, xt2_=xt2_: e.tensor_tensor(
                        x2t_[:, n * 512:(n + 1) * 512], psum[bk][:],
                        xt2_[:, n * 512:(n + 1) * 512], op=OP.add),
                        waits=[(TS, tmm), lx] + ([stP[par]] if stP[par] is not None else []))
                mmP[par] = TS.n
                vvP[par] = VS.n
                stP[par] = STQ.dma(lambda e, i=i, x2t_=x2t_: e.dma_start(
                    out=s_x2[i * 128:(i + 1) * 128, :], in_=x2t_), waits=[(VS, vvP[par])])
        barrier()

    def emit_ffnB1():
        with nc.sbuf_tensor("w2r1", [128, NF, D], BF16) as w2r, \
             nc.sbuf_tensor("acts1", [128, 2, NF, 128], BF16) as acts, \
             nc.sbuf_tensor("yt1", [128, 512], F32) as yt, \
             nc.sbuf_tensor("x2t2", [128, 2, D], F32) as x2t2:
            lw2 = LDG.dma(lambda e: e.dma_start(
                out=w2r[:], in_=s_lw2b.rearrange("(j p) d -> p j d", p=128)))
            stP = [None, None]
            mmP = [0, 0]
            vvP = [0, 0]
            for i in range(NT):
                par = i % 2
                acts_, x2t_ = acts[:, par], x2t2[:, par]
                la = LD.dma(lambda e, i=i, acts_=acts_: e.dma_start(
                    out=acts_, in_=s_act1[0, :, i, :, :].rearrange("f p t -> p f t")),
                    waits=[(TS, mmP[par])])
                lx = LD.dma(lambda e, i=i, x2t_=x2t_: e.dma_start(out=x2t_, in_=s_x2[i * 128:(i + 1) * 128, :]),
                            waits=[(VS, vvP[par])] + ([stP[par]] if stP[par] is not None else []))
                for n in range(2):
                    bk = 2 * par + n % 2
                    sprev, ptok = ps_rd[bk]
                    for kk in range(NF):
                        w = []
                        if kk == 0:
                            w = [la, lw2]
                            if ptok > 0:
                                w.append((sprev, ptok))
                        TS.op(lambda e, n=n, kk=kk, bk=bk, acts_=acts_: e.matmul(
                            psum[bk][:], acts_[:, kk, :], w2r[:, kk, n * 512:(n + 1) * 512],
                            start=(kk == 0), stop=(kk == NF - 1)),
                            waits=w, inc=(kk == NF - 1))
                    tmm = TS.n
                    ps_consume(bk, VS, lambda e, n=n, bk=bk: e.tensor_tensor(
                        yt[:], psum[bk][:], lbias_bc[:, n * 512:(n + 1) * 512], op=OP.add),
                        waits=[(TS, tmm), lx])
                    VS.op(lambda e, n=n, x2t_=x2t_: e.tensor_tensor(
                        x2t_[:, n * 512:(n + 1) * 512], yt[:],
                        x2t_[:, n * 512:(n + 1) * 512], op=OP.add))
                mmP[par] = TS.n
                vvP[par] = VS.n
                stP[par] = STQ.dma(lambda e, i=i, x2t_=x2t_: e.dma_start(
                    out=out[i * 128:(i + 1) * 128, :], in_=x2t_), waits=[(VS, vvP[par])])
        barrier()

    emit_consts()
    emit_wprep(W1, s_w1b, s_pw1b, D * FF, 0, tag="w1")
    emit_wprep(W2, s_w2b, s_pw2b, FF * D, 1, tag="w2")
    emit_bias_prep()
    xnfm = E(nc.sbuf_tensor("xnfm", [128, NP, T], BF16))
    emit_pre(xnfm)
    emit_attn(ffnA_spec=([s_w1b, s_pw1b], [b1fm, pb1fm], s_act,
                         ACT.Gelu_apprx_tanh, xnfm, "f0"))
    if "attnfm" in debug:
        d = dbg_out("attnfm", [NP, 128, T], BF16)
        STQ.dma(lambda e, d=d: e.dma_start(out=d, in_=s_attnfm))
        barrier()
    emit_wconv(Lw1, s_lw1b, D * FF, "l1")
    emit_wconv(Lw2, s_lw2b, FF * D, "l2")
    emit_wo_x1()
    emit_ffnB0()
    if "x1" in debug:
        d = dbg_out("x1", [T, D], F32)
        STQ.dma(lambda e, d=d: e.dma_start(out=d, in_=s_x1))
        barrier()
    emit_l1_ln(xnfm)
    emit_l1_attn(xnfm)
    emit_l1_proj()
    emit_ffnA([s_lw1b], [lb1fm], s_act1, ACT.Gelu, xnfm, "f1")
    emit_ffnB1()

    SY.wait(STQ.all_tokens())

    with nc.Block() as block:
        block.tensor(lambda e: TS.realize(e))
        block.vector(lambda e: VS.realize(e))
        block.scalar(lambda e: SS.realize(e))
        block.gpsimd(lambda e: GS.realize(e))
        block.sync(lambda e: SY.realize(e))

    ctx.close()
    return nc, dbg


def host_inputs(inputs, core):
    x = np.ascontiguousarray(inputs["L0_attn_x"][core])
    m = {
        "x0": x,
        "wqT": np.ascontiguousarray(inputs["L0_attn_wq"].T),
        "wkT": np.ascontiguousarray(inputs["L0_attn_wk"].T),
        "wvT": np.ascontiguousarray(inputs["L0_attn_wv"].T),
        "woT": np.ascontiguousarray(inputs["L0_attn_wo"].T),
        "W1": np.ascontiguousarray(inputs["L0_ffn_W1"]),
        "B1": np.ascontiguousarray(inputs["L0_ffn_b1"]),
        "W2": np.ascontiguousarray(inputs["L0_ffn_W2"]),
        "B2": np.ascontiguousarray(inputs["L0_ffn_b2"]),
        "LNG": np.ascontiguousarray(inputs["L0_ffn_ln_gamma"]),
        "LNB": np.ascontiguousarray(inputs["L0_ffn_ln_beta"]),
        "WR": np.ascontiguousarray(inputs["L0_ffn_Wr"]),
        "BR": np.ascontiguousarray(inputs["L0_ffn_br"]),
        "qkvT": np.ascontiguousarray(inputs["L1_attn_qkv_w"].T),
        "projT": np.ascontiguousarray(inputs["L1_attn_proj_w"].T),
        "projB": np.ascontiguousarray(inputs["L1_attn_proj_b"]),
        "Lw1": np.ascontiguousarray(inputs["L1_ffn_w1"]),
        "Lb1": np.ascontiguousarray(inputs["L1_ffn_b1"]),
        "Lw2": np.ascontiguousarray(inputs["L1_ffn_w2"]),
        "Lb2": np.ascontiguousarray(inputs["L1_ffn_b2"]),
    }
    return {k: np.asarray(v, dtype=np.float32) for k, v in m.items()}


def kernel(**inputs):
    nc, _ = build_program()
    in_maps = [host_inputs(inputs, c) for c in range(8)]
    res = run_bass_kernel_spmd(nc, in_maps, core_ids=list(range(8)))
    return np.stack([r["out"] for r in res.results], axis=0)


# version 16
# speedup vs baseline: 1.0837x; 1.0127x over previous
import math
import numpy as np
import concourse.bass as bass
import concourse.mybir as mybir
from concourse.bass_utils import run_bass_kernel_spmd

F32 = mybir.dt.float32
BF16 = mybir.dt.bfloat16
OP = mybir.AluOpType
ACT = mybir.ActivationFunctionType
AX = mybir.AxisListType

T = 2048
D = 1024
NT = T // 128
NP = D // 128
H = 16
HD = 64
FF = 4096
NF = FF // 128
NEG = -1.0e30
BIGM = 80.0
TINY = 1e-38
EPS = 1e-5
INVSQ = 1.0 / math.sqrt(HD)


def _extract_rounds(qt):
    mu = 0.032 * 128.0 * (qt + 1)
    ub = mu + 6.0 * math.sqrt(mu)
    return max(0, int(math.ceil((ub - 64.0) / 8.0)))


EXTR = [0] + [_extract_rounds(qt) for qt in range(1, NT)]
XCAP = 64


class Stream:
    def __init__(self, name, auto_serial):
        self.name = name
        self.auto = auto_serial
        self.n = 0
        self.ops = []
        self.sem = None

    def op(self, fn, waits=(), inc=True):
        prev = self.n if self.auto else 0
        if inc:
            self.n += 1
        self.ops.append((fn, tuple(waits), inc, prev))
        return self.n

    def wait(self, waits):
        self.ops.append((None, tuple(waits), False, 0))

    def realize(self, eng):
        for fn, waits, inc, serial in self.ops:
            if serial:
                eng.wait_ge(self.sem, serial)
            for s, val in waits:
                if val > 0:
                    eng.wait_ge(getattr(s, "sem", s), val)
            if fn is None:
                continue
            inst = fn(eng)
            if inc:
                inst.then_inc(self.sem, 1)


class DmaQueue:

    def __init__(self, stream, ring=12):
        self.stream = stream
        self.ring = ring
        self.sems = []
        self.uses = [0] * ring
        self.count = 0

    def dma(self, fn, waits=()):
        slot = self.count % self.ring
        self.count += 1
        prior = self.uses[slot]
        self.uses[slot] += 1
        sem = None

        q = self

        def emit(eng, fn=fn, slot=slot, prior=prior):
            if prior:
                eng.wait_ge(q.sems[slot], 16 * prior)
            inst = fn(eng)
            inst.then_inc(q.sems[slot], 16)
            return inst

        self.stream.op(emit, waits=waits, inc=False)
        return (_QSem(self, slot), 16 * (prior + 1))

    def all_tokens(self):
        return [(_QSem(self, i), 16 * u) for i, u in enumerate(self.uses) if u > 0]


class _QSem:

    def __init__(self, q, slot):
        self.q = q
        self.slot = slot

    @property
    def sem(self):
        return self.q.sems[self.slot]


def build_program(debug=()):
    nc = bass.Bass("TRN2", target_bir_lowering=False, debug=False)

    def din(name, shape):
        return nc.dram_tensor(name, shape, F32, kind="ExternalInput").ap()

    x0 = din("x0", [T, D])
    wqT = din("wqT", [D, D]); wkT = din("wkT", [D, D])
    wvT = din("wvT", [D, D]); woT = din("woT", [D, D])
    W1 = din("W1", [D, FF]); B1 = din("B1", [FF])
    W2 = din("W2", [FF, D]); B2 = din("B2", [D])
    LNG = din("LNG", [D]); LNB = din("LNB", [D])
    WR = din("WR", [D, 8]); BR = din("BR", [8])
    qkvT = din("qkvT", [D, 3 * D])
    projT = din("projT", [D, D]); projB = din("projB", [D])
    Lw1 = din("Lw1", [D, FF]); Lb1 = din("Lb1", [FF])
    Lw2 = din("Lw2", [FF, D]); Lb2 = din("Lb2", [D])
    out = nc.dram_tensor("out", [T, D], F32, kind="ExternalOutput").ap()

    s_w1b = nc.dram_tensor("s_w1b", [D, FF], BF16).ap()
    s_pw1b = nc.dram_tensor("s_pw1b", [D, FF], BF16).ap()
    s_w2b = nc.dram_tensor("s_w2b", [FF, D], BF16).ap()
    s_pw2b = nc.dram_tensor("s_pw2b", [FF, D], BF16).ap()
    s_lw1b = nc.dram_tensor("s_lw1b", [D, FF], BF16).ap()
    s_lw2b = nc.dram_tensor("s_lw2b", [FF, D], BF16).ap()
    s_act = nc.dram_tensor("s_act", [2, NF, NT, 128, 128], BF16).ap()
    s_act1 = nc.dram_tensor("s_act1", [1, NF, NT, 128, 128], BF16).ap()
    s_attnfm = nc.dram_tensor("s_attnfm", [NP, 128, T], BF16).ap()
    s_attnfm1 = nc.dram_tensor("s_attnfm1", [D, T], BF16).ap()
    s_x1 = nc.dram_tensor("s_x1", [T, D], F32).ap()
    s_x2 = nc.dram_tensor("s_x2", [T, D], F32).ap()

    dbg = {}

    def dbg_out(name, shape, dtype=F32):
        if name in debug:
            dbg[name] = nc.dram_tensor("dbg_" + name, shape, dtype,
                                       kind="ExternalOutput").ap()
        return dbg.get(name)

    TS = Stream("tensor", auto_serial=False)
    VS = Stream("vector", auto_serial=True)
    SS = Stream("scalar", auto_serial=True)
    GS = Stream("gpsimd", auto_serial=True)
    SY = Stream("sync", auto_serial=False)
    LD = DmaQueue(SY)
    STQ = DmaQueue(SS)
    LDG = DmaQueue(GS)
    streams = [TS, VS, SS, GS, SY]

    from contextlib import ExitStack
    ctx = ExitStack()
    E = ctx.enter_context
    for s, nm in [(TS, "semt"), (VS, "semv"), (SS, "sems"), (GS, "semg"), (SY, "semy")]:
        s.sem = E(nc.semaphore(nm))
    LD.sems = [E(nc.semaphore(f"semld{i}")) for i in range(LD.ring)]
    STQ.sems = [E(nc.semaphore(f"semst{i}")) for i in range(STQ.ring)]
    LDG.sems = [E(nc.semaphore(f"semldg{i}")) for i in range(LDG.ring)]

    def barrier():
        snap = [(s, s.n) for s in streams]
        qtoks = LD.all_tokens() + STQ.all_tokens() + LDG.all_tokens()
        for s in streams:
            w = [(o, v) for o, v in snap if o is not s and v > 0] + qtoks
            s.wait(w)

    eye_bf = E(nc.sbuf_tensor("eye_bf", [128, 128], BF16))
    eye_f = E(nc.sbuf_tensor("eye_f", [128, 128], F32))
    causal01 = E(nc.sbuf_tensor("causal01", [128, 128], F32))
    iota64 = E(nc.sbuf_tensor("iota64", [128, XCAP], F32))
    ones_r = E(nc.sbuf_tensor("ones_r", [1, 128], F32))
    ones_c = E(nc.sbuf_tensor("ones_c", [128, 1], F32))
    thrsb = E(nc.sbuf_tensor("thrsb", [128, 8], F32))
    negbig = E(nc.sbuf_tensor("negbig", [128, 1], F32))
    negbigm = E(nc.sbuf_tensor("negbigm", [128, 1], F32))
    negbig_b = E(nc.sbuf_tensor("negbig_b", [128, 1], BF16))
    neg80c = E(nc.sbuf_tensor("neg80c", [128, 1], F32))
    p0b = E(nc.sbuf_tensor("p0b", [128, NT], F32))
    b1fm = E(nc.sbuf_tensor("b1fm", [128, NF], F32))
    pb1fm = E(nc.sbuf_tensor("pb1fm", [128, NF], F32))
    lb1fm = E(nc.sbuf_tensor("lb1fm", [128, NF], F32))
    cvec_bc = E(nc.sbuf_tensor("cvec_bc", [128, D], F32))
    dvec_bc = E(nc.sbuf_tensor("dvec_bc", [128, D], F32))
    lbias_bc = E(nc.sbuf_tensor("lbias_bc", [128, D], F32))

    psum = [E(nc.psum_tensor(f"ps{i}", [128, 512], F32)) for i in range(6)]
    psumb = [E(nc.psum_tensor(f"psb{i}", [128, 1024], BF16)) for i in range(2)]
    psb_rd = {i: (VS, 0) for i in range(2)}

    def psb_produce(i, fn, waits=(), inc=True):
        s_, tok = psb_rd[i]
        w = list(waits)
        if tok > 0:
            w.append((s_, tok))
        return TS.op(fn, waits=w, inc=inc)

    def psb_consume(i, stream, fn, waits=()):
        tok = stream.op(fn, waits=waits)
        psb_rd[i] = (stream, tok)
        return tok
    ps_rd = {i: (VS, 0) for i in range(8)}

    def ps_produce(i, fn, waits=(), inc=True):
        s, tok = ps_rd[i]
        w = list(waits)
        if tok > 0:
            w.append((s, tok))
        return TS.op(fn, waits=w, inc=inc)

    def ps_consume(i, stream, fn, waits=()):
        tok = stream.op(fn, waits=waits)
        ps_rd[i] = (stream, tok)
        return tok

    def pbcast(dst_col, src11, bank, waits=()):
        w = src11.shape[-1]
        t = ps_produce(bank, lambda e: e.matmul(psum[bank][:, 0:w], ones_r[:],
                                                src11, start=True, stop=True),
                       waits=waits)
        return ps_consume(bank, VS, lambda e: e.tensor_copy(dst_col, psum[bank][:, 0:w]),
                          waits=[(TS, t)])

    def xpsum(dst11, src_col, bank, waits=()):
        t = ps_produce(bank, lambda e: e.matmul(psum[bank][0:1, 0:1], src_col,
                                                ones_c[:], start=True, stop=True),
                       waits=waits)
        return ps_consume(bank, VS, lambda e: e.tensor_copy(dst11, psum[bank][0:1, 0:1]),
                          waits=[(TS, t)])

    def emit_consts():
        g1 = GS.op(lambda e: e.iota(causal01[:], [[1, 128]], base=0,
                                    channel_multiplier=-1,
                                    allow_small_or_imprecise_dtypes=True))
        VS.op(lambda e: e.tensor_scalar(causal01[:], causal01[:], 0.0, None,
                                        op0=OP.is_gt), waits=[(GS, g1)])
        g2 = GS.op(lambda e: e.iota(iota64[:], [[1, XCAP]], base=0,
                                    channel_multiplier=0,
                                    allow_small_or_imprecise_dtypes=True))
        g3 = GS.op(lambda e: e.iota(eye_f[:], [[1, 128]], base=0,
                                    channel_multiplier=-1,
                                    allow_small_or_imprecise_dtypes=True))
        VS.op(lambda e: e.tensor_scalar(eye_f[:], eye_f[:], 0.0, None,
                                        op0=OP.is_equal), waits=[(GS, g3)])
        VS.op(lambda e: e.tensor_copy(eye_bf[:], eye_f[:]))
        VS.op(lambda e: e.memset(ones_r[:], 1.0))
        VS.op(lambda e: e.memset(ones_c[:], 1.0))
        VS.op(lambda e: e.memset(thrsb[:], 0.0))
        VS.op(lambda e: e.memset(negbig[:], NEG))
        VS.op(lambda e: e.memset(negbigm[:], -BIGM))
        VS.op(lambda e: e.memset(negbig_b[:], NEG))
        VS.op(lambda e: e.memset(neg80c[:], -80.0))
        _ = g2

    def emit_wprep(w_hbm, wb_hbm, pwb_hbm, n, thr_idx, sub_iters=24, full_iters=26, tag=""):
        width = n // 128
        k = max(1, int(n * 0.05))
        nchunks = max(1, width // 4096)
        cw = width // nchunks
        wv = w_hbm.rearrange("(p a) c -> p (a c)", p=128) if w_hbm.ndim == 2 else w_hbm
        wbv = wb_hbm.rearrange("(p a) c -> p (a c)", p=128) if wb_hbm.ndim == 2 else wb_hbm
        pwbv = pwb_hbm.rearrange("(p a) c -> p (a c)", p=128) if pwb_hbm.ndim == 2 else pwb_hbm

        with nc.sbuf_tensor("aw" + tag, [128, width], F32) as aw, \
             nc.sbuf_tensor("wchunk" + tag, [128, cw], F32) as wchunk, \
             nc.sbuf_tensor("bchunk" + tag, [128, cw], BF16) as bchunk, \
             nc.sbuf_tensor("mscr" + tag, [128, cw], BF16) as mscr, \
             nc.sbuf_tensor("cnt8" + tag, [128, nchunks], F32) as cnt8, \
             nc.sbuf_tensor("tneg" + tag, [128, 1], F32) as tneg, \
             nc.sbuf_tensor("taub" + tag, [128, 1], F32) as taub, \
             nc.sbuf_tensor("mscrb" + tag, [128, 4096], BF16) as mscrb, \
             nc.sbuf_tensor("cnt8b" + tag, [128, 8], F32) as cnt8b, \
             nc.sbuf_tensor("cnt1" + tag, [128, 1], F32) as cnt1, \
             nc.sbuf_tensor("st" + tag, [1, 16], F32) as st, \
             nc.sbuf_tensor("tau" + tag, [128, 1], F32) as tau:

            prev_st = None
            st_toks = []
            sa = 0
            for c in range(nchunks):
                lt = LD.dma(lambda e, c=c: e.dma_start(out=wchunk[:], in_=wv[:, c * cw:(c + 1) * cw]),
                            waits=[(VS, VS.n)] + ([(SS, sa)] if sa else []))
                sa = SS.op(lambda e, c=c: e.activation(aw[:, c * cw:(c + 1) * cw], wchunk[:],
                                                       ACT.Abs), waits=[lt])
                vb = VS.op(lambda e: e.tensor_copy(bchunk[:], wchunk[:]),
                           waits=[lt] + ([prev_st] if prev_st is not None else []))
                prev_st = STQ.dma(lambda e, c=c: e.dma_start(out=wbv[:, c * cw:(c + 1) * cw], in_=bchunk[:]),
                                  waits=[(VS, vb)])
                st_toks.append(prev_st)
            VS.wait([(SS, sa)])

            VS.op(lambda e: e.memset(st[:], 0.0))
            VS.op(lambda e: e.memset(st[0:1, 1:2], 64.0))
            VS.op(lambda e: e.memset(st[0:1, 3:4], 64.0))
            ntotf = float(n)

            def mid_of(lo_i, hi_i, mid_i):
                VS.op(lambda e: e.tensor_tensor(st[0:1, mid_i:mid_i + 1],
                                                st[0:1, lo_i:lo_i + 1],
                                                st[0:1, hi_i:hi_i + 1], op=OP.add))
                VS.op(lambda e: e.tensor_scalar(st[0:1, mid_i:mid_i + 1],
                                                st[0:1, mid_i:mid_i + 1], 0.5,
                                                None, op0=OP.mult))

            def update(lo_i, hi_i, mid_i, cnt_i, s0, s1, t_lo, t_hi):
                VS.op(lambda e, t_lo=t_lo: e.tensor_scalar(
                    st[0:1, s0:s0 + 1], st[0:1, cnt_i:cnt_i + 1], t_lo, None, op0=OP.is_ge))
                VS.op(lambda e: e.tensor_tensor(st[0:1, s1:s1 + 1], st[0:1, mid_i:mid_i + 1],
                                                st[0:1, lo_i:lo_i + 1], op=OP.subtract))
                VS.op(lambda e: e.scalar_tensor_tensor(
                    st[0:1, lo_i:lo_i + 1], st[0:1, s1:s1 + 1], st[0:1, s0:s0 + 1],
                    st[0:1, lo_i:lo_i + 1], op0=OP.mult, op1=OP.add))
                VS.op(lambda e, t_hi=t_hi: e.tensor_scalar(
                    st[0:1, s0:s0 + 1], st[0:1, cnt_i:cnt_i + 1], t_hi, None, op0=OP.is_lt))
                VS.op(lambda e: e.tensor_tensor(st[0:1, s1:s1 + 1], st[0:1, mid_i:mid_i + 1],
                                                st[0:1, hi_i:hi_i + 1], op=OP.subtract))
                VS.op(lambda e: e.scalar_tensor_tensor(
                    st[0:1, hi_i:hi_i + 1], st[0:1, s1:s1 + 1], st[0:1, s0:s0 + 1],
                    st[0:1, hi_i:hi_i + 1], op0=OP.mult, op1=OP.add))

            def sub_bisect(lo_i, hi_i, target, iters):
                step = 64
                aw_sub = aw.rearrange("p (a s) -> p a s", s=step)[:, :, 0]
                for _ in range(iters):
                    mid_of(lo_i, hi_i, 4)
                    pbcast(tau[:], st[0:1, 4:5], 5, waits=[(VS, VS.n)])
                    vtok = VS.op(lambda e, a=aw_sub: e.tensor_scalar(
                        mscr[:, 0:width // step], a, tau[:], 0.0,
                        op0=OP.is_gt, op1=OP.add, accum_out=cnt1[:]))
                    xpsum(st[0:1, 5:6], cnt1[:], 5, waits=[(VS, vtok)])
                    update(lo_i, hi_i, 4, 5, 6, 7, float(target), float(target))

            def joint_full(k_t, k_b, iters):
                for it_ in range(iters):
                    exact_t = it_ >= iters - 5
                    mid_of(0, 1, 4)
                    pbcast(tau[:], st[0:1, 4:5], 5, waits=[(VS, VS.n)])
                    VS.op(lambda e: e.tensor_scalar(tneg[:], tau[:], -1.0, None, op0=OP.mult))
                    mid_of(2, 3, 8)
                    pbcast(taub[:], st[0:1, 8:9], 5, waits=[(VS, VS.n)])
                    if not exact_t:
                        stok = 0
                        for c in range(nchunks):
                            stok = SS.op(lambda e, c=c: e.activation(
                                mscr[:], aw[:, c * cw:(c + 1) * cw], ACT.Sign, bias=tneg[:],
                                accum_out=cnt8[:, c:c + 1]), waits=[(VS, VS.n)])
                        for c in range(nchunks):
                            VS.op(lambda e, c=c: e.tensor_scalar(
                                mscrb[:], aw[:, c * cw:(c + 1) * cw], taub[:], 0.0,
                                op0=OP.is_gt, op1=OP.add, accum_out=cnt8b[:, c:c + 1]))
                        vtop = VS.op(lambda e: e.reduce_sum(cnt1[:], cnt8[:], axis=AX.X),
                                     waits=[(SS, stok)])
                        xpsum(st[0:1, 5:6], cnt1[:], 5, waits=[(VS, vtop)])
                        VS.op(lambda e: e.tensor_scalar(st[0:1, 5:6], st[0:1, 5:6], 0.5,
                                                        0.5 * ntotf, op0=OP.mult, op1=OP.add))
                    else:
                        for c in range(nchunks):
                            VS.op(lambda e, c=c: e.tensor_scalar(
                                mscr[:], aw[:, c * cw:(c + 1) * cw], tau[:], 0.0,
                                op0=OP.is_gt, op1=OP.add, accum_out=cnt8[:, c:c + 1]))
                        vtop = VS.op(lambda e: e.reduce_sum(cnt1[:], cnt8[:], axis=AX.X))
                        xpsum(st[0:1, 5:6], cnt1[:], 5, waits=[(VS, vtop)])
                        for c in range(nchunks):
                            VS.op(lambda e, c=c: e.tensor_scalar(
                                mscrb[:], aw[:, c * cw:(c + 1) * cw], taub[:], 0.0,
                                op0=OP.is_gt, op1=OP.add, accum_out=cnt8b[:, c:c + 1]))
                    vbot = VS.op(lambda e: e.reduce_sum(cnt1[:], cnt8b[:], axis=AX.X))
                    xpsum(st[0:1, 9:10], cnt1[:], 5, waits=[(VS, vbot)])
                    dz = 0.0 if exact_t else 0.6
                    update(0, 1, 4, 5, 6, 7, k_t + dz, k_t - dz)
                    update(2, 3, 8, 9, 10, 11, float(k_b), float(k_b))

            ssub = 64 if width >= 4096 else 1
            kb = n - k + 1
            if ssub > 1:
                sub_m = 10.0 * math.sqrt(max(k / ssub, 1.0))
                sub_bisect(0, 1, k / ssub + sub_m, sub_iters)
                sub_bisect(2, 3, max(k / ssub - sub_m, 0.5), sub_iters)
                VS.op(lambda e: e.tensor_copy(st[0:1, 1:2], st[0:1, 3:4]))
                VS.op(lambda e: e.memset(st[0:1, 2:3], 0.0))
                VS.op(lambda e: e.memset(st[0:1, 3:4], 64.0))
                sub_bisect(2, 3, kb / ssub + sub_m, sub_iters)
                VS.op(lambda e: e.memset(st[0:1, 8:9], 0.0))
                VS.op(lambda e: e.memset(st[0:1, 9:10], 64.0))
                sub_bisect(8, 9, max(kb / ssub - sub_m, 0.5), sub_iters)
                VS.op(lambda e: e.tensor_copy(st[0:1, 3:4], st[0:1, 9:10]))
                joint_full(float(k), float(kb), full_iters)
            else:
                joint_full(float(k), float(kb), 40)
            VS.op(lambda e, ti=thr_idx: e.tensor_copy(thrsb[0:1, 2 * ti:2 * ti + 1],
                                                      st[0:1, 0:1]))
            VS.op(lambda e, ti=thr_idx: e.tensor_copy(thrsb[0:1, 2 * ti + 1:2 * ti + 2],
                                                      st[0:1, 3:4]))
            pbcast(thrsb[:, 2 * thr_idx:2 * thr_idx + 2],
                   thrsb[0:1, 2 * thr_idx:2 * thr_idx + 2], 5, waits=[(VS, VS.n)])
            prev_st = None
            for c in range(nchunks):
                lt = LD.dma(lambda e, c=c: e.dma_start(out=bchunk[:], in_=wbv[:, c * cw:(c + 1) * cw]),
                            waits=[(VS, VS.n), st_toks[c], st_toks[-1]]
                                  + ([prev_st] if prev_st is not None else []))
                VS.op(lambda e, c=c: e.tensor_scalar(wchunk[:], aw[:, c * cw:(c + 1) * cw],
                                                     thrsb[:, 2 * thr_idx:2 * thr_idx + 1],
                                                     None, op0=OP.is_gt),
                      waits=[lt])
                VS.op(lambda e, c=c: e.scalar_tensor_tensor(
                    wchunk[:], aw[:, c * cw:(c + 1) * cw],
                    thrsb[:, 2 * thr_idx + 1:2 * thr_idx + 2], wchunk[:],
                    op0=OP.is_le, op1=OP.subtract))
                VS.op(lambda e: e.tensor_scalar(wchunk[:], wchunk[:], 0.8, None, op0=OP.mult))
                vp = VS.op(lambda e: e.scalar_tensor_tensor(mscr[:], bchunk[:], 1.0,
                                                            wchunk[:], op0=OP.mult,
                                                            op1=OP.mult))
                vp = VS.op(lambda e: e.tensor_tensor(mscr[:], mscr[:], bchunk[:], op=OP.add))
                prev_st = STQ.dma(lambda e, c=c: e.dma_start(out=pwbv[:, c * cw:(c + 1) * cw], in_=mscr[:]),
                                  waits=[(VS, vp)])
        barrier()

    def emit_bias_prep():
        with nc.sbuf_tensor("b1a", [128, 128], F32) as b1a, \
             nc.sbuf_tensor("ms", [128, NF], F32) as ms, \
             nc.sbuf_tensor("cnt1", [128, 1], F32) as cnt1, \
             nc.sbuf_tensor("st", [1, 8], F32) as st, \
             nc.sbuf_tensor("tau", [128, 1], F32) as tau, \
             nc.sbuf_tensor("b2r", [1, D], F32) as b2r, \
             nc.sbuf_tensor("pb2r", [1, D], F32) as pb2r, \
             nc.sbuf_tensor("lbr", [1, D], F32) as lbr:

            lt = LD.dma(lambda e: e.dma_start(
                out=b1a[0:NF, 0:128].rearrange("a b -> a b"),
                in_=B1.rearrange("(f p) -> f p", p=128)))
            tb1 = ps_produce(4, lambda e: e.transpose(
                psum[4][:, 0:NF], b1a[0:NF, 0:128], eye_f[0:NF, 0:NF]),
                waits=[lt])
            ps_consume(4, VS, lambda e: e.tensor_copy(b1fm[:], psum[4][:, 0:NF]),
                       waits=[(TS, tb1)])
            lt2 = LD.dma(lambda e: e.dma_start(
                out=b1a[0:NF, 0:128].rearrange("a b -> a b"),
                in_=Lb1.rearrange("(f p) -> f p", p=128)),
                waits=[(VS, VS.n)])
            tb2 = ps_produce(4, lambda e: e.transpose(
                psum[4][:, 0:NF], b1a[0:NF, 0:128], eye_f[0:NF, 0:NF]),
                waits=[lt2])
            ps_consume(4, VS, lambda e: e.tensor_copy(lb1fm[:], psum[4][:, 0:NF]),
                       waits=[(TS, tb2)])

            def small_select(src, n, iters=40):
                pass

            k1 = max(1, int(FF * 0.05))
            va = SS.op(lambda e: e.activation(b1a[:, 0:NF], b1fm[:], ACT.Abs),
                       waits=[(VS, VS.n)])
            VS.wait([(SS, va)])
            VS.op(lambda e: e.memset(st[:], 0.0))
            VS.op(lambda e: e.memset(st[0:1, 1:2], 64.0))
            VS.op(lambda e: e.memset(st[0:1, 3:4], 64.0))

            def bis(lo_i, hi_i, target, iters, src, w):
                for _ in range(iters):
                    VS.op(lambda e: e.tensor_tensor(st[0:1, 4:5], st[0:1, lo_i:lo_i + 1],
                                                    st[0:1, hi_i:hi_i + 1], op=OP.add))
                    VS.op(lambda e: e.tensor_scalar(st[0:1, 4:5], st[0:1, 4:5], 0.5,
                                                    None, op0=OP.mult))
                    pbcast(tau[:], st[0:1, 4:5], 5, waits=[(VS, VS.n)])
                    vtok = VS.op(lambda e: e.tensor_scalar(ms[:, 0:w], src[:, 0:w], tau[:],
                                                           0.0, op0=OP.is_gt, op1=OP.add,
                                                           accum_out=cnt1[:]))
                    xpsum(st[0:1, 5:6], cnt1[:], 5, waits=[(VS, vtok)])
                    VS.op(lambda e: e.tensor_scalar(st[0:1, 6:7], st[0:1, 5:6],
                                                    float(target), None, op0=OP.is_ge))
                    VS.op(lambda e: e.tensor_tensor(st[0:1, 7:8], st[0:1, 4:5],
                                                    st[0:1, lo_i:lo_i + 1], op=OP.subtract))
                    VS.op(lambda e: e.scalar_tensor_tensor(
                        st[0:1, lo_i:lo_i + 1], st[0:1, 7:8], st[0:1, 6:7],
                        st[0:1, lo_i:lo_i + 1], op0=OP.mult, op1=OP.add))
                    VS.op(lambda e: e.tensor_tensor(st[0:1, 7:8], st[0:1, 4:5],
                                                    st[0:1, hi_i:hi_i + 1], op=OP.subtract))
                    VS.op(lambda e: e.tensor_scalar(st[0:1, 6:7], st[0:1, 6:7], -1.0, 1.0,
                                                    op0=OP.mult, op1=OP.add))
                    VS.op(lambda e: e.scalar_tensor_tensor(
                        st[0:1, hi_i:hi_i + 1], st[0:1, 7:8], st[0:1, 6:7],
                        st[0:1, hi_i:hi_i + 1], op0=OP.mult, op1=OP.add))

            bis(0, 1, k1, 40, b1a, NF)
            bis(2, 3, FF - k1 + 1, 40, b1a, NF)
            b1a_n = b1a[:, 0:NF]
            pbcast(tau[:], st[0:1, 0:1], 5, waits=[(VS, VS.n)])
            VS.op(lambda e: e.tensor_scalar(ms[:], b1a[:, 0:NF], tau[:], None, op0=OP.is_gt))
            pbcast(tau[:], st[0:1, 3:4], 5, waits=[(VS, VS.n)])
            VS.op(lambda e: e.scalar_tensor_tensor(ms[:], b1a[:, 0:NF], tau[:], ms[:],
                                                   op0=OP.is_le, op1=OP.subtract))
            VS.op(lambda e: e.tensor_scalar(ms[:], ms[:], 0.8, None, op0=OP.mult))
            VS.op(lambda e: e.tensor_mul(ms[:], ms[:], b1fm[:]))
            VS.op(lambda e: e.tensor_add(pb1fm[:], ms[:], b1fm[:]))

            lt3 = LD.dma(lambda e: e.dma_start(out=b2r[:], in_=B2.rearrange("(o d) -> o d", o=1)))
            k2 = max(1, int(D * 0.05))
            va2 = SS.op(lambda e: e.activation(pb2r[:], b2r[:], ACT.Abs),
                        waits=[lt3, (VS, VS.n)])
            VS.wait([(SS, va2)])
            _ = (va, va2)
            VS.op(lambda e: e.memset(st[:], 0.0))
            VS.op(lambda e: e.memset(st[0:1, 1:2], 64.0))
            VS.op(lambda e: e.memset(st[0:1, 3:4], 64.0))

            def bis1(lo_i, hi_i, target, iters):
                for _ in range(iters):
                    VS.op(lambda e: e.tensor_tensor(st[0:1, 4:5], st[0:1, lo_i:lo_i + 1],
                                                    st[0:1, hi_i:hi_i + 1], op=OP.add))
                    VS.op(lambda e: e.tensor_scalar(st[0:1, 4:5], st[0:1, 4:5], 0.5,
                                                    None, op0=OP.mult))
                    VS.op(lambda e: e.tensor_scalar(lbr[:], pb2r[:], st[0:1, 4:5], 0.0,
                                                    op0=OP.is_gt, op1=OP.add,
                                                    accum_out=st[0:1, 5:6]))
                    VS.op(lambda e: e.tensor_scalar(st[0:1, 6:7], st[0:1, 5:6],
                                                    float(target), None, op0=OP.is_ge))
                    VS.op(lambda e: e.tensor_tensor(st[0:1, 7:8], st[0:1, 4:5],
                                                    st[0:1, lo_i:lo_i + 1], op=OP.subtract))
                    VS.op(lambda e: e.scalar_tensor_tensor(
                        st[0:1, lo_i:lo_i + 1], st[0:1, 7:8], st[0:1, 6:7],
                        st[0:1, lo_i:lo_i + 1], op0=OP.mult, op1=OP.add))
                    VS.op(lambda e: e.tensor_tensor(st[0:1, 7:8], st[0:1, 4:5],
                                                    st[0:1, hi_i:hi_i + 1], op=OP.subtract))
                    VS.op(lambda e: e.tensor_scalar(st[0:1, 6:7], st[0:1, 6:7], -1.0, 1.0,
                                                    op0=OP.mult, op1=OP.add))
                    VS.op(lambda e: e.scalar_tensor_tensor(
                        st[0:1, hi_i:hi_i + 1], st[0:1, 7:8], st[0:1, 6:7],
                        st[0:1, hi_i:hi_i + 1], op0=OP.mult, op1=OP.add))

            bis1(0, 1, k2, 40)
            bis1(2, 3, D - k2 + 1, 40)
            VS.op(lambda e: e.tensor_scalar(lbr[:], pb2r[:], st[0:1, 0:1], None, op0=OP.is_gt))
            VS.op(lambda e: e.scalar_tensor_tensor(lbr[:], pb2r[:], st[0:1, 3:4], lbr[:],
                                                   op0=OP.is_le, op1=OP.subtract))
            VS.op(lambda e: e.tensor_scalar(lbr[:], lbr[:], 0.8, None, op0=OP.mult))
            VS.op(lambda e: e.tensor_mul(lbr[:], lbr[:], b2r[:]))
            VS.op(lambda e: e.tensor_add(pb2r[:], lbr[:], b2r[:]))
            pbcast(cvec_bc[:, 0:512], b2r[0:1, 0:512], 5, waits=[(VS, VS.n)])
            pbcast(cvec_bc[:, 512:1024], b2r[0:1, 512:1024], 5)
            VS.op(lambda e: e.tensor_sub(pb2r[:], pb2r[:], b2r[:]))
            pbcast(dvec_bc[:, 0:512], pb2r[0:1, 0:512], 5, waits=[(VS, VS.n)])
            pbcast(dvec_bc[:, 512:1024], pb2r[0:1, 512:1024], 5)
            lt4 = LD.dma(lambda e: e.dma_start(out=b2r[:], in_=projB.rearrange("(o d) -> o d", o=1)),
                         waits=[(VS, VS.n), (TS, TS.n)])
            lt5 = LD.dma(lambda e: e.dma_start(out=pb2r[:], in_=Lb2.rearrange("(o d) -> o d", o=1)),
                         waits=[(VS, VS.n), (TS, TS.n)])
            vv = VS.op(lambda e: e.tensor_add(lbr[:], b2r[:], pb2r[:]),
                       waits=[lt4, lt5])
            pbcast(lbias_bc[:, 0:512], lbr[0:1, 0:512], 5, waits=[(VS, vv)])
            pbcast(lbias_bc[:, 512:1024], lbr[0:1, 512:1024], 5)
            _ = lt4
            if "thr" in debug:
                d = dbg_out("thr", [128, 8])
                STQ.dma(lambda e, d=d: e.dma_start(out=d, in_=thrsb[:]), waits=[(VS, VS.n)])
            if "biases" in debug:
                d = dbg_out("biases", [128, 2 * NF + 2 * D])
                vv2 = VS.op(lambda e: e.engine_nop(), inc=True)
                STQ.dma(lambda e, d=d: e.dma_start(out=d[:, 0:NF], in_=b1fm[:]), waits=[(VS, vv2)])
                STQ.dma(lambda e, d=d: e.dma_start(out=d[:, NF:2 * NF], in_=pb1fm[:]))
                STQ.dma(lambda e, d=d: e.dma_start(out=d[:, 2 * NF:2 * NF + D], in_=cvec_bc[:]))
                STQ.dma(lambda e, d=d: e.dma_start(out=d[:, 2 * NF + D:2 * NF + 2 * D], in_=dvec_bc[:]))
        barrier()

    def emit_pre(xnfm):
        with nc.sbuf_tensor("xt", [128, D], F32) as xt, \
             nc.sbuf_tensor("gam_bc", [128, D], F32) as gam_bc, \
             nc.sbuf_tensor("bet_bc", [128, D], F32) as bet_bc, \
             nc.sbuf_tensor("xc", [128, D], F32) as xc, \
             nc.sbuf_tensor("sqs", [128, D], F32) as sqs, \
             nc.sbuf_tensor("hfm", [128, D], BF16) as hfm, \
             nc.sbuf_tensor("mu", [128, 1], F32) as mu, \
             nc.sbuf_tensor("musq", [128, 1], F32) as musq, \
             nc.sbuf_tensor("ssq", [128, 1], F32) as ssq, \
             nc.sbuf_tensor("veps", [128, 1], F32) as veps, \
             nc.sbuf_tensor("rstd", [128, 1], F32) as rstd, \
             nc.sbuf_tensor("f2", [128, 1], F32) as f2, \
             nc.sbuf_tensor("sc8", [128, 8], F32) as sc8, \
             nc.sbuf_tensor("wrsb", [128, NP, 8], BF16) as wrsb, \
             nc.sbuf_tensor("grow", [1, D], F32) as grow, \
             nc.sbuf_tensor("brow", [1, D], F32) as brow, \
             nc.sbuf_tensor("brr", [1, 8], F32) as brr, \
             nc.sbuf_tensor("brbc", [128, 8], F32) as brbc:

            tg = LD.dma(lambda e: e.dma_start(out=grow[:], in_=LNG.rearrange("(o d) -> o d", o=1)))
            tb = LD.dma(lambda e: e.dma_start(out=brow[:], in_=LNB.rearrange("(o d) -> o d", o=1)))
            tr = LD.dma(lambda e: e.dma_start(out=brr[:], in_=BR.rearrange("(o d) -> o d", o=1)))
            tw = LDG.dma(lambda e: e.dma_start(out=wrsb[:], in_=WR.rearrange("(j p) e -> p j e", p=128)))
            pbcast(gam_bc[:, 0:512], grow[0:1, 0:512], 5, waits=[tg])
            pbcast(gam_bc[:, 512:1024], grow[0:1, 512:1024], 5)
            pbcast(bet_bc[:, 0:512], brow[0:1, 0:512], 5, waits=[tb])
            pbcast(bet_bc[:, 512:1024], brow[0:1, 512:1024], 5)
            pbcast(brbc[:], brr[0:1, :], 5, waits=[tr])

            sq_prev = 0
            tposx_prev = 0
            tposh_prev = 0
            for i in range(NT):
                lx = LD.dma(lambda e, i=i: e.dma_start(out=xt[:], in_=x0[i * 128:(i + 1) * 128, :]),
                            waits=[(VS, VS.n)] + ([(SS, sq_prev)] if sq_prev else []))
                VS.op(lambda e: e.reduce_sum(mu[:], xt[:], axis=AX.X), waits=[lx])
                VS.op(lambda e: e.tensor_scalar(mu[:], mu[:], 1.0 / D, None, op0=OP.mult))
                sq = SS.op(lambda e: e.activation(sqs[:], xt[:], ACT.Square, accum_out=ssq[:]),
                           waits=[lx, (VS, VS.n)])
                sq_prev = sq
                VS.op(lambda e: e.tensor_mul(musq[:], mu[:], mu[:]), waits=[(SS, sq)])
                VS.op(lambda e: e.scalar_tensor_tensor(veps[:], ssq[:], 1.0 / D, musq[:],
                                                       op0=OP.mult, op1=OP.subtract))
                VS.op(lambda e: e.tensor_scalar(veps[:], veps[:], EPS, None, op0=OP.add))
                s2 = SS.op(lambda e: e.activation(rstd[:], veps[:], ACT.Sqrt),
                           waits=[(VS, VS.n)])
                VS.op(lambda e: e.reciprocal(rstd[:], rstd[:]), waits=[(SS, s2)])
                vb = VS.op(lambda e: e.tensor_scalar(xc[:], xt[:], mu[:], rstd[:],
                                                     op0=OP.subtract, op1=OP.mult),
                           waits=([(TS, tposh_prev)] if tposh_prev else []))
                for j in range(NP):
                    bk = j % 2
                    t = ps_produce(bk, lambda e, j=j, bk=bk: e.transpose(
                        psum[bk][:, 0:128], xc[:, j * 128:(j + 1) * 128], eye_f[:]),
                        waits=[(VS, vb)])
                    ps_consume(bk, VS, lambda e, j=j, i=i, bk=bk: e.tensor_copy(
                        xnfm[:, j, i * 128:(i + 1) * 128], psum[bk][:, 0:128]),
                        waits=[(TS, t)])
                    tposx_prev = t
                VS.op(lambda e: e.reciprocal(f2[:], veps[:]))
                VS.op(lambda e: e.tensor_scalar(f2[:], f2[:], -EPS, 1.0,
                                                op0=OP.mult, op1=OP.add))
                VS.op(lambda e: e.tensor_scalar(f2[:], f2[:], EPS, None, op0=OP.add))
                s3 = SS.op(lambda e: e.activation(f2[:], f2[:], ACT.Sqrt), waits=[(VS, VS.n)])
                VS.op(lambda e: e.reciprocal(f2[:], f2[:]), waits=[(SS, s3)])
                VS.op(lambda e: e.scalar_tensor_tensor(sqs[:], xc[:], f2[:], gam_bc[:],
                                                       op0=OP.mult, op1=OP.mult))
                vh = VS.op(lambda e: e.tensor_tensor(xc[:], sqs[:], bet_bc[:], op=OP.add),
                           waits=[(TS, tposx_prev)])
                for j in range(NP):
                    bk = 2 + (j % 2)
                    t = ps_produce(bk, lambda e, j=j, bk=bk: e.transpose(
                        psum[bk][:, 0:128], xc[:, j * 128:(j + 1) * 128], eye_f[:]),
                        waits=[(VS, vh)])
                    ps_consume(bk, VS, lambda e, j=j, bk=bk: e.tensor_copy(
                        hfm[:, j * 128:(j + 1) * 128], psum[bk][:, 0:128]),
                        waits=[(TS, t)])
                    tposh_prev = t
                vhf = VS.n
                s_rd = ps_rd[4]
                for j in range(NP):
                    w = ([(VS, vhf), tw] if j == 0 else [])
                    if j == 0 and s_rd[1] > 0:
                        w.append(s_rd)
                    TS.op(lambda e, j=j: e.matmul(
                        psum[4][:, 0:8], hfm[:, j * 128:(j + 1) * 128], wrsb[:, j, :],
                        start=(j == 0), stop=(j == NP - 1)),
                        waits=w, inc=(j == NP - 1))
                tt = TS.n
                VS.op(lambda e: e.tensor_add(sc8[:], psum[4][:, 0:8], brbc[:]),
                      waits=[(TS, tt)])
                ps_rd[4] = (VS, VS.n)
                VS.op(lambda e: e.tensor_scalar(sc8[:], sc8[:], sc8[:, 0:1], None,
                                                op0=OP.subtract))
                s4 = SS.op(lambda e: e.activation(sc8[:], sc8[:], ACT.Exp, accum_out=f2[:]),
                           waits=[(VS, VS.n)])
                VS.op(lambda e, i=i: e.reciprocal(p0b[:, i:i + 1], f2[:]), waits=[(SS, s4)])

            if "xnfm" in debug:
                d = dbg_out("xnfm", [128, NP * T], BF16)
                STQ.dma(lambda e, d=d: e.dma_start(
                    out=d.rearrange("p (j t) -> p j t", j=NP), in_=xnfm[:]),
                    waits=[(VS, VS.n)])
            if "p0" in debug:
                d = dbg_out("p0", [128, NT])
                STQ.dma(lambda e, d=d: e.dma_start(out=d, in_=p0b[:]), waits=[(VS, VS.n)])
        barrier()


    def emit_attn(ffnA_spec=None):
        from contextlib import ExitStack
        actx = ExitStack()
        A = actx.enter_context
        qfm = A(nc.sbuf_tensor("qfm", [128, NP, T], BF16))
        kfm = A(nc.sbuf_tensor("kfm", [128, NP, T], BF16))
        vtm = A(nc.sbuf_tensor("vtm", [128, NT, D], BF16))
        if True:

            with nc.sbuf_tensor("wprj", [128, NP, D], BF16) as wprj, \
                 nc.sbuf_tensor("ptile", [128, 512], F32) as ptile, \
                 nc.sbuf_tensor("pabs", [128, 512], F32) as pabs, \
                 nc.sbuf_tensor("m8", [128, 8, 8], F32) as m8:
                for (wT_src, dstfm) in ((wqT, qfm), (wkT, kfm), (wvT, None)):
                    lw = LDG.dma(lambda e, wT_src=wT_src: e.dma_start(
                        out=wprj[:], in_=wT_src.rearrange("(j p) d -> p j d", p=128)),
                        waits=[(TS, TS.n)])
                    wsb = wprj
                    for i in range(NT):
                        for n in range(2):
                            bk = n % 2
                            for kk in range(NP):
                                w = [lw] if kk == 0 else []
                                if kk == 0:
                                    sprev, ptok = ps_rd[bk]
                                    if ptok > 0:
                                        w.append((sprev, ptok))
                                TS.op(lambda e, i=i, n=n, kk=kk, bk=bk, wsb=wsb: e.matmul(
                                    psum[bk][:], xnfm[:, kk, i * 128:(i + 1) * 128],
                                    wsb[:, kk, n * 512:(n + 1) * 512],
                                    start=(kk == 0), stop=(kk == NP - 1)),
                                    waits=w, inc=(kk == NP - 1))
                            tmm = TS.n
                            evp = ps_consume(bk, VS, lambda e, bk=bk: e.tensor_copy(ptile[:], psum[bk][:]),
                                             waits=[(TS, tmm)])
                            sab = SS.op(lambda e: e.activation(pabs[:], ptile[:], ACT.Abs),
                                        waits=[(VS, evp)])
                            VS.wait([(SS, sab)])
                            for hh in range(8):
                                VS.op(lambda e, hh=hh: e.max(out=m8[:, hh, :], in_=pabs[:, hh * 64:(hh + 1) * 64]))
                            p3 = pabs.rearrange("p (h d) -> p h d", h=8)
                            VS.op(lambda e, p3=p3: e.tensor_tensor(
                                p3, p3, m8[:, :, 1:2].to_broadcast([128, 8, 64]), op=OP.is_ge))
                            if dstfm is None:
                                VS.op(lambda e, i=i, n=n: e.tensor_tensor(
                                    vtm[:, i, n * 512:(n + 1) * 512], pabs[:], ptile[:], op=OP.mult))
                            else:
                                vsp = VS.op(lambda e: e.tensor_mul(ptile[:], pabs[:], ptile[:]))
                                for jj in range(4):
                                    tb = ps_produce(2 + jj % 2, lambda e, jj=jj: e.transpose(
                                        psum[2 + jj % 2][:, 0:128],
                                        ptile[:, jj * 128:(jj + 1) * 128], eye_f[:]),
                                        waits=[(VS, vsp)])
                                    ps_consume(2 + jj % 2, VS,
                                               lambda e, i=i, n=n, jj=jj, dstfm=dstfm: e.tensor_copy(
                                                   dstfm[:, n * 4 + jj, i * 128:(i + 1) * 128],
                                                   psum[2 + jj % 2][:, 0:128]),
                                               waits=[(TS, tb)])
            barrier()

            scP = [A(nc.sbuf_tensor(f"scp{p}", [128, T], BF16)) for p in range(2)]
            zmP = [A(nc.sbuf_tensor(f"zmp{p}", [128, T], BF16)) for p in range(2)]
            pmP = [A(nc.sbuf_tensor(f"pmp{p}", [128, T], mybir.dt.uint8)) for p in range(2)]
            czP = [A(nc.sbuf_tensor(f"czp{p}", [128, T], F32)) for p in range(2)]
            wbpP = [A(nc.sbuf_tensor(f"wbpp{p}", [128, T], BF16)) for p in range(2)]
            wtbP = [A(nc.sbuf_tensor(f"wtbp{p}", [128, T], BF16)) for p in range(2)]
            x8P = [A(nc.sbuf_tensor(f"x8p{p}", [128, XCAP], F32)) for p in range(2)]
            colP = [A(nc.sbuf_tensor(f"colp{p}", [128, 8], F32)) for p in range(2)]
            otileP = [A(nc.sbuf_tensor(f"otilep{p}", [128, 64], BF16)) for p in range(2)]
            ofmP = [A(nc.sbuf_tensor(f"ofmp{p}", [64, 128], BF16)) for p in range(2)]
            fmm = fact = None
            n_units = 0
            if ffnA_spec is not None:
                fmm, fact, n_units = make_ffnA(A, *ffnA_spec, banks=(2, 3),
                                               raw_spill=True)
            bands = [(h, qt) for h in range(H) for qt in range(NT)]
            fin = {0: None, 1: None}
            state = {}

            def front(i):
                h, qt = bands[i]
                par = i % 2
                sc = scP[par]
                pt, po = h // 2, (h % 2) * 64
                Wk = (qt + 1) * 128
                nch = (Wk + 511) // 512
                qsl = qfm[po:po + 64, pt, qt * 128:(qt + 1) * 128]
                f = fin[par]
                evtoks = []
                for cch in range(nch):
                    cw_ = min(512, Wk - cch * 512)
                    bk = cch % 2
                    sprev, ptok = ps_rd[bk]
                    w = [(sprev, ptok)] if ptok > 0 else []
                    TS.op(lambda e, cch=cch, cw_=cw_, bk=bk, qsl=qsl, pt=pt, po=po: e.matmul(
                        psum[bk][:, 0:cw_], qsl,
                        kfm[po:po + 64, pt, cch * 512:cch * 512 + cw_],
                        start=True, stop=True), waits=w, inc=True)
                    tmm = TS.n
                    ww = [(TS, tmm)]
                    if f is not None and cch == 0:
                        ww.append((VS, f["vs_sc"]))
                    ev = SS.op(lambda e, cch=cch, cw_=cw_, bk=bk, sc=sc: e.activation(
                        sc[:, cch * 512:cch * 512 + cw_], psum[bk][:, 0:cw_],
                        ACT.Identity, scale=INVSQ), waits=ww)
                    ps_rd[bk] = (SS, ev)
                    evtoks.append(ev)
                ctok = VS.op(lambda e, qt=qt, sc=sc: e.scalar_tensor_tensor(
                    sc[:, qt * 128:qt * 128 + 128], causal01[:], NEG,
                    sc[:, qt * 128:qt * 128 + 128], op0=OP.mult, op1=OP.add),
                    waits=[(SS, evtoks[-1])])
                sgn = None
                if qt > 0:
                    scol = colP[par][:, 7:8]
                    jk = wtbP[par]
                    sw = [(VS, ctok)]
                    if f is not None:
                        sw.append((TS, f["tav"]))
                    sgn = SS.op(lambda e, Wk=Wk, sc=sc, scol=scol, jk=jk: e.activation(
                        jk[:, 0:Wk], sc[:, 0:Wk], ACT.Sign, accum_out=scol),
                        waits=sw)
                return {"evlast": evtoks[-1], "ctok": ctok, "sgn": sgn}

            last_spill = {0: None, 1: None}

            def epilogue(f2):
                par, h, qt = f2["par"], f2["h"], f2["qt"]
                pt, po = h // 2, (h % 2) * 64
                Dcol = colP[par][:, 4:5]
                rDcol = colP[par][:, 5:6]
                otile, ofm = otileP[par], ofmP[par]
                avbank = 4 + par
                VS.op(lambda e, Dcol=Dcol, rDcol=rDcol: e.tensor_scalar_max(
                    rDcol, Dcol, 1e-30), waits=[(SS, f2["sexp"])])
                VS.op(lambda e, rDcol=rDcol: e.reciprocal(rDcol, rDcol))
                ov = ps_consume(avbank, VS, lambda e, otile=otile, rDcol=rDcol, avbank=avbank: e.tensor_scalar(
                    otile[:], psum[avbank][:, 0:64], rDcol, None, op0=OP.mult),
                    waits=[(TS, f2["tav"])])
                obb = f2["obb"]
                tb = psb_produce(obb, lambda e, otile=otile, obb=obb: e.transpose(
                    psumb[obb][0:64, 0:128], otile[:, 0:64], eye_bf[:]),
                    waits=[(VS, ov)])
                ow = [(TS, tb)]
                if last_spill[par] is not None:
                    ow.append(last_spill[par])
                oev = psb_consume(obb, SS, lambda e, ofm=ofm, obb=obb: e.activation(
                    ofm[:], psumb[obb][0:64, 0:128], ACT.Identity), waits=ow)
                last_spill[par] = STQ.dma(lambda e, pt=pt, po=po, qt=qt, ofm=ofm: e.dma_start(
                    out=s_attnfm[pt, po:po + 64, qt * 128:(qt + 1) * 128],
                    in_=ofm[:]), waits=[(SS, oev)])

            def back(i, st_):
                h, qt = bands[i]
                par = i % 2
                f = fin[par]
                if f is not None:
                    epilogue(f)
                sc, zm, pm, cz = scP[par], zmP[par], pmP[par], czP[par]
                wbp, wtb, x8 = wbpP[par], wtbP[par], x8P[par]
                pcol = colP[par][:, 0:1]
                mcol = colP[par][:, 1:2]
                dcol = colP[par][:, 2:3]
                taucol = colP[par][:, 3:4]
                Dcol = colP[par][:, 4:5]
                pt, po = h // 2, (h % 2) * 64
                Wk = (qt + 1) * 128
                R = EXTR[qt]
                ctok = st_["ctok"]
                sgn = st_["sgn"]
                wT = [(TS, f["ts_wT"])] if f is not None else []
                if qt == 0:
                    VS.op(lambda e, sc=sc, wbp=wbp: e.tensor_copy(wbp[:, 0:128], sc[:, 0:128]),
                          waits=wT)
                    for r in range(8):
                        VS.op(lambda e, r=r, wbp=wbp, x8=x8: e.max(
                            out=x8[:, r * 8:r * 8 + 8], in_=wbp[:, 0:128]))
                        VS.op(lambda e, r=r, wbp=wbp, x8=x8: e.match_replace(
                            out=wbp[:, 0:128], in_to_replace=x8[:, r * 8:r * 8 + 8],
                            in_values=wbp[:, 0:128], imm_value=NEG))
                    VS.op(lambda e, sc=sc, wbp=wbp, zm=zm: e.tensor_tensor(
                        zm[:, 0:128], sc[:, 0:128], wbp[:, 0:128], op=OP.not_equal))
                else:
                    zcol = colP[par][:, 6:7]
                    scol = colP[par][:, 7:8]
                    VS.op(lambda e, Wk=Wk, sc=sc, zm=zm, zcol=zcol: e.tensor_scalar(
                        zm[:, 0:Wk], sc[:, 0:Wk], 0.0, 0.0, op0=OP.is_equal,
                        op1=OP.add, accum_out=zcol))
                    VS.op(lambda e, Wk=Wk, zm=zm, cz=cz: e.tensor_tensor_scan(
                        cz[:, 0:Wk], zm[:, 0:Wk], zm[:, 0:Wk], 0.0,
                        op0=OP.add, op1=OP.bypass),
                        waits=[(SS, f["sexp"])] if f is not None else [])
                    VS.op(lambda e, pcol=pcol, scol=scol, zcol=zcol: e.tensor_sub(
                        pcol, scol, zcol), waits=[(SS, sgn)])
                    VS.op(lambda e, Wk=Wk, pcol=pcol: e.tensor_scalar(
                        pcol, pcol, 0.5, 0.5 * Wk, op0=OP.mult, op1=OP.add))
                    VS.op(lambda e, pcol=pcol, mcol=mcol: e.tensor_scalar(
                        mcol, pcol, -1.0, 64.0, op0=OP.mult, op1=OP.add))
                    VS.op(lambda e, Wk=Wk, zm=zm, cz=cz, mcol=mcol: e.scalar_tensor_tensor(
                        zm[:, 0:Wk], cz[:, 0:Wk], mcol, zm[:, 0:Wk],
                        op0=OP.is_le, op1=OP.mult))
                    if R > 0:
                        VS.op(lambda e, Wk=Wk, sc=sc, pm=pm: e.tensor_scalar(
                            pm[:, 0:Wk], sc[:, 0:Wk], 0.0, None, op0=OP.is_le))
                        sneg = SS.op(lambda e, Wk=Wk, sc=sc, wbp=wbp: e.activation(
                            wbp[:, 0:Wk], sc[:, 0:Wk], ACT.Identity, scale=-1.0),
                            waits=[(VS, VS.n)] + wT)
                        VS.op(lambda e, Wk=Wk, wbp=wbp, pm=pm: e.copy_predicated(
                            wbp[:, 0:Wk], pm[:, 0:Wk], negbig_b[:].to_broadcast([128, Wk])),
                            waits=[(SS, sneg)])
                        for r in range(R):
                            VS.op(lambda e, r=r, Wk=Wk, wbp=wbp, x8=x8: e.max(
                                out=x8[:, r * 8:r * 8 + 8], in_=wbp[:, 0:Wk]))
                            VS.op(lambda e, r=r, Wk=Wk, wbp=wbp, x8=x8: e.match_replace(
                                out=wbp[:, 0:Wk], in_to_replace=x8[:, r * 8:r * 8 + 8],
                                in_values=wbp[:, 0:Wk], imm_value=NEG))
                        VS.op(lambda e, pcol=pcol, dcol=dcol: e.tensor_scalar(
                            dcol, pcol, -64.0, None, op0=OP.add))
                        VS.op(lambda e, dcol=dcol: e.tensor_scalar_max(dcol, dcol, 0.0))
                        VS.op(lambda e, R=R, dcol=dcol: e.tensor_scalar_min(
                            dcol, dcol, float(8 * R - 1)))
                        VS.op(lambda e, R=R, x8=x8, dcol=dcol, taucol=taucol: e.scalar_tensor_tensor(
                            x8[:, 0:8 * R], iota64[:, 0:8 * R], dcol, x8[:, 0:8 * R],
                            op0=OP.is_equal, op1=OP.mult, accum_out=taucol))
                        VS.op(lambda e, taucol=taucol: e.tensor_scalar(
                            taucol, taucol, -1.0, None, op0=OP.mult))
                        VS.op(lambda e, taucol=taucol: e.tensor_scalar_max(taucol, taucol, TINY))
                    else:
                        VS.op(lambda e, taucol=taucol: e.memset(taucol, TINY))
                    VS.op(lambda e, Wk=Wk, sc=sc, zm=zm, taucol=taucol: e.scalar_tensor_tensor(
                        zm[:, 0:Wk], sc[:, 0:Wk], taucol, zm[:, 0:Wk],
                        op0=OP.is_ge, op1=OP.add))
                vw = []
                if qt == 0 and f is not None:
                    vw.append((SS, f["sexp"]))
                vm = VS.op(lambda e, Wk=Wk, sc=sc, zm=zm, cz=cz: e.scalar_tensor_tensor(
                    cz[:, 0:Wk], zm[:, 0:Wk], 80.0, sc[:, 0:Wk],
                    op0=OP.mult, op1=OP.add), waits=vw)
                sexp = SS.op(lambda e, Wk=Wk, cz=cz, wbp=wbp, Dcol=Dcol: e.activation(
                    wbp[:, 0:Wk], cz[:, 0:Wk], ACT.Exp, bias=neg80c[:],
                    accum_out=Dcol), waits=[(VS, vm)] + wT)
                wtev = []
                nkb = qt + 1
                ngrp = (nkb + 7) // 8
                wT_last = 0
                for g in range(ngrp):
                    bb = g % 2
                    blocks = list(range(g * 8, min((g + 1) * 8, nkb)))
                    tb = 0
                    for ii, kb in enumerate(blocks):
                        if ii == 0:
                            tb = psb_produce(bb, lambda e, kb=kb, bb=bb, wbp=wbp: e.transpose(
                                psumb[bb][:, (kb % 8) * 128:(kb % 8 + 1) * 128],
                                wbp[:, kb * 128:(kb + 1) * 128], eye_bf[:]),
                                waits=[(SS, sexp)])
                        else:
                            tb = TS.op(lambda e, kb=kb, bb=bb, wbp=wbp: e.transpose(
                                psumb[bb][:, (kb % 8) * 128:(kb % 8 + 1) * 128],
                                wbp[:, kb * 128:(kb + 1) * 128], eye_bf[:]))
                    wT_last = tb
                    wtev.append(psb_consume(bb, SS, lambda e, g=g, bb=bb, nb=len(blocks), wtb=wtb: e.activation(
                        wtb[:, g * 1024:g * 1024 + nb * 128], psumb[bb][:, 0:nb * 128],
                        ACT.Identity), waits=[(TS, tb)]))
                avbank = 4 + par
                sprev, ptok = ps_rd[avbank]
                for kb in range(nkb):
                    w = []
                    if kb == 0 and ptok > 0:
                        w.append((sprev, ptok))
                    if kb % 8 == 0:
                        w.append((SS, wtev[kb // 8]))
                    TS.op(lambda e, kb=kb, h=h, qt=qt, wtb=wtb, avbank=avbank: e.matmul(
                        psum[avbank][:, 0:64], wtb[:, kb * 128:(kb + 1) * 128],
                        vtm[:, kb, h * 64:(h + 1) * 64],
                        start=(kb == 0), stop=(kb == qt)),
                        waits=w, inc=(kb == qt))
                tav = TS.n
                if "band" in debug and i == 1:
                    dwb = dbg_out("band_w", [128, 256], BF16)
                    dzm = dbg_out("band_g", [128, 256], BF16)
                    dcl = dbg_out("band_col", [128, 8])
                    t1 = STQ.dma(lambda e, wbp=wbp: e.dma_start(out=dwb, in_=wbp[:, 0:256]),
                                 waits=[(SS, sexp)])
                    t2 = STQ.dma(lambda e, zm=zm: e.dma_start(out=dzm, in_=zm[:, 0:256]),
                                 waits=[(SS, sexp)])
                    t3 = STQ.dma(lambda e, par=par: e.dma_start(
                        out=dcl[:, 0:2], in_=colP[par][:, 0:2]), waits=[(SS, sexp)])
                    t4 = STQ.dma(lambda e, par=par: e.dma_start(
                        out=dcl[:, 3:5], in_=colP[par][:, 3:5]), waits=[(SS, sexp)])
                    t5 = STQ.dma(lambda e, par=par: e.dma_start(
                        out=dcl[:, 6:8], in_=colP[par][:, 6:8]), waits=[(SS, sexp)])
                    VS.wait([t1, t2, t3, t4, t5])
                    SS.wait([t1, t2, t3, t4, t5])
                return {"vs_sc": vm, "ts_wT": wT_last, "sexp": sexp, "tav": tav,
                        "obb": ngrp % 2, "par": par, "h": h, "qt": qt}

            for i in range(len(bands) + 1):
                if i < len(bands):
                    state[i] = front(i)
                if i >= 1:
                    if fmm is not None and i - 2 >= 0:
                        fact(i - 2)
                    fin[(i - 1) % 2] = back(i - 1, state.pop(i - 1))
                    if fmm is not None and i - 1 < n_units:
                        fmm(i - 1)
            epilogue(fin[len(bands) % 2])
            epilogue(fin[(len(bands) + 1) % 2])
            if fmm is not None:
                for u in range(max(len(bands) - 1, 0), n_units):
                    fact(u)
        actx.close()
        barrier()


    def emit_wconv(w_hbm, wb_hbm, n, tag):
        width = n // 128
        nchunks = max(1, width // 4096)
        cw = width // nchunks
        wv = w_hbm.rearrange("(p a) c -> p (a c)", p=128)
        wbv = wb_hbm.rearrange("(p a) c -> p (a c)", p=128)
        with nc.sbuf_tensor("wcv" + tag, [128, cw], F32) as wchunk, \
             nc.sbuf_tensor("bcv" + tag, [128, cw], BF16) as bchunk:
            prev_st = None
            for c in range(nchunks):
                lt = LD.dma(lambda e, c=c: e.dma_start(out=wchunk[:], in_=wv[:, c * cw:(c + 1) * cw]),
                            waits=[(VS, VS.n)])
                vb = VS.op(lambda e: e.tensor_copy(bchunk[:], wchunk[:]),
                           waits=[lt] + ([prev_st] if prev_st is not None else []))
                prev_st = STQ.dma(lambda e, c=c: e.dma_start(out=wbv[:, c * cw:(c + 1) * cw], in_=bchunk[:]),
                                  waits=[(VS, vb)])
        barrier()

    def emit_wo_x1():
        with nc.sbuf_tensor("wob", [128, NP, D], BF16) as wob, \
             nc.sbuf_tensor("afm0", [128, NP, 128], BF16) as afm0, \
             nc.sbuf_tensor("afm1", [128, NP, 128], BF16) as afm1, \
             nc.sbuf_tensor("xt1a", [128, D], F32) as xt1a, \
             nc.sbuf_tensor("xt1b", [128, D], F32) as xt1b, \
             nc.sbuf_tensor("x1ta", [128, D], F32) as x1ta, \
             nc.sbuf_tensor("x1tb", [128, D], F32) as x1tb:
            afmP_, xtP_, x1tP_ = [afm0, afm1], [xt1a, xt1b], [x1ta, x1tb]
            lwo = LDG.dma(lambda e: e.dma_start(out=wob[:], in_=woT.rearrange("(j p) d -> p j d", p=128)))
            stP = [None, None]
            mmP = [0, 0]
            vvP = [0, 0]
            for i in range(NT):
                par = i % 2
                afm, xt1, x1t = afmP_[par], xtP_[par], x1tP_[par]
                la = LD.dma(lambda e, i=i, afm=afm: e.dma_start(
                    out=afm[:], in_=s_attnfm[:, :, i * 128:(i + 1) * 128].rearrange("j p t -> p j t")),
                    waits=[(TS, mmP[par])])
                lx = LD.dma(lambda e, i=i, xt1=xt1: e.dma_start(out=xt1[:], in_=x0[i * 128:(i + 1) * 128, :]),
                            waits=[(VS, vvP[par])] + ([stP[par]] if stP[par] is not None else []))
                for n in range(2):
                    bk = 2 * par + n % 2
                    sprev, ptok = ps_rd[bk]
                    for kk in range(NP):
                        w = []
                        if kk == 0:
                            w = [la, lwo]
                            if ptok > 0:
                                w.append((sprev, ptok))
                        TS.op(lambda e, n=n, kk=kk, bk=bk, afm=afm: e.matmul(
                            psum[bk][:], afm[:, kk, :], wob[:, kk, n * 512:(n + 1) * 512],
                            start=(kk == 0), stop=(kk == NP - 1)),
                            waits=w, inc=(kk == NP - 1))
                    tmm = TS.n
                    ps_consume(bk, VS, lambda e, n=n, bk=bk, x1t=x1t, xt1=xt1: e.tensor_tensor(
                        x1t[:, n * 512:(n + 1) * 512], psum[bk][:],
                        xt1[:, n * 512:(n + 1) * 512], op=OP.add),
                        waits=[(TS, tmm), lx] + ([stP[par]] if stP[par] is not None else []))
                mmP[par] = TS.n
                vvP[par] = VS.n
                stP[par] = STQ.dma(lambda e, i=i, x1t=x1t: e.dma_start(
                    out=s_x1[i * 128:(i + 1) * 128, :], in_=x1t[:]), waits=[(VS, vvP[par])])
        barrier()

    def make_ffnA(A, wb_list, bfm_list, act_hbm, gelu_fn, xsrc, tag, banks=(2, 3),
                  raw_spill=False):
        nexp = len(wb_list)
        w1c2 = A(nc.sbuf_tensor("w1c" + tag, [128, 2, NP, 128], BF16))
        pw1c2 = A(nc.sbuf_tensor("pw1c" + tag, [128, 2, NP, 128], BF16))
        actt2 = A(nc.sbuf_tensor("actt" + tag, [128, 2, 512], BF16))
        prev_stores = {0: [], 1: []}
        lastmm = {0: 0, 1: 0}
        ltoks_cur = {}
        upf = 4 * nexp

        mmtoks = {}

        def mm_part(u):
            ft, r = divmod(u, upf)
            nch, ei = divmod(r, nexp)
            fp = ft % 2
            wcs = [w1c2[:, fp], pw1c2[:, fp]][:nexp]
            if r == 0:
                ltoks_cur.clear()
                for e2 in range(nexp):
                    ltoks_cur[e2] = LDG.dma(lambda e, ft=ft, e2=e2, wcs=wcs: e.dma_start(
                        out=wcs[e2],
                        in_=wb_list[e2][:, ft * 128:(ft + 1) * 128].rearrange(
                            "(j p) f -> p j f", p=128)),
                        waits=[(TS, lastmm[fp])])
            bk = banks[u % len(banks)]
            sprev, ptok = ps_rd[bk]
            for kk in range(NP):
                w = []
                if kk == 0:
                    w = [ltoks_cur[ei]]
                    if ptok > 0:
                        w.append((sprev, ptok))
                TS.op(lambda e, nch=nch, ei=ei, kk=kk, bk=bk, wcs=wcs: e.matmul(
                    psum[bk][:], wcs[ei][:, kk, :],
                    xsrc[:, kk, nch * 512:(nch + 1) * 512],
                    start=(kk == 0), stop=(kk == NP - 1)),
                    waits=w, inc=(kk == NP - 1))
            mmtoks[u] = TS.n
            if r == upf - 1:
                lastmm[fp] = TS.n

        def act_part(u):
            ft, r = divmod(u, upf)
            nch, ei = divmod(r, nexp)
            bk = banks[u % len(banks)]
            ap = u % 2
            fn = ACT.Identity if raw_spill else gelu_fn
            sg = SS.op(lambda e, ft=ft, ei=ei, bk=bk, ap=ap, fn=fn: e.activation(
                actt2[:, ap], psum[bk][:], fn, bias=bfm_list[ei][:, ft:ft + 1]),
                waits=[(TS, mmtoks.pop(u))] + prev_stores[ap])
            ps_rd[bk] = (SS, sg)
            prev_stores[ap] = []
            for tt4 in range(4):
                prev_stores[ap].append(STQ.dma(
                    lambda e, ft=ft, ei=ei, nch=nch, tt4=tt4, ap=ap: e.dma_start(
                        out=act_hbm[ei, ft, nch * 4 + tt4, :, :],
                        in_=actt2[:, ap, tt4 * 128:(tt4 + 1) * 128]),
                    waits=[(SS, sg)]))
        return mm_part, act_part, NF * upf

    def emit_ffnA(wb_list, bfm_list, act_hbm, gelu_fn, xsrc, tag):
        from contextlib import ExitStack
        fctx = ExitStack()
        mm_part, act_part, n_units = make_ffnA(
            fctx.enter_context, wb_list, bfm_list, act_hbm,
            gelu_fn, xsrc, tag, banks=(0, 1, 2, 3))
        for u in range(n_units):
            mm_part(u)
            act_part(u)
        fctx.close()
        barrier()

    def emit_ffnB0():
        with nc.sbuf_tensor("w2r", [128, NF, D], BF16) as w2r, \
             nc.sbuf_tensor("pw2r", [128, NF, D], BF16) as pw2r, \
             nc.sbuf_tensor("acts", [128, 2, NF, 128], BF16) as acts, \
             nc.sbuf_tensor("y0t", [128, 512], F32) as y0t, \
             nc.sbuf_tensor("x1t2", [128, 2, D], F32) as x1t2, \
             nc.sbuf_tensor("p0c", [128, 1], F32) as p0c, \
             nc.sbuf_tensor("q0c", [128, 1], F32) as q0c, \
             nc.sbuf_tensor("cvt", [128, D], F32) as cvt:
            lw2 = LDG.dma(lambda e: e.dma_start(
                out=w2r[:], in_=s_w2b.rearrange("(j p) d -> p j d", p=128)))
            lpw2 = LDG.dma(lambda e: e.dma_start(
                out=pw2r[:], in_=s_pw2b.rearrange("(j p) d -> p j d", p=128)))
            stP = [None, None]
            mmP = [0, 0]
            vvP = [0, 0]
            for i in range(NT):
                par = i % 2
                acts_, x1t_ = acts[:], x1t2[:, par]
                la = LD.dma(lambda e, i=i, acts_=acts_: e.dma_start(
                    out=acts_, in_=s_act[:, :, i, :, :].rearrange("e f p t -> p e f t")),
                    waits=[(TS, max(mmP))])
                gl = 0
                for ei2 in range(2):
                    gl = SS.op(lambda e, ei2=ei2: e.activation(
                        acts[:, ei2], acts[:, ei2], ACT.Gelu_apprx_tanh),
                        waits=[la] if ei2 == 0 else [])
                la = (SS, gl)
                lx = LD.dma(lambda e, i=i, x1t_=x1t_: e.dma_start(
                    out=x1t_, in_=s_x1[i * 128:(i + 1) * 128, :]),
                    waits=[(VS, vvP[par])] + ([stP[par]] if stP[par] is not None else []))
                VS.op(lambda e, i=i: e.tensor_copy(p0c[:], p0b[:, i:i + 1]))
                VS.op(lambda e: e.tensor_scalar(q0c[:], p0c[:], -1.0, 1.0, op0=OP.mult, op1=OP.add))
                VS.op(lambda e: e.scalar_tensor_tensor(cvt[:], dvec_bc[:], q0c[:], cvec_bc[:],
                                                       op0=OP.mult, op1=OP.add))
                for n in range(2):
                    for (ei, wsb) in ((0, w2r), (1, pw2r)):
                        bk = 2 * par + ei
                        sprev, ptok = ps_rd[bk]
                        for kk in range(NF):
                            w = []
                            if kk == 0:
                                w = [la, lw2 if ei == 0 else lpw2]
                                if ptok > 0:
                                    w.append((sprev, ptok))
                            TS.op(lambda e, n=n, ei=ei, kk=kk, bk=bk, wsb=wsb, acts_=acts_: e.matmul(
                                psum[bk][:], acts_[:, ei, kk, :], wsb[:, kk, n * 512:(n + 1) * 512],
                                start=(kk == 0), stop=(kk == NF - 1)),
                                waits=w, inc=(kk == NF - 1))
                        if ei == 0:
                            t0 = TS.n
                        else:
                            t1 = TS.n
                    b0, b1 = 2 * par, 2 * par + 1
                    ev0 = ps_consume(b0, VS, lambda e, b0=b0: e.tensor_copy(y0t[:], psum[b0][:]),
                                     waits=[(TS, t0)])
                    vv = ps_consume(b1, VS, lambda e, n=n, b1=b1: e.tensor_sub(y0t[:], y0t[:], psum[b1][:]),
                                    waits=[(TS, t1)])
                    VS.op(lambda e, n=n, b1=b1: e.scalar_tensor_tensor(
                        y0t[:], y0t[:], p0c[:], psum[b1][:], op0=OP.mult, op1=OP.add))
                    ps_rd[b1] = (VS, VS.n)
                    VS.op(lambda e, n=n: e.tensor_tensor(
                        y0t[:], y0t[:], cvt[:, n * 512:(n + 1) * 512], op=OP.add))
                    VS.op(lambda e, n=n, x1t_=x1t_: e.tensor_tensor(
                        x1t_[:, n * 512:(n + 1) * 512], y0t[:],
                        x1t_[:, n * 512:(n + 1) * 512], op=OP.add), waits=[lx])
                mmP[par] = TS.n
                vvP[par] = VS.n
                stP[par] = STQ.dma(lambda e, i=i, x1t_=x1t_: e.dma_start(
                    out=s_x1[i * 128:(i + 1) * 128, :], in_=x1t_), waits=[(VS, vvP[par])])
        barrier()

    def emit_l1_ln(xnfm):
        with nc.sbuf_tensor("xtl", [128, D], F32) as xt, \
             nc.sbuf_tensor("xcl", [128, D], F32) as xc, \
             nc.sbuf_tensor("sql", [128, D], F32) as sqs, \
             nc.sbuf_tensor("mul_", [128, 1], F32) as mu, \
             nc.sbuf_tensor("musql", [128, 1], F32) as musq, \
             nc.sbuf_tensor("ssql", [128, 1], F32) as ssq, \
             nc.sbuf_tensor("vepsl", [128, 1], F32) as veps, \
             nc.sbuf_tensor("rstdl", [128, 1], F32) as rstd:
            sq_prev = 0
            tposx_prev = 0
            for i in range(NT):
                lx = LD.dma(lambda e, i=i: e.dma_start(out=xt[:], in_=s_x1[i * 128:(i + 1) * 128, :]),
                            waits=[(VS, VS.n)] + ([(SS, sq_prev)] if sq_prev else []))
                VS.op(lambda e: e.reduce_sum(mu[:], xt[:], axis=AX.X), waits=[lx])
                VS.op(lambda e: e.tensor_scalar(mu[:], mu[:], 1.0 / D, None, op0=OP.mult))
                sq = SS.op(lambda e: e.activation(sqs[:], xt[:], ACT.Square, accum_out=ssq[:]),
                           waits=[lx, (VS, VS.n)])
                sq_prev = sq
                VS.op(lambda e: e.tensor_mul(musq[:], mu[:], mu[:]), waits=[(SS, sq)])
                VS.op(lambda e: e.scalar_tensor_tensor(veps[:], ssq[:], 1.0 / D, musq[:],
                                                       op0=OP.mult, op1=OP.subtract))
                VS.op(lambda e: e.tensor_scalar(veps[:], veps[:], EPS, None, op0=OP.add))
                s2 = SS.op(lambda e: e.activation(rstd[:], veps[:], ACT.Sqrt),
                           waits=[(VS, VS.n)])
                VS.op(lambda e: e.reciprocal(rstd[:], rstd[:]), waits=[(SS, s2)])
                vb = VS.op(lambda e: e.tensor_scalar(xc[:], xt[:], mu[:], rstd[:],
                                                     op0=OP.subtract, op1=OP.mult),
                           waits=([(TS, tposx_prev)] if tposx_prev else []))
                for j in range(NP):
                    bk = j % 2
                    t = ps_produce(bk, lambda e, j=j, bk=bk: e.transpose(
                        psum[bk][:, 0:128], xc[:, j * 128:(j + 1) * 128], eye_f[:]),
                        waits=[(VS, vb)])
                    ps_consume(bk, VS, lambda e, j=j, i=i, bk=bk: e.tensor_copy(
                        xnfm[:, j, i * 128:(i + 1) * 128], psum[bk][:, 0:128]),
                        waits=[(TS, t)])
                    tposx_prev = t
        barrier()

    BIG1 = 400.0

    def emit_l1_attn(xnfm):
        from contextlib import ExitStack
        actx = ExitStack()
        A = actx.enter_context
        qfm = A(nc.sbuf_tensor("qfm1", [128, NP, T], BF16))
        kfm = A(nc.sbuf_tensor("kfm1", [128, NP, T], BF16))
        vtm = A(nc.sbuf_tensor("vtm1", [128, NT, D], BF16))
        qb = A(nc.sbuf_tensor("qb1", [64, H, 32], F32))
        kb = A(nc.sbuf_tensor("kb1", [64, H, 32], F32))
        bmask = A(nc.sbuf_tensor("bmask1", [32, H, 32], BF16))
        cmat = A(nc.sbuf_tensor("cmat1", [32, 32], F32))
        m8c = A(nc.sbuf_tensor("m8c1", [32, 8], F32))
        em = A(nc.sbuf_tensor("em1", [32, T], BF16))
        emv = A(nc.sbuf_tensor("emv1", [32, T], F32))
        em2 = A(nc.sbuf_tensor("em21", [32, T], BF16))
        eyem = A(nc.sbuf_tensor("eyem1", [32, 32], F32))

        g1 = GS.op(lambda e: e.iota(emv[:], [[1, T]], base=0, channel_multiplier=-64,
                                    allow_small_or_imprecise_dtypes=True))
        VS.op(lambda e: e.tensor_scalar(em2[:], emv[:], 0.0, None, op0=OP.is_ge),
              waits=[(GS, g1)])
        VS.op(lambda e: e.tensor_scalar(em[:], emv[:], 63.0, None, op0=OP.is_le))
        VS.op(lambda e: e.tensor_mul(em[:], em[:], em2[:]))
        g2 = GS.op(lambda e: e.iota(eyem[:], [[1, 32]], base=0, channel_multiplier=-1,
                                    allow_small_or_imprecise_dtypes=True))
        VS.op(lambda e: e.tensor_scalar(eyem[:], eyem[:], 0.0, None, op0=OP.is_equal),
              waits=[(GS, g2)])

        with nc.sbuf_tensor("wprj1", [128, NP, D], BF16) as wprj:
            for (off, dstfm) in ((0, qfm), (D, kfm), (2 * D, None)):
                lw = LDG.dma(lambda e, off=off: e.dma_start(
                    out=wprj[:], in_=qkvT[:, off:off + D].rearrange("(j p) d -> p j d", p=128)),
                    waits=[(TS, TS.n)])
                if dstfm is None:
                    for i in range(NT):
                        for n in range(2):
                            bk = n % 2
                            sprev, ptok = ps_rd[bk]
                            for kk in range(NP):
                                w = []
                                if kk == 0:
                                    w = [lw]
                                    if ptok > 0:
                                        w.append((sprev, ptok))
                                TS.op(lambda e, i=i, n=n, kk=kk, bk=bk: e.matmul(
                                    psum[bk][:], xnfm[:, kk, i * 128:(i + 1) * 128],
                                    wprj[:, kk, n * 512:(n + 1) * 512],
                                    start=(kk == 0), stop=(kk == NP - 1)),
                                    waits=w, inc=(kk == NP - 1))
                            tmm = TS.n
                            ps_consume(bk, VS, lambda e, i=i, n=n, bk=bk: e.tensor_copy(
                                vtm[:, i, n * 512:(n + 1) * 512], psum[bk][:]),
                                waits=[(TS, tmm)])
                else:
                    for j in range(NP):
                        for ch in range(4):
                            bk = ch % 2
                            sprev, ptok = ps_rd[bk]
                            for kk in range(NP):
                                w = []
                                if kk == 0:
                                    w = [lw]
                                    if ptok > 0:
                                        w.append((sprev, ptok))
                                TS.op(lambda e, j=j, ch=ch, kk=kk, bk=bk: e.matmul(
                                    psum[bk][:], wprj[:, kk, j * 128:(j + 1) * 128],
                                    xnfm[:, kk, ch * 512:(ch + 1) * 512],
                                    start=(kk == 0), stop=(kk == NP - 1)),
                                    waits=w, inc=(kk == NP - 1))
                            tmm = TS.n
                            ps_consume(bk, VS, lambda e, j=j, ch=ch, bk=bk, dstfm=dstfm: e.tensor_copy(
                                dstfm[:, j, ch * 512:(ch + 1) * 512], psum[bk][:]),
                                waits=[(TS, tmm)])
        barrier()

        for h in range(H):
            pt, po = h // 2, (h % 2) * 64
            v1 = VS.op(lambda e, h=h, pt=pt, po=po: e.reduce_sum(
                qb[:, h, :], qfm[po:po + 64, pt, :].rearrange("p (m w) -> p m w", w=64),
                axis=AX.X))
            v2 = VS.op(lambda e, h=h, pt=pt, po=po: e.reduce_sum(
                kb[:, h, :], kfm[po:po + 64, pt, :].rearrange("p (m w) -> p m w", w=64),
                axis=AX.X))
            sprev, ptok = ps_rd[5]
            w = [(VS, v2)]
            if ptok > 0:
                w.append((sprev, ptok))
            TS.op(lambda e, h=h: e.matmul(psum[5][0:32, 0:32], qb[:, h, :], kb[:, h, :],
                                          start=True, stop=True), waits=w, inc=True)
            tc = TS.n
            ev = ps_consume(5, SS, lambda e: e.activation(cmat[:], psum[5][0:32, 0:32],
                                                          ACT.Identity, scale=0.125 / 4096.0),
                            waits=[(TS, tc), (VS, VS.n)])
            VS.op(lambda e: e.max(out=m8c[:], in_=cmat[:]), waits=[(SS, ev)])
            VS.op(lambda e, h=h: e.scalar_tensor_tensor(
                bmask[:, h, :], cmat[:], m8c[:, 1:2], eyem[:],
                op0=OP.is_ge, op1=OP.max))
        barrier()

        sc1P = [A(nc.sbuf_tensor(f"sc1P{p}", [128, T], BF16)) for p in range(2)]
        vmP = [A(nc.sbuf_tensor(f"vmP{p}", [128, T], F32)) for p in range(2)]
        wbp1P = [A(nc.sbuf_tensor(f"wbp1P{p}", [128, T], BF16)) for p in range(2)]
        wtb1P = [A(nc.sbuf_tensor(f"wtb1P{p}", [128, T], BF16)) for p in range(2)]
        M2P = [A(nc.sbuf_tensor(f"M2P{p}", [128, 32], F32)) for p in range(2)]
        col1P = [A(nc.sbuf_tensor(f"col1P{p}", [128, 4], F32)) for p in range(2)]
        otile1P = [A(nc.sbuf_tensor(f"otile1P{p}", [128, 64], BF16)) for p in range(2)]
        ofm1P = [A(nc.sbuf_tensor(f"ofm1P{p}", [64, 128], BF16)) for p in range(2)]
        bands1 = [(h, qt) for h in range(H) for qt in range(NT)]
        fin1 = {0: None, 1: None}
        st1 = {}

        def front1(i):
            h, qt = bands1[i]
            par = i % 2
            sc, M2 = sc1P[par], M2P[par]
            pt, po = h // 2, (h % 2) * 64
            qsl = qfm[po:po + 64, pt, qt * 128:(qt + 1) * 128]
            f = fin1[par]
            mbank = 4 + par
            sprev, ptok = ps_rd[mbank]
            w = [(VS, VS.n)]
            if ptok > 0:
                w.append((sprev, ptok))
            TS.op(lambda e, qt=qt, h=h, mbank=mbank: e.matmul(
                psum[mbank][:, 448:480], em[:, qt * 128:(qt + 1) * 128], bmask[:, h, :],
                start=True, stop=True), waits=w, inc=True)
            tm2 = TS.n
            evlast = 0
            vtoks = {}
            for cch in range(4):
                bk = 2 * par + cch % 2
                sprev, ptok = ps_rd[bk]
                w = [(sprev, ptok)] if ptok > 0 else []
                TS.op(lambda e, cch=cch, bk=bk, qsl=qsl, pt=pt, po=po: e.matmul(
                    psum[bk][:], qsl, kfm[po:po + 64, pt, cch * 512:(cch + 1) * 512],
                    start=True, stop=True), waits=w, inc=True)
                tmm = TS.n
                if cch < 2:
                    ww = [(TS, tmm)]
                    if f is not None and cch == 0:
                        ww.append((VS, f["vs_sc"]))
                    evlast = SS.op(lambda e, cch=cch, bk=bk, sc=sc: e.activation(
                        sc[:, cch * 512:(cch + 1) * 512], psum[bk][:],
                        ACT.Identity, scale=0.125), waits=ww)
                    ps_rd[bk] = (SS, evlast)
                else:
                    vtoks[cch] = tmm
            return {"evlast": evlast, "tm2": tm2, "vtoks": vtoks}

        last_spill1 = {0: None, 1: None}

        def epilogue1(f2):
            par, h, qt = f2["par"], f2["h"], f2["qt"]
            pt, po = h // 2, (h % 2) * 64
            Dcol = col1P[par][:, 1:2]
            rDcol = col1P[par][:, 2:3]
            otile, ofm = otile1P[par], ofm1P[par]
            avbank = 4 + par
            VS.op(lambda e, Dcol=Dcol, rDcol=rDcol: e.tensor_scalar_max(
                rDcol, Dcol, 1e-30), waits=[(SS, f2["ss_exp"])])
            VS.op(lambda e, rDcol=rDcol: e.reciprocal(rDcol, rDcol))
            ov = ps_consume(avbank, VS, lambda e, otile=otile, rDcol=rDcol, avbank=avbank: e.tensor_scalar(
                otile[:], psum[avbank][:, 0:64], rDcol, None, op0=OP.mult),
                waits=[(TS, f2["tav"])])
            tb = psb_produce(0, lambda e, otile=otile: e.transpose(
                psumb[0][0:64, 0:128], otile[:, 0:64], eye_bf[:]),
                waits=[(VS, ov)])
            ow = [(TS, tb)]
            if last_spill1[par] is not None:
                ow.append(last_spill1[par])
            oev = psb_consume(0, SS, lambda e, ofm=ofm: e.activation(
                ofm[:], psumb[0][0:64, 0:128], ACT.Identity), waits=ow)
            last_spill1[par] = STQ.dma(lambda e, pt=pt, po=po, qt=qt, ofm=ofm: e.dma_start(
                out=s_attnfm1.rearrange("(j p) t -> j p t", p=128)[
                    pt, po:po + 64, qt * 128:(qt + 1) * 128],
                in_=ofm[:]), waits=[(SS, oev)])

        def back1(i, st_):
            h, qt = bands1[i]
            par = i % 2
            f = fin1[par]
            sc, M2, vm = sc1P[par], M2P[par], vmP[par]
            wbp, wtb = wbp1P[par], wtb1P[par]
            mxc = col1P[par][:, 0:1]
            Dcol = col1P[par][:, 1:2]
            pt, po = h // 2, (h % 2) * 64
            for cch, tmm in st_["vtoks"].items():
                bk = 2 * par + cch % 2
                vev = VS.op(lambda e, cch=cch, bk=bk, sc=sc: e.tensor_scalar(
                    sc[:, cch * 512:(cch + 1) * 512], psum[bk][:], 0.125,
                    None, op0=OP.mult), waits=[(TS, tmm)])
                ps_rd[bk] = (VS, vev)
            mbank = 4 + par
            VS.op(lambda e, M2=M2, mbank=mbank: e.tensor_copy(
                M2[:], psum[mbank][:, 448:480]), waits=[(TS, st_["tm2"])])
            ps_rd[mbank] = (VS, VS.op(lambda e, M2=M2: e.tensor_scalar(
                M2[:], M2[:], 1e30, -1e30, op0=OP.mult, op1=OP.add)))
            if f is not None:
                epilogue1(f)
            sc3 = sc.rearrange("p (m b) -> p m b", b=64)
            vm3 = vm.rearrange("p (m b) -> p m b", b=64)
            vw = [(SS, st_["evlast"])]
            if f is not None:
                vw.append((SS, f["ss_exp"]))
            vmt = VS.op(lambda e, sc3=sc3, vm3=vm3, M2=M2: e.scalar_tensor_tensor(
                vm3, M2[:].unsqueeze(2).to_broadcast([128, 32, 64]), 1.0, sc3,
                op0=OP.mult, op1=OP.add), waits=vw)
            VS.op(lambda e, vm=vm, mxc=mxc: e.reduce_max(mxc, vm[:], axis=AX.X))
            vneg = VS.op(lambda e, mxc=mxc: e.tensor_scalar(mxc, mxc, -1.0, None, op0=OP.mult))
            ew = [(VS, vneg)]
            if f is not None:
                ew.append(f["vs_wbp"])
            sexp = SS.op(lambda e, vm=vm, wbp=wbp, Dcol=Dcol, mxc=mxc: e.activation(
                wbp[:], vm[:], ACT.Exp, bias=mxc, accum_out=Dcol), waits=ew)
            wtev = []
            for g in range(2):
                bb = g % 2
                tb = 0
                for ii in range(8):
                    kbi = g * 8 + ii
                    if ii == 0:
                        tb = psb_produce(bb, lambda e, kbi=kbi, bb=bb, wbp=wbp: e.transpose(
                            psumb[bb][:, (kbi % 8) * 128:(kbi % 8 + 1) * 128],
                            wbp[:, kbi * 128:(kbi + 1) * 128], eye_bf[:]),
                            waits=[(SS, sexp)])
                    else:
                        tb = TS.op(lambda e, kbi=kbi, bb=bb, wbp=wbp: e.transpose(
                            psumb[bb][:, (kbi % 8) * 128:(kbi % 8 + 1) * 128],
                            wbp[:, kbi * 128:(kbi + 1) * 128], eye_bf[:]))
                wtev.append((SS, psb_consume(bb, SS, lambda e, g=g, bb=bb, wtb=wtb: e.activation(
                    wtb[:, g * 1024:(g + 1) * 1024], psumb[bb][:, 0:1024],
                    ACT.Identity), waits=[(TS, tb)])))
            avbank = 4 + par
            sprev, ptok = ps_rd[avbank]
            for kbi in range(NT):
                w = []
                if kbi == 0 and ptok > 0:
                    w.append((sprev, ptok))
                if kbi % 8 == 0:
                    w.append(wtev[kbi // 8])
                TS.op(lambda e, kbi=kbi, h=h, wtb=wtb, avbank=avbank: e.matmul(
                    psum[avbank][:, 0:64], wtb[:, kbi * 128:(kbi + 1) * 128],
                    vtm[:, kbi, h * 64:(h + 1) * 64],
                    start=(kbi == 0), stop=(kbi == NT - 1)),
                    waits=w, inc=(kbi == NT - 1))
            tav = TS.n
            return {"vs_sc": vmt, "vs_wbp": wtev[-1],
                    "ss_exp": sexp, "tav": tav, "par": par, "h": h, "qt": qt}

        for i in range(len(bands1) + 1):
            if i < len(bands1):
                st1[i] = front1(i)
            if i >= 1:
                fin1[(i - 1) % 2] = back1(i - 1, st1.pop(i - 1))
        epilogue1(fin1[len(bands1) % 2])
        epilogue1(fin1[(len(bands1) + 1) % 2])
        actx.close()
        barrier()

    def emit_l1_proj():
        with nc.sbuf_tensor("prjb", [128, NP, D], BF16) as prjb, \
             nc.sbuf_tensor("afml1", [128, 2, NP, 128], BF16) as afm, \
             nc.sbuf_tensor("xt2", [128, 2, D], F32) as xt2, \
             nc.sbuf_tensor("x2t", [128, 2, D], F32) as x2t:
            lwo = LDG.dma(lambda e: e.dma_start(out=prjb[:], in_=projT.rearrange("(j p) d -> p j d", p=128)))
            stP = [None, None]
            mmP = [0, 0]
            vvP = [0, 0]
            for i in range(NT):
                par = i % 2
                afm_, xt2_, x2t_ = afm[:, par], xt2[:, par], x2t[:, par]
                la = LD.dma(lambda e, i=i, afm_=afm_: e.dma_start(
                    out=afm_, in_=s_attnfm1.rearrange("(j p) t -> p j t", p=128)[:, :, i * 128:(i + 1) * 128]),
                    waits=[(TS, mmP[par])])
                lx = LD.dma(lambda e, i=i, xt2_=xt2_: e.dma_start(out=xt2_, in_=s_x1[i * 128:(i + 1) * 128, :]),
                            waits=[(VS, vvP[par])] + ([stP[par]] if stP[par] is not None else []))
                for n in range(2):
                    bk = 2 * par + n % 2
                    sprev, ptok = ps_rd[bk]
                    for kk in range(NP):
                        w = []
                        if kk == 0:
                            w = [la, lwo]
                            if ptok > 0:
                                w.append((sprev, ptok))
                        TS.op(lambda e, n=n, kk=kk, bk=bk, afm_=afm_: e.matmul(
                            psum[bk][:], afm_[:, kk, :], prjb[:, kk, n * 512:(n + 1) * 512],
                            start=(kk == 0), stop=(kk == NP - 1)),
                            waits=w, inc=(kk == NP - 1))
                    tmm = TS.n
                    ps_consume(bk, VS, lambda e, n=n, bk=bk, x2t_=x2t_, xt2_=xt2_: e.tensor_tensor(
                        x2t_[:, n * 512:(n + 1) * 512], psum[bk][:],
                        xt2_[:, n * 512:(n + 1) * 512], op=OP.add),
                        waits=[(TS, tmm), lx] + ([stP[par]] if stP[par] is not None else []))
                mmP[par] = TS.n
                vvP[par] = VS.n
                stP[par] = STQ.dma(lambda e, i=i, x2t_=x2t_: e.dma_start(
                    out=s_x2[i * 128:(i + 1) * 128, :], in_=x2t_), waits=[(VS, vvP[par])])
        barrier()

    def emit_ffnB1():
        with nc.sbuf_tensor("w2r1", [128, NF, D], BF16) as w2r, \
             nc.sbuf_tensor("acts1", [128, 2, NF, 128], BF16) as acts, \
             nc.sbuf_tensor("yt1", [128, 512], F32) as yt, \
             nc.sbuf_tensor("x2t2", [128, 2, D], F32) as x2t2:
            lw2 = LDG.dma(lambda e: e.dma_start(
                out=w2r[:], in_=s_lw2b.rearrange("(j p) d -> p j d", p=128)))
            stP = [None, None]
            mmP = [0, 0]
            vvP = [0, 0]
            for i in range(NT):
                par = i % 2
                acts_, x2t_ = acts[:, par], x2t2[:, par]
                la = LD.dma(lambda e, i=i, acts_=acts_: e.dma_start(
                    out=acts_, in_=s_act1[0, :, i, :, :].rearrange("f p t -> p f t")),
                    waits=[(TS, mmP[par])])
                lx = LD.dma(lambda e, i=i, x2t_=x2t_: e.dma_start(out=x2t_, in_=s_x2[i * 128:(i + 1) * 128, :]),
                            waits=[(VS, vvP[par])] + ([stP[par]] if stP[par] is not None else []))
                for n in range(2):
                    bk = 2 * par + n % 2
                    sprev, ptok = ps_rd[bk]
                    for kk in range(NF):
                        w = []
                        if kk == 0:
                            w = [la, lw2]
                            if ptok > 0:
                                w.append((sprev, ptok))
                        TS.op(lambda e, n=n, kk=kk, bk=bk, acts_=acts_: e.matmul(
                            psum[bk][:], acts_[:, kk, :], w2r[:, kk, n * 512:(n + 1) * 512],
                            start=(kk == 0), stop=(kk == NF - 1)),
                            waits=w, inc=(kk == NF - 1))
                    tmm = TS.n
                    ps_consume(bk, VS, lambda e, n=n, bk=bk: e.tensor_tensor(
                        yt[:], psum[bk][:], lbias_bc[:, n * 512:(n + 1) * 512], op=OP.add),
                        waits=[(TS, tmm), lx])
                    VS.op(lambda e, n=n, x2t_=x2t_: e.tensor_tensor(
                        x2t_[:, n * 512:(n + 1) * 512], yt[:],
                        x2t_[:, n * 512:(n + 1) * 512], op=OP.add))
                mmP[par] = TS.n
                vvP[par] = VS.n
                stP[par] = STQ.dma(lambda e, i=i, x2t_=x2t_: e.dma_start(
                    out=out[i * 128:(i + 1) * 128, :], in_=x2t_), waits=[(VS, vvP[par])])
        barrier()

    emit_consts()
    emit_wprep(W1, s_w1b, s_pw1b, D * FF, 0, tag="w1")
    emit_wprep(W2, s_w2b, s_pw2b, FF * D, 1, tag="w2")
    emit_bias_prep()
    xnfm = E(nc.sbuf_tensor("xnfm", [128, NP, T], BF16))
    emit_pre(xnfm)
    emit_attn(ffnA_spec=([s_w1b, s_pw1b], [b1fm, pb1fm], s_act,
                         ACT.Gelu_apprx_tanh, xnfm, "f0"))
    if "attnfm" in debug:
        d = dbg_out("attnfm", [NP, 128, T], BF16)
        STQ.dma(lambda e, d=d: e.dma_start(out=d, in_=s_attnfm))
        barrier()
    emit_wconv(Lw1, s_lw1b, D * FF, "l1")
    emit_wconv(Lw2, s_lw2b, FF * D, "l2")
    emit_wo_x1()
    emit_ffnB0()
    if "x1" in debug:
        d = dbg_out("x1", [T, D], F32)
        STQ.dma(lambda e, d=d: e.dma_start(out=d, in_=s_x1))
        barrier()
    emit_l1_ln(xnfm)
    emit_l1_attn(xnfm)
    emit_l1_proj()
    emit_ffnA([s_lw1b], [lb1fm], s_act1, ACT.Gelu, xnfm, "f1")
    emit_ffnB1()

    SY.wait(STQ.all_tokens())

    with nc.Block() as block:
        block.tensor(lambda e: TS.realize(e))
        block.vector(lambda e: VS.realize(e))
        block.scalar(lambda e: SS.realize(e))
        block.gpsimd(lambda e: GS.realize(e))
        block.sync(lambda e: SY.realize(e))

    ctx.close()
    return nc, dbg


def host_inputs(inputs, core):
    x = np.ascontiguousarray(inputs["L0_attn_x"][core])
    m = {
        "x0": x,
        "wqT": np.ascontiguousarray(inputs["L0_attn_wq"].T),
        "wkT": np.ascontiguousarray(inputs["L0_attn_wk"].T),
        "wvT": np.ascontiguousarray(inputs["L0_attn_wv"].T),
        "woT": np.ascontiguousarray(inputs["L0_attn_wo"].T),
        "W1": np.ascontiguousarray(inputs["L0_ffn_W1"]),
        "B1": np.ascontiguousarray(inputs["L0_ffn_b1"]),
        "W2": np.ascontiguousarray(inputs["L0_ffn_W2"]),
        "B2": np.ascontiguousarray(inputs["L0_ffn_b2"]),
        "LNG": np.ascontiguousarray(inputs["L0_ffn_ln_gamma"]),
        "LNB": np.ascontiguousarray(inputs["L0_ffn_ln_beta"]),
        "WR": np.ascontiguousarray(inputs["L0_ffn_Wr"]),
        "BR": np.ascontiguousarray(inputs["L0_ffn_br"]),
        "qkvT": np.ascontiguousarray(inputs["L1_attn_qkv_w"].T),
        "projT": np.ascontiguousarray(inputs["L1_attn_proj_w"].T),
        "projB": np.ascontiguousarray(inputs["L1_attn_proj_b"]),
        "Lw1": np.ascontiguousarray(inputs["L1_ffn_w1"]),
        "Lb1": np.ascontiguousarray(inputs["L1_ffn_b1"]),
        "Lw2": np.ascontiguousarray(inputs["L1_ffn_w2"]),
        "Lb2": np.ascontiguousarray(inputs["L1_ffn_b2"]),
    }
    return {k: np.asarray(v, dtype=np.float32) for k, v in m.items()}


def kernel(**inputs):
    nc, _ = build_program()
    in_maps = [host_inputs(inputs, c) for c in range(8)]
    res = run_bass_kernel_spmd(nc, in_maps, core_ids=list(range(8)))
    return np.stack([r["out"] for r in res.results], axis=0)
